# version 07
# baseline (speedup 1.0000x reference)
import math
from contextlib import ExitStack

import numpy as np
import concourse.bass as bass
import concourse.mybir as mybir
from concourse.bass_utils import run_bass_kernel_spmd

F32 = mybir.dt.float32
BF16 = mybir.dt.bfloat16
I32 = mybir.dt.int32
I8 = mybir.dt.int8
AF = mybir.ActivationFunctionType
ALU = mybir.AluOpType
AX = mybir.AxisListType.X

D = 2048; L = 2048; NB = 2
HD = 128; HQ = 16; HK = 8; CCH = 64; NCH = L // CCH; NPAIR = L // 128
NUM_ITER = 30; RIDGE = 0.02; EPS0 = 1e-6
H1 = 32; P1 = 64; N1 = 32; CONV_DIM = D + 2 * N1; PROJ1 = D + CONV_DIM + H1
MLP0 = 16384; MLP1 = 8192
LNEPS = 1e-5
LNQS = -0.5 * math.log(128.0)


def ts(i, n):
    return slice(i * n, (i + 1) * n)


class Sem:
    def __init__(self, h):
        self.h = h
        self.n = 0

    def after(self, ins, d=1):
        ins.then_inc(self.h, d)
        self.n += d
        return self.n


def sw(e, sem, ins):
    sem.after(ins)
    e.wait_ge(sem.h, sem.n)


class Ctx:
    def __init__(self, nc):
        self.nc = nc
        self.blk = None
        self.t = {}
        self.sems = {}

    def on(self, eng, fn):
        getattr(self.blk, eng)(fn)

    def sem(self, name):
        return self.sems[name]


def declare_dram(nc, over=None):
    over = over or {}
    t = {}
    def ei(name, shape, dt):
        t[name] = nc.dram_tensor(name, shape, dt, kind=over.get(name, "ExternalInput")).ap()
    ei("L0_attn_hidden_states", [NB, L, D], F32)
    ei("L0_attn_Wq", [D, D], F32)
    ei("L0_attn_Wk", [D, HK * HD], F32)
    ei("L0_attn_Wv", [D, HK * HD], F32)
    ei("L0_attn_conv_q", [D, 4], F32)
    ei("L0_attn_conv_k", [HK * HD, 4], F32)
    ei("L0_attn_conv_v", [HK * HD, 4], F32)
    ei("L0_attn_Wa", [D, HQ], F32)
    ei("L0_attn_A_log", [HQ], F32)
    ei("L0_attn_dt_bias", [HQ], F32)
    ei("L0_attn_Wb", [D, HQ], F32)
    ei("L0_attn_bb", [HQ], F32)
    ei("L0_attn_Walpha", [D, HQ], F32)
    ei("L0_attn_balpha", [HQ], F32)
    ei("L0_attn_Wg", [D, D], F32)
    ei("L0_attn_norm_w", [HD], F32)
    ei("L0_attn_Wo", [D, D], F32)
    ei("L0_ffn_gate_W_packed", [MLP0, D // 2], I32)
    ei("L0_ffn_gate_scales", [MLP0, D // 32], F32)
    ei("L0_ffn_up_W_packed", [MLP0, D // 2], I32)
    ei("L0_ffn_up_scales", [MLP0, D // 32], F32)
    ei("L0_ffn_down_W_packed", [D, MLP0 // 2], I32)
    ei("L0_ffn_down_scales", [D, MLP0 // 32], F32)
    ei("L1_attn_W_in", [D, PROJ1], F32)
    ei("L1_attn_b_in", [PROJ1], F32)
    ei("L1_attn_conv_w", [CONV_DIM, 4], F32)
    ei("L1_attn_conv_b", [CONV_DIM], F32)
    ei("L1_attn_A_log", [H1], F32)
    ei("L1_attn_dt_bias", [H1], F32)
    ei("L1_attn_D_param", [H1], F32)
    ei("L1_attn_norm_w", [D], F32)
    ei("L1_attn_W_out", [D, D], F32)
    ei("L1_ffn_w_gate", [D, MLP1], I8)
    ei("L1_ffn_w_up", [D, MLP1], I8)
    ei("L1_ffn_w_down", [MLP1, D], I8)
    ei("L1_ffn_s_gate", [MLP1], F32)
    ei("L1_ffn_s_up", [MLP1], F32)
    ei("L1_ffn_s_down", [D], F32)
    t["y"] = nc.dram_tensor("y", [NB, L, D], F32, kind="ExternalOutput").ap()

    def sc(name, shape, dt):
        t[name] = nc.dram_tensor(name, shape, dt, kind=over.get(name, "Internal")).ap()
    sc("xn1d", [D, L], BF16)
    sc("bmcm", [64, L], F32)
    sc("x1b", [L, D], F32)
    sc("yb", [L, D], F32)
    sc("x1", [NB, L, D], F32)
    sc("qT", [D, L], BF16)
    sc("kT", [HK * HD, L], BF16)
    sc("vT", [HK * HD, L], BF16)
    sc("gT", [D, L], BF16)
    sc("s0T", [D, L], BF16)
    sc("f0T", [D, L], BF16)
    sc("inter0", [MLP0, L], BF16)
    sc("Wg_dq", [MLP0, D], BF16)
    sc("Wu_dq", [MLP0, D], BF16)
    sc("Wd_dq", [D, MLP0], BF16)
    sc("gate1T", [D, L], BF16)
    sc("xsT", [D, L], BF16)
    sc("inter1", [MLP1, L], BF16)
    sc("s1T", [D, L], BF16)
    sc("f1T", [D, L], BF16)
    sc("rowbuf", [65536], F32)
    sc("csflat", [HQ, L], F32)
    sc("csflat1", [H1, L], F32)
    return t


SB_SPECS = [
    ("sbA", [128, 16, 1024], F32),
    ("sbB", [128, 16384], F32),
    ("sbW", [128, 4096], BF16),
    ("sbM", [128, 6144], F32),
    ("sbC", [128, 2048], F32),
    ("sbS", [128, 512], F32),
]
PS_SPECS = [
    ("psA", [128, 6, 512], F32),
    ("psTr", [128, 2, 1024], BF16),
]
SEM_NAMES = ["dma1", "dma2", "dma3", "mm1", "ev1", "sc0", "gp1"]


def build_ctx(nc, stack, over=None):
    ctx = Ctx(nc)
    ctx.t.update(declare_dram(nc, over))
    for name, shape, dt in SB_SPECS:
        ctx.t[name] = stack.enter_context(nc.sbuf_tensor(name, shape, dt)).ap()
    for name, shape, dt in PS_SPECS:
        ctx.t[name] = stack.enter_context(nc.psum_tensor(name, shape, dt)).ap()
    for name in SEM_NAMES:
        ctx.sems[name] = Sem(stack.enter_context(nc.semaphore(name)))
    return ctx


def psT(ctx, i):
    return ctx.t["psTr"][:, i, :]


def barrier(ctx):
    waits = [(s, s.n) for s in ctx.sems.values() if s.n > 0]
    for engname in ("sync", "tensor", "vector", "scalar", "gpsimd"):
        def _f(e, waits=waits):
            for s, v in waits:
                e.wait_ge(s.h, v)
        ctx.on(engname, _f)


def c_idf32(c): return c.t["sbC"][:, 0:128]
def c_mtri64(c): return c.t["sbC"][:, 128:192]
def c_mtri128(c): return c.t["sbC"][:, 192:320]
def c_negridgeI(c): return c.t["sbC"][:, 320:448]
def c_normwB(c): return c.t["sbC"][:, 448:576]
def c_epsln(c): return c.t["sbC"][:, 576:577]
def c_eps0(c): return c.t["sbC"][:, 577:578]
def c_onecol(c): return c.t["sbC"][:, 578:579]
def c_lnqs(c): return c.t["sbC"][:, 579:580]
def c_idbf16(c): return c.t["sbC"][:, 600:664].bitcast(BF16)
def c_dbrow(c): return c.t["sbC"][:, 728:760]
def c_normw1B(c): return c.t["sbC"][:, 768:1792].bitcast(BF16)


def emit_consts(ctx):
    t = ctx.t
    dma1, gp1, mm1, ev1 = ctx.sem("dma1"), ctx.sem("gp1"), ctx.sem("mm1"), ctx.sem("ev1")
    sbS, sbM, psA, c = t["sbS"], t["sbM"], t["psA"], t["sbC"]

    def f_sync(e):
        e.dma_start(out=sbS[0:1, 0:128], in_=t["L0_attn_norm_w"][None, :]).then_inc(dma1.h, 16)
        e.dma_start(out=sbS[0:1, 128:160], in_=t["L1_attn_D_param"][None, :]).then_inc(dma1.h, 16)
        e.dma_start(out=sbM[0:1, 0:D], in_=t["L1_attn_norm_w"][None, :]).then_inc(dma1.h, 16)
    ctx.on("sync", f_sync)
    dma1.n += 48
    ld = dma1.n

    def f_gp(e):
        e.memset(c[:, 0:128], 1.0)
        e.affine_select(c_idf32(ctx), c_idf32(ctx), [[1, 128]], ALU.is_equal, 0.0,
                        base=0, channel_multiplier=-1)
        e.memset(c[:, 128:192], 0.0)
        e.affine_select(c_mtri64(ctx), c_mtri64(ctx), [[1, 64]], ALU.is_ge, -1e30,
                        base=0, channel_multiplier=-1)
        e.memset(c[:, 192:320], 0.0)
        e.affine_select(c_mtri128(ctx), c_mtri128(ctx), [[1, 128]], ALU.is_ge, -1e30,
                        base=0, channel_multiplier=-1)
        e.memset(c[:, 320:448], -RIDGE)
        e.affine_select(c_negridgeI(ctx), c_negridgeI(ctx), [[1, 128]], ALU.is_equal, 0.0,
                        base=0, channel_multiplier=-1)
        e.memset(c_epsln(ctx), LNEPS)
        e.memset(c_eps0(ctx), EPS0)
        e.memset(c_onecol(ctx), 1.0)
        e.memset(c_lnqs(ctx), LNQS)
        e.memset(c_idbf16(ctx), 1.0)
        ins = e.affine_select(c_idbf16(ctx), c_idbf16(ctx), [[1, 128]], ALU.is_equal, 0.0,
                              base=0, channel_multiplier=-1)
        gp1.after(ins)
    ctx.on("gpsimd", f_gp)

    def f_vec0(e):
        ins = e.memset(c[0:1, 1792:1920], 1.0)
        ev1.after(ins)
    ctx.on("vector", f_vec0)
    od = ev1.n

    def f_mm(e, ld=ld, od=od):
        e.wait_ge(dma1.h, ld)
        e.wait_ge(ev1.h, od)
        e.matmul(psA[:, 0, 0:128], c[0:1, 1792:1920], sbS[0:1, 0:128], start=True, stop=True)
        e.matmul(psA[:, 1, 0:32], c[0:1, 1792:1920], sbS[0:1, 128:160], start=True, stop=True)
        for j in range(4):
            ins = e.matmul(psA[:, 2 + j, :], c[0:1, 1792:1920], sbM[0:1, ts(j, 512)],
                           start=True, stop=True)
        mm1.after(ins)
    ctx.on("tensor", f_mm)
    mmn = mm1.n

    def f_vec(e, mmn=mmn):
        e.wait_ge(mm1.h, mmn)
        e.tensor_copy(c_normwB(ctx), psA[:, 0, 0:128])
        e.tensor_copy(c_dbrow(ctx), psA[:, 1, 0:32])
        for j in range(4):
            ins = e.tensor_copy(c_normw1B(ctx)[:, ts(j, 512)], psA[:, 2 + j, :])
        ev1.after(ins)
    ctx.on("vector", f_vec)
    barrier(ctx)


def row_bcast(ctx, e_tensor, psum_slice, row_ap):
    return e_tensor.matmul(psum_slice, ctx.t["sbC"][0:1, 1792:1920], row_ap,
                           start=True, stop=True)


class TQ:
    def __init__(self, ctx, bf16=True):
        self.ctx = ctx
        self.bf16 = bf16
        self.reqs = []

    def add(self, src, dst):
        self.reqs.append((src, dst))

    def add_multi(self, src, dsts):
        self.reqs.append((src, dsts))

    def emit(self, wait_sems=()):
        ctx = self.ctx
        mm1, ev1 = ctx.sem("mm1"), ctx.sem("ev1")
        reqs = list(self.reqs)
        self.reqs = []
        base_mm = mm1.n
        bf = self.bf16
        evacs = [len(d) if isinstance(d, list) else 1 for _, d in reqs]
        cume = [0]
        for x in evacs:
            cume.append(cume[-1] + x)
        base_ev = ev1.n

        def f_t(e, reqs=reqs, ws=list(wait_sems), base_ev=base_ev, bf=bf, cume=cume):
            for s, v in ws:
                e.wait_ge(s.h, v)
            for i, (src, dst) in enumerate(reqs):
                if i >= 2:
                    e.wait_ge(ev1.h, base_ev + cume[i - 1])
                if bf:
                    ins = e.transpose(psT(ctx, i % 2)[:, 0:128], src, c_idbf16(ctx))
                else:
                    ins = e.transpose(ctx.t["psA"][:, 4 + i % 2, 0:128], src, c_idf32(ctx))
                mm1.after(ins)
        ctx.on("tensor", f_t)

        def f_v(e, reqs=reqs, base_mm=base_mm, bf=bf):
            for i, (src, dst) in enumerate(reqs):
                e.wait_ge(mm1.h, base_mm + i + 1)
                p = psT(ctx, i % 2) if bf else ctx.t["psA"][:, 4 + i % 2, 0:128]
                if isinstance(dst, list):
                    for cols, d in dst:
                        ins = e.tensor_copy(d, p[0:d.shape[0], cols])
                        ev1.after(ins)
                else:
                    ins = e.tensor_copy(dst, p[0:dst.shape[0], 0:dst.shape[1]])
                    ev1.after(ins)
        ctx.on("vector", f_v)


def xn_view(ctx):
    return ctx.t["sbA"].rearrange("p a b -> p (a b)").bitcast(BF16) \
        .rearrange("p (k t) -> p k t", k=16)


def emit_ln_token(ctx, x_src_2d):
    t = ctx.t
    dma1, mm1, ev1, sc0 = ctx.sem("dma1"), ctx.sem("mm1"), ctx.sem("ev1"), ctx.sem("sc0")
    sbB, sbM, sbS = t["sbB"], t["sbM"], t["sbS"]
    xnv = xn_view(ctx)

    prev_tr_mm = mm1.n
    prev_sqred_ev = 0
    for i in range(16):
        j = i % 2
        xslot = sbB[:, ts(j, 2048)]
        prev_done = ev1.n

        def f_ld(e, i=i, xslot=xslot, prev=prev_done):
            if i >= 2:
                e.wait_ge(ev1.h, prev)
            e.dma_start(out=xslot, in_=x_src_2d[ts(i, 128), :]).then_inc(dma1.h, 16)
        ctx.on("sync", f_ld)
        dma1.n += 16
        ld = dma1.n

        def f_sq(e, xslot=xslot, ld=ld, psq=prev_sqred_ev):
            e.wait_ge(dma1.h, ld)
            e.wait_ge(ev1.h, psq)
            ins = e.activation(sbM[:, 2048:4096], xslot, AF.Square)
            sc0.after(ins)
        ctx.on("scalar", f_sq)
        sq = sc0.n

        def f_stats(e, xslot=xslot, ld=ld, sq=sq):
            e.wait_ge(dma1.h, ld)
            sw(e, ev1, e.reduce_sum(sbS[:, 200:201], xslot, axis=AX))
            e.wait_ge(sc0.h, sq)
            sw(e, ev1, e.reduce_sum(sbS[:, 201:202], sbM[:, 2048:4096], axis=AX))
            e.tensor_scalar(sbS[:, 200:201], sbS[:, 200:201], 1.0 / D, None, ALU.mult)
            sw(e, ev1, e.tensor_scalar(sbS[:, 201:202], sbS[:, 201:202], 1.0 / D, None, ALU.mult))
            sw(e, ev1, e.tensor_scalar(sbS[:, 202:203], sbS[:, 200:201], sbS[:, 200:201],
                                       None, ALU.mult))
            ins = e.tensor_tensor(sbS[:, 201:202], sbS[:, 201:202], sbS[:, 202:203], ALU.subtract)
            ev1.after(ins)
        ctx.on("vector", f_stats)
        var_done = ev1.n

        def f_sqrt(e, v=var_done):
            e.wait_ge(ev1.h, v)
            ins = e.activation(sbS[:, 202:203], sbS[:, 201:202], AF.Sqrt, bias=c_epsln(ctx))
            sc0.after(ins)
        ctx.on("scalar", f_sqrt)
        sqr = sc0.n

        def f_apply(e, xslot=xslot, v=sqr, ptr=prev_tr_mm):
            e.wait_ge(sc0.h, v)
            e.wait_ge(mm1.h, ptr)
            sw(e, ev1, e.reciprocal(sbS[:, 203:204], sbS[:, 202:203]))
            ins = e.tensor_scalar(sbM[:, 0:1024].bitcast(BF16), xslot,
                                  sbS[:, 200:201], sbS[:, 203:204],
                                  ALU.subtract, ALU.mult)
            ev1.after(ins)
        ctx.on("vector", f_apply)
        xn_done = ev1.n

        tq = TQ(ctx, bf16=True)
        xn_tm = sbM[:, 0:1024].bitcast(BF16)
        for kk in range(16):
            tq.add(xn_tm[:, ts(kk, 128)], xnv[:, kk, ts(i, 128)])
        tq.emit(wait_sems=[(ev1, xn_done)])
        prev_tr_mm = mm1.n
        prev_sqred_ev = var_done
    barrier(ctx)


def wslab(ctx, buf):
    return ctx.t["sbW"].rearrange("p (u k m) -> p u k m", u=2, k=16)[:, buf]


def emit_proj_job(ctx, ji, wd, mcols, kt=16, mrows=128):
    t = ctx.t
    dma1, mm1, ev1 = ctx.sem("dma1"), ctx.sem("mm1"), ctx.sem("ev1")
    psA, sbM = t["psA"], t["sbM"]
    buf = ji % 2
    need_mm = ctx._slab_mm[buf]
    mw = mcols.stop - mcols.start

    def f_ld(e, wd=wd, mcols=mcols, buf=buf, need=need_mm, kt=kt, mw=mw):
        e.wait_ge(mm1.h, need)
        wv = wslab(ctx, buf)
        src = wd[:, mcols].rearrange("(k p) m -> p k m", p=128)
        e.dma_start(out=wv[:, 0:kt, 0:mw], in_=src).then_inc(dma1.h, 16)
    ctx.on("gpsimd", f_ld)
    dma1.n += 16
    slab_ld = dma1.n

    done = 0
    for th in range(2):
        slot = (2 * ji + th) % 3
        ev_need = ctx._slot_ev[slot]
        wv = wslab(ctx, buf)

        def f_mm(e, th=th, slot=slot, ld=slab_ld, ev_need=ev_need, wv=wv, kt=kt,
                 mw=mw):
            e.wait_ge(dma1.h, ld)
            if ev_need:
                e.wait_ge(ev1.h, ev_need)
            ins = None
            for k in range(kt):
                for tb in range(2):
                    ins = e.matmul(psA[0:mw, 2 * slot + tb, :], wv[:, k, 0:mw],
                                   xn_view(ctx)[:, k, ts(2 * th + tb, 512)],
                                   start=(k == 0), stop=(k == kt - 1))
            mm1.after(ins)
        ctx.on("tensor", f_mm)
        mmn = mm1.n

        gate = ctx._stage_gate if th == 0 else None

        def f_ev(e, th=th, slot=slot, mmn=mmn, gate=gate, mw=mw):
            e.wait_ge(mm1.h, mmn)
            if gate:
                e.wait_ge(gate[0].h, gate[1])
            e.tensor_copy(sbM[0:mw, th * 1024:th * 1024 + 512], psA[0:mw, 2 * slot, :])
            ins = e.tensor_copy(sbM[0:mw, th * 1024 + 512:(th + 1) * 1024],
                                psA[0:mw, 2 * slot + 1, :])
            ev1.after(ins)
        ctx.on("vector", f_ev)
        ctx._slot_ev[slot] = ev1.n
        done = ev1.n
    ctx._slab_mm[buf] = mm1.n
    return done


def emit_proj0(ctx):
    t = ctx.t
    dma1, dma2, mm1, ev1, sc0, gp1 = (ctx.sem("dma1"), ctx.sem("dma2"), ctx.sem("mm1"),
                                      ctx.sem("ev1"), ctx.sem("sc0"), ctx.sem("gp1"))
    sbM, sbS = t["sbM"], t["sbS"]
    ctx._slot_ev = {0: 0, 1: 0, 2: 0}
    ctx._slab_mm = [0, 0]
    ctx._stage_gate = None

    def f_taps(e):
        e.dma_start(out=sbS[:, 300:364].rearrange("p (k j) -> p k j", j=4),
                    in_=t["L0_attn_conv_q"].rearrange("(k p) j -> p k j", p=128)).then_inc(dma1.h, 16)
        e.dma_start(out=sbS[:, 364:396].rearrange("p (k j) -> p k j", j=4),
                    in_=t["L0_attn_conv_k"].rearrange("(k p) j -> p k j", p=128)).then_inc(dma1.h, 16)
        e.dma_start(out=sbS[:, 396:428].rearrange("p (k j) -> p k j", j=4),
                    in_=t["L0_attn_conv_v"].rearrange("(k p) j -> p k j", p=128)).then_inc(dma1.h, 16)
        e.dma_start(out=sbS[0:16, 428:429], in_=t["L0_attn_dt_bias"][:, None]).then_inc(dma1.h, 16)
        e.dma_start(out=sbS[32:48, 428:429], in_=t["L0_attn_bb"][:, None]).then_inc(dma1.h, 16)
        e.dma_start(out=sbS[64:80, 428:429], in_=t["L0_attn_balpha"][:, None]).then_inc(dma1.h, 16)
        e.dma_start(out=sbS[0:16, 430:431], in_=t["L0_attn_A_log"][:, None]).then_inc(dma1.h, 16)
    ctx.on("sync", f_taps)
    dma1.n += 7 * 16
    taps_ld = dma1.n

    def f_expa(e, v=taps_ld):
        e.wait_ge(dma1.h, v)
        ins = e.activation(sbS[0:16, 429:430], sbS[0:16, 430:431], AF.Exp)
        sc0.after(ins)
    ctx.on("scalar", f_expa)

    jobs = ([("q", t["L0_attn_Wq"], m, t["qT"], 300) for m in range(16)]
            + [("k", t["L0_attn_Wk"], m, t["kT"], 364) for m in range(8)]
            + [("v", t["L0_attn_Wv"], m, t["vT"], 396) for m in range(8)]
            + [("g", t["L0_attn_Wg"], m, t["gT"], None) for m in range(16)])

    for ji, (kind, wd, m, dst, tap0) in enumerate(jobs):
        stag_done = emit_proj_job(ctx, ji, wd, ts(m, 128))

        if kind != "g":
            def f_conv(e, tap0=tap0, m=m, sd=stag_done):
                e.wait_ge(ev1.h, sd)
                y = sbM[:, 0:2048]
                acc = sbM[:, 2048:4096]
                tap = lambda j: sbS[:, tap0 + 4 * m + j:tap0 + 4 * m + j + 1]
                sw(e, ev1, e.tensor_scalar(acc, y, tap(3), None, ALU.mult))
                sw(e, ev1, e.scalar_tensor_tensor(acc[:, 1:2048], y[:, 0:2047], tap(2),
                                                  acc[:, 1:2048], ALU.mult, ALU.add))
                sw(e, ev1, e.scalar_tensor_tensor(acc[:, 2:2048], y[:, 0:2046], tap(1),
                                                  acc[:, 2:2048], ALU.mult, ALU.add))
                ins = e.scalar_tensor_tensor(acc[:, 3:2048], y[:, 0:2045], tap(0),
                                             acc[:, 3:2048], ALU.mult, ALU.add)
                ev1.after(ins)
            ctx.on("vector", f_conv)
            conv_done = ev1.n

            def f_silu(e, cd=conv_done, pst=dma2.n):
                e.wait_ge(ev1.h, cd)
                e.wait_ge(dma2.h, pst)
                ins = e.activation(sbM[:, 4096 - 1024 + 1024:5120].bitcast(BF16),
                                   sbM[:, 2048:4096], AF.Silu)
                sc0.after(ins)
            ctx.on("scalar", f_silu)
            sdone = sc0.n

            def f_st(e, dst=dst, m=m, sd=sdone):
                e.wait_ge(sc0.h, sd)
                e.dma_start(out=dst[ts(m, 128), :], in_=sbM[:, 4096:5120].bitcast(BF16)) \
                    .then_inc(dma2.h, 16)
            ctx.on("gpsimd", f_st)
            dma2.n += 16
            ctx._stage_gate = (sc0, sdone)
        else:
            def f_cast(e, sd=stag_done, pst=dma2.n):
                e.wait_ge(ev1.h, sd)
                e.wait_ge(dma2.h, pst)
                ins = e.tensor_copy(sbM[:, 4096:5120].bitcast(BF16), sbM[:, 0:2048])
                ev1.after(ins)
            ctx.on("vector", f_cast)
            cdone = ev1.n

            def f_st(e, dst=dst, m=m, cd=cdone):
                e.wait_ge(ev1.h, cd)
                e.dma_start(out=dst[ts(m, 128), :], in_=sbM[:, 4096:5120].bitcast(BF16)) \
                    .then_inc(dma2.h, 16)
            ctx.on("gpsimd", f_st)
            dma2.n += 16
            ctx._stage_gate = (ev1, cdone)

    ji = len(jobs)
    buf = ji % 2
    need_mm = ctx._slab_mm[buf]

    def f_zero(e, buf=buf, need=need_mm):
        e.wait_ge(mm1.h, need)
        ins = e.memset(wslab(ctx, buf)[:, :, 0:80], 0.0)
        gp1.after(ins)
    ctx.on("gpsimd", f_zero)
    zd = gp1.n

    def f_lda(e, buf=buf, zd=zd):
        e.wait_ge(gp1.h, zd)
        wv = wslab(ctx, buf)
        e.dma_start(out=wv[:, 0:16, 0:16],
                    in_=t["L0_attn_Wa"].rearrange("(k p) m -> p k m", p=128)).then_inc(dma1.h, 16)
        e.dma_start(out=wv[:, 0:16, 32:48],
                    in_=t["L0_attn_Wb"].rearrange("(k p) m -> p k m", p=128)).then_inc(dma1.h, 16)
        e.dma_start(out=wv[:, 0:16, 64:80],
                    in_=t["L0_attn_Walpha"].rearrange("(k p) m -> p k m", p=128)).then_inc(dma1.h, 16)
    ctx.on("gpsimd", f_lda)
    dma1.n += 48
    slab_ld = dma1.n

    wv = wslab(ctx, buf)
    for th in range(2):
        slot = th
        ev_need = ctx._slot_ev[slot]

        def f_mm(e, th=th, slot=slot, ld=slab_ld, ev_need=ev_need, wv=wv):
            e.wait_ge(dma1.h, ld)
            if ev_need:
                e.wait_ge(ev1.h, ev_need)
            ins = None
            for k in range(16):
                for tb in range(2):
                    ins = e.matmul(ctx.t["psA"][0:80, 2 * slot + tb, :], wv[:, k, 0:80],
                                   xn_view(ctx)[:, k, ts(2 * th + tb, 512)],
                                   start=(k == 0), stop=(k == 15))
            mm1.after(ins)
        ctx.on("tensor", f_mm)
        mmn = mm1.n

        def f_ev(e, th=th, slot=slot, mmn=mmn):
            e.wait_ge(mm1.h, mmn)
            e.tensor_copy(sbM[0:80, 4096 + th * 1024:4096 + th * 1024 + 512],
                          ctx.t["psA"][0:80, 2 * slot, :])
            ins = e.tensor_copy(sbM[0:80, 4096 + th * 1024 + 512:4096 + (th + 1) * 1024],
                                ctx.t["psA"][0:80, 2 * slot + 1, :])
            ev1.after(ins)
        ctx.on("vector", f_ev)
        ctx._slot_ev[slot] = ev1.n
    barrier(ctx)


def emit_afam_post(ctx):
    t = ctx.t
    dma2, dma3, mm1, ev1, sc0, gp1 = (ctx.sem("dma2"), ctx.sem("dma3"), ctx.sem("mm1"),
                                      ctx.sem("ev1"), ctx.sem("sc0"), ctx.sem("gp1"))
    sbM, sbS, sbB, psA = t["sbM"], t["sbS"], t["sbB"], t["psA"]
    z = sbM[:, 4096:6144]
    za, zb, zal = z[0:16, :], z[32:48, :], z[64:80, :]
    CS = sbB[0:16, 0:2048]
    SCR = sbB[0:16, 2048:4096]
    LNQ = sbB[0:16, 4096:6144]
    BET = sbB[0:16, 6144:8192]
    WEN = sbB[0:16, 8192:10240]
    ALP = sbB[0:16, 10240:12288]
    FAM = sbM[:, 1024:3072]
    LNB = sbM[:, 3072:5120]
    biascol = lambda r0: sbS[r0:r0 + 16, 428:429]

    def f_zeros(e):
        ins = e.memset(sbS[0:16, 440:504], 0.0)
        gp1.after(ins)
    ctx.on("gpsimd", f_zeros)
    zzd = gp1.n

    def f_sc1(e):
        sw(e, sc0, e.activation(ALP, zal, AF.Sigmoid, bias=biascol(64)))
        sw(e, sc0, e.activation(BET, zb, AF.Sigmoid, bias=biascol(32)))
        sw(e, sc0, e.activation(CS, za, AF.Identity, bias=biascol(0)))
        sw(e, sc0, e.activation(SCR, CS, AF.Abs))
        sw(e, sc0, e.activation(SCR, SCR, AF.Exp, scale=-1.0))
        ins = e.activation(SCR, SCR, AF.Ln, bias=c_onecol(ctx)[0:16, :])
        sc0.after(ins)
    ctx.on("scalar", f_sc1)
    spd = sc0.n

    def f_glog(e, v=spd):
        e.wait_ge(sc0.h, v)
        sw(e, ev1, e.tensor_scalar(CS, CS, 0.0, None, ALU.max))
        sw(e, ev1, e.tensor_tensor(CS, CS, SCR, ALU.add))
        ins = e.tensor_scalar(CS, CS, sbS[0:16, 429:430], -1.0, ALU.mult, ALU.mult)
        ev1.after(ins)
    ctx.on("vector", f_glog)
    gld = ev1.n

    def f_scan(e, v=gld, zz=zzd):
        e.wait_ge(ev1.h, v)
        e.wait_ge(gp1.h, zz)
        ins = None
        for n in range(NCH):
            ins = e.tensor_tensor_scan(CS[:, ts(n, 64)], CS[:, ts(n, 64)],
                                       sbS[0:16, 440:504], 0.0, ALU.add, ALU.add)
        ev1.after(ins)
    ctx.on("vector", f_scan)
    csd = ev1.n

    def f_csout(e, v=csd):
        e.wait_ge(ev1.h, v)
        e.dma_start(out=t["csflat"][:, :], in_=CS).then_inc(dma2.h, 16)
    ctx.on("gpsimd", f_csout)
    dma2.n += 16

    def f_sc2(e, v=csd):
        e.wait_ge(ev1.h, v)
        sw(e, sc0, e.activation(LNQ, zb, AF.Identity, bias=biascol(32)))
        sw(e, sc0, e.activation(SCR, LNQ, AF.Abs))
        sw(e, sc0, e.activation(SCR, SCR, AF.Exp, scale=-1.0))
        ins = e.activation(SCR, SCR, AF.Ln, bias=c_onecol(ctx)[0:16, :])
        sc0.after(ins)
    ctx.on("scalar", f_sc2)
    lsd = sc0.n

    def f_lnbq(e, v=lsd):
        e.wait_ge(sc0.h, v)
        sw(e, ev1, e.tensor_scalar(LNQ, LNQ, 0.0, LNQS, ALU.min, ALU.add))
        ins = e.tensor_tensor(LNQ, LNQ, SCR, ALU.subtract)
        ev1.after(ins)
    ctx.on("vector", f_lnbq)
    lbd = ev1.n

    def f_wpre(e, v=csd):
        e.wait_ge(ev1.h, v)
        ins = None
        for n in range(NCH):
            gcol = CS[:, 64 * n + 63:64 * n + 64]
            ins = e.activation(WEN[:, ts(n, 64)], CS[:, ts(n, 64)], AF.Exp,
                               scale=-1.0, bias=gcol)
        sc0.after(ins)
    ctx.on("scalar", f_wpre)
    wpd = sc0.n

    def f_wend(e, v=wpd):
        e.wait_ge(sc0.h, v)
        ins = e.tensor_tensor(WEN, WEN, BET, ALU.mult)
        ev1.after(ins)
    ctx.on("vector", f_wend)
    wed = ev1.n

    def f_expg(e, v=csd):
        e.wait_ge(ev1.h, v)
        ins = e.activation(sbS[0:16, 440:472], CS[:, 63:2048:64], AF.Exp)
        sc0.after(ins)
    ctx.on("scalar", f_expg)
    egd = sc0.n

    def f_flat(e, v=egd, base=dma3.n):
        e.wait_ge(sc0.h, v)
        e.dma_start(out=t["rowbuf"][0:512], in_=sbS[0:16, 440:472]).then_inc(dma3.h, 16)
        e.wait_ge(dma3.h, base + 16)
        e.dma_start(out=sbB[0:1, 12288:12800], in_=t["rowbuf"][None, 0:512]).then_inc(dma3.h, 16)
    ctx.on("gpsimd", f_flat)
    dma3.n += 32
    fld = dma3.n

    def f_bc(e, v=fld):
        e.wait_ge(dma3.h, v)
        ins = row_bcast(ctx, e, psA[:, 0, 0:512], sbB[0:1, 12288:12800])
        mm1.after(ins)
    ctx.on("tensor", f_bc)
    bcd = mm1.n

    def f_bcev(e, v=bcd):
        e.wait_ge(mm1.h, v)
        ins = e.tensor_copy(sbM[:, 5120:5632], psA[:, 0, 0:512])
        ev1.after(ins)
    ctx.on("vector", f_bcev)
    dgd = ev1.n

    def f_place(e, v=max(wed, lbd, dgd, csd)):
        e.wait_ge(ev1.h, v)
        sw(e, ev1, e.tensor_copy(FAM[0:16, :], CS))
        sw(e, ev1, e.tensor_copy(FAM[32:48, :], WEN))
        sw(e, ev1, e.tensor_copy(FAM[64:80, :], ALP))
        ins = e.tensor_copy(LNB[0:16, :], LNQ)
        ev1.after(ins)
    ctx.on("vector", f_place)
    pld = ev1.n

    tq = TQ(ctx, bf16=False)
    famT = sbM[:, 0:1024]
    for i in range(16):
        tq.add_multi(FAM[:, ts(i, 128)], [
            (slice(0, 16), famT[:, 64 * i + 0:64 * i + 16]),
            (slice(32, 48), famT[:, 64 * i + 16:64 * i + 32]),
            (slice(64, 80), famT[:, 64 * i + 32:64 * i + 48]),
        ])
        tq.add_multi(LNB[:, ts(i, 128)], [
            (slice(0, 16), famT[:, 64 * i + 48:64 * i + 64]),
        ])
    tq.emit(wait_sems=[(ev1, pld)])
    barrier(ctx)


def fam_csT(ctx, i):   return ctx.t["sbM"][:, 64 * i + 0:64 * i + 16]
def fam_wendT(ctx, i): return ctx.t["sbM"][:, 64 * i + 16:64 * i + 32]
def fam_alphaT(ctx, i): return ctx.t["sbM"][:, 64 * i + 32:64 * i + 48]
def fam_lnbqT(ctx, i): return ctx.t["sbM"][:, 64 * i + 48:64 * i + 64]
def dgB(ctx):          return ctx.t["sbM"][:, 5120:5632]


class PsumPool:
    SLOTS = [(4, 0, 512), (5, 0, 512)]

    def __init__(self, ctx):
        self.ctx = ctx
        self.i = 0
        self.marks = [[] for _ in range(2)]

    def acquire(self, width):
        idx = self.i % 2
        self.i += 1
        bank, lo, hi = self.SLOTS[idx]
        assert width <= hi - lo
        waits = []
        for j, (b, _, _) in enumerate(self.SLOTS):
            if b == bank:
                waits.extend(self.marks[j])
                self.marks[j] = [] if j == idx else self.marks[j]
        ap = self.ctx.t["psA"][:, bank, lo:lo + width]
        return idx, ap, waits

    def release(self, idx, marks):
        self.marks[idx] = list(marks)


def emit_chunk_solve(ctx):
    t = ctx.t
    dma1, dma2, dma3 = ctx.sem("dma1"), ctx.sem("dma2"), ctx.sem("dma3")
    mm1, ev1, sc0, gp1 = ctx.sem("mm1"), ctx.sem("ev1"), ctx.sem("sc0"), ctx.sem("gp1")
    sbB, sbM, sbS, psA = t["sbB"], t["sbM"], t["sbS"], t["psA"]
    pool = PsumPool(ctx)
    ctx._oslab_free = 0

    Hst = lambda h: sbB[:, 128 * h:128 * (h + 1)]
    Bst = lambda h: sbB[:, 2048 + 128 * h:2048 + 128 * (h + 1)]
    negA = lambda h: sbB[:, 4096 + 64 * h:4096 + 64 * (h + 1)].bitcast(BF16)
    bbuf = lambda h: sbB[:, 5120 + 64 * h:5120 + 64 * (h + 1)].bitcast(BF16)
    dbuf = lambda h: sbB[:, 6144 + 64 * h:6144 + 64 * (h + 1)].bitcast(BF16)
    xsol = lambda h: sbB[:, 7168 + 128 * h:7168 + 128 * (h + 1)]
    kctm = sbB[:, 9216:9728].bitcast(BF16)
    vctm = sbB[:, 9728:10240].bitcast(BF16)
    qpair = sbB[:, 10240:11264].bitcast(BF16)
    gcsl = sbB[:, 11264:12288].bitcast(BF16)
    kpair = sbB[:, 12288:12800].bitcast(BF16)
    csrow = sbB[0:1, 12800:14848]
    exprow = sbB[0:1, 14848:14912]
    rn2 = sbB[:, 14912:14928]
    crow = sbB[0:1, 15360:16384]
    coef = sbM[:, 2048:2112]
    cBv = sbM[:, 2560:3584]
    oslab = sbM[:, 1024:2048].bitcast(BF16)
    Xbf = sbM[:, 3584:3648].bitcast(BF16)
    qexp = sbM[:, 3648:3680].bitcast(BF16)
    dmat = sbM[:, 3680:3744]
    sct = sbM[:, 3744:3776].bitcast(BF16)
    o1 = sbM[:, 3776:3904]
    osq = sbM[:, 3904:4032]
    sgl = sbM[:, 4032:4096].bitcast(BF16)
    tbufs = sbM[:, 4096 - 0 + 512 - 512:4096]
    tbuf = lambda h: sbM[:, 4352 + 64 * (h % 4):4352 + 64 * (h % 4) + 64].bitcast(BF16)
    vc0 = sbM[:, 4608:4672].bitcast(BF16)
    o3b = sbM[:, 4672:4736].bitcast(BF16)
    cols3 = sbS[:, 213:216]
    trc = sbS[:, 504:505]
    ones64 = sbS[:, 505:506]
    expG = sbS[0:16, 440:472]
    rstd_ = sbS[:, 216:217]

    def f_init(e):
        ins = e.memset(sbB[:, 0:4096], 0.0)
        gp1.after(ins)
    ctx.on("gpsimd", f_init)

    def f_init2(e):
        sw(e, ev1, e.memset(trc, 0.0))
        ins = e.memset(ones64, 1.0)
        ev1.after(ins)
    ctx.on("vector", f_init2)

    barrier(ctx)
    cheb_bank_t = {0: 0, 1: 0, 2: 0, 3: 0}

    for n in range(NCH):
        p, half = n // 2, n % 2
        c0 = half * 64

        if half == 0:
            pe_, pm_ = ev1.n, mm1.n

            def f_pl(e, p=p, pe=pe_, pm=pm_):
                e.wait_ge(ev1.h, pe)
                e.wait_ge(mm1.h, pm)
                e.dma_start(out=kctm, in_=t["kT"][:, ts(p, 128)], transpose=True) \
                    .then_inc(dma1.h, 16)
                e.dma_start(out=vctm, in_=t["vT"][:, ts(p, 128)], transpose=True) \
                    .then_inc(dma1.h, 16)
                e.dma_start(out=gcsl, in_=t["gT"][:, ts(p, 128)], transpose=True) \
                    .then_inc(dma1.h, 16)
                e.dma_start(out=qpair.rearrange("p (k c) -> p k c", k=16),
                            in_=t["qT"][:, ts(p, 128)].rearrange("(k q) c -> q k c", q=128)) \
                    .then_inc(dma1.h, 16)
                e.dma_start(out=kpair.rearrange("p (k c) -> p k c", k=8),
                            in_=t["kT"][:, ts(p, 128)].rearrange("(k q) c -> q k c", q=128)) \
                    .then_inc(dma1.h, 16)
                e.dma_start(out=csrow.rearrange("a (h c) -> a h c", h=16),
                            in_=t["csflat"][:, ts(p, 128)][None, :, :]) \
                    .then_inc(dma1.h, 16)
            ctx.on("sync", f_pl)
            dma1.n += 96
        pair_ld = dma1.n

        base_sq = sc0.n
        base_red = ev1.n
        sq_of = {kv: base_sq + kv + 1 for kv in range(8)}
        red_of = {kv: base_red + 2 * (kv + 1) for kv in range(8)}
        sqbuf = lambda kv: (osq if kv % 2 == 0 else o1)

        def f_rn2_sc(e, c0=c0, ld=pair_ld, pv=base_red):
            e.wait_ge(dma1.h, ld)
            e.wait_ge(ev1.h, pv)
            for kv in range(8):
                if kv >= 2:
                    e.wait_ge(ev1.h, red_of[kv - 2])
                ins = e.activation(sqbuf(kv)[0:64, :], kctm[c0:c0 + 64, ts(kv, 128)],
                                   AF.Square)
                sc0.after(ins)
        ctx.on("scalar", f_rn2_sc)
        assert sc0.n == base_sq + 8

        def f_rn2_v(e, c0=c0):
            for kv in range(8):
                e.wait_ge(sc0.h, sq_of[kv])
                sw(e, ev1, e.reduce_sum(rn2[c0:c0 + 64, 2 * kv:2 * kv + 1],
                                        sqbuf(kv)[0:64, :], axis=AX))
                ins = e.tensor_copy(rn2[c0:c0 + 64, 2 * kv + 1:2 * kv + 2],
                                    rn2[c0:c0 + 64, 2 * kv:2 * kv + 1])
                ev1.after(ins)
        ctx.on("vector", f_rn2_v)
        assert ev1.n == base_red + 16
        rn2_done = ev1.n

        def f_wr(e, n=n, c0=c0, v=rn2_done):
            e.wait_ge(ev1.h, v)
            ins = e.tensor_tensor(sbM[:, 2112:2128][c0:c0 + 64, :],
                                  sbM[c0:c0 + 64, 64 * (n // 2) + 16:64 * (n // 2) + 32],
                                  rn2[c0:c0 + 64, :], ALU.mult)
            ev1.after(ins)
        ctx.on("vector", f_wr)
        wrd = ev1.n

        idx, ap_trd, waits = pool.acquire(16)
        def f_trd(e, ap=ap_trd, v=wrd, waits=waits, c0=c0):
            e.wait_ge(ev1.h, v)
            for s, vv in waits:
                e.wait_ge(s.h, vv)
            ins = e.matmul(ap[0:16, 0:1], sbM[:, 2112:2128][c0:c0 + 64, :],
                           ones64[c0:c0 + 64, :], start=True, stop=True)
            mm1.after(ins)
        ctx.on("tensor", f_trd)
        trd_mm = mm1.n

        def f_coef(e, n=n, v=trd_mm, ap=ap_trd):
            e.wait_ge(mm1.h, v)
            C = coef
            th_ = sbS[:, 506:507]; dl_ = sbS[:, 507:508]; sg1 = sbS[:, 508:509]
            rho = sbS[:, 509:510]; tmp = sbS[:, 510:511]; i2d = sbS[:, 511:512]
            sw(e, ev1, e.tensor_scalar(th_[0:16, :], trc[0:16, :], 128 * RIDGE + RIDGE, 0.5,
                                       ALU.add, ALU.mult))
            sw(e, ev1, e.tensor_scalar(dl_[0:16, :], trc[0:16, :], 128 * RIDGE - RIDGE, 0.5,
                                       ALU.add, ALU.mult))
            sw(e, ev1, e.scalar_tensor_tensor(trc[0:16, :], trc[0:16, :], expG[:, n:n + 1],
                                              ap[0:16, 0:1], ALU.mult, ALU.add))
            sw(e, ev1, e.reciprocal(tmp[0:16, :], dl_[0:16, :]))
            sw(e, ev1, e.tensor_scalar(i2d[0:16, :], tmp[0:16, :], 2.0, None, ALU.mult))
            sw(e, ev1, e.tensor_scalar(sg1[0:16, :], tmp[0:16, :], th_[0:16, :], None, ALU.mult))
            sw(e, ev1, e.reciprocal(tmp[0:16, :], th_[0:16, :]))
            sw(e, ev1, e.tensor_copy(C[0:16, 63:64], tmp[0:16, :]))
            sw(e, ev1, e.tensor_scalar(rho[0:16, :], tmp[0:16, :], dl_[0:16, :], None, ALU.mult))
            for i in range(NUM_ITER - 1):
                sw(e, ev1, e.scalar_tensor_tensor(tmp[0:16, :], sg1[0:16, :], 2.0,
                                                  rho[0:16, :], ALU.mult, ALU.subtract))
                sw(e, ev1, e.reciprocal(tmp[0:16, :], tmp[0:16, :]))
                sw(e, ev1, e.tensor_scalar(C[0:16, i:i + 1], tmp[0:16, :], rho[0:16, :],
                                           None, ALU.mult))
                sw(e, ev1, e.tensor_scalar(C[0:16, 32 + i:33 + i], tmp[0:16, :],
                                           i2d[0:16, :], None, ALU.mult))
                sw(e, ev1, e.tensor_copy(rho[0:16, :], tmp[0:16, :]))
            ev1.after(e.engine_nop())
        ctx.on("vector", f_coef)
        coef_done = ev1.n
        pool.release(idx, [(ev1, coef_done)])

        def f_cflat(e, v=coef_done, base=dma3.n):
            e.wait_ge(ev1.h, v)
            e.dma_start(out=t["rowbuf"][0:1024], in_=coef[0:16, :]).then_inc(dma3.h, 16)
            e.wait_ge(dma3.h, base + 16)
            e.dma_start(out=crow, in_=t["rowbuf"][None, 0:1024]).then_inc(dma3.h, 16)
        ctx.on("gpsimd", f_cflat)
        dma3.n += 32
        cfl = dma3.n

        i1, apc1, w1 = pool.acquire(512)
        i2, apc2, w2 = pool.acquire(512)
        def f_cbc(e, v=cfl, apc1=apc1, apc2=apc2, w1=w1, w2=w2):
            e.wait_ge(dma3.h, v)
            for s, vv in w1 + w2:
                e.wait_ge(s.h, vv)
            e.matmul(apc1, ctx.t["sbC"][0:1, 1792:1920], crow[:, 0:512],
                     start=True, stop=True)
            ins = e.matmul(apc2, ctx.t["sbC"][0:1, 1792:1920], crow[:, 512:1024],
                           start=True, stop=True)
            mm1.after(ins)
        ctx.on("tensor", f_cbc)
        cbcd = mm1.n

        def f_cbev(e, v=cbcd, apc1=apc1, apc2=apc2):
            e.wait_ge(mm1.h, v)
            e.tensor_copy(cBv[:, 0:512], apc1)
            ins = e.tensor_copy(cBv[:, 512:1024], apc2)
            ev1.after(ins)
        ctx.on("vector", f_cbev)
        cbd = ev1.n
        pool.release(i1, [(ev1, cbd)])
        pool.release(i2, [(ev1, cbd)])
        cb_c1 = lambda h, i: cBv[:, 64 * h + i:64 * h + i + 1]
        cb_c2 = lambda h, i: cBv[:, 64 * h + 32 + i:64 * h + 33 + i]
        cb_ith = lambda h: cBv[:, 64 * h + 63:64 * h + 64]

        if getattr(ctx, "_dbg_chunk", None) == n:
            def f_dbgst(e, v=cbd, base=dma2.n):
                e.wait_ge(ev1.h, v)
                for h in range(HQ):
                    e.dma_start(out=ctx.t["d_Hst"][ts(h, 128), :], in_=Hst(h)).then_inc(dma2.h, 16)
                    e.dma_start(out=ctx.t["d_Bst"][ts(h, 128), :], in_=Bst(h)).then_inc(dma2.h, 16)
                e.dma_start(out=ctx.t["d_coef"], in_=coef[0:16, :]).then_inc(dma2.h, 16)
                e.wait_ge(dma2.h, base + 33 * 16)
            ctx.on("gpsimd", f_dbgst)
            dma2.n += 33 * 16
            dbg_hold = dma2.n

            def f_hold(e, v=dbg_hold):
                e.wait_ge(dma2.h, v)
            ctx.on("vector", f_hold)

        upd_marks = []
        for h in range(HQ):
            kv = h // 2

            def f_prep(e, h=h, v=cbd):
                e.wait_ge(ev1.h, v)
                sw(e, ev1, e.scalar_tensor_tensor(negA(h), Hst(h), -1.0, c_negridgeI(ctx),
                                                  ALU.mult, ALU.add))
                sw(e, ev1, e.tensor_copy(bbuf(h), Bst(h)))
                ins = e.tensor_scalar(dbuf(h), Bst(h), cb_ith(h), None, ALU.mult)
                ev1.after(ins)
            ctx.on("vector", f_prep)
            prep_done = ev1.n

            def f_kcw(e, h=h, kv=kv, n=n, c0=c0, v=prep_done):
                ins = e.tensor_scalar(sbM[:, 4032:4096].bitcast(BF16)[c0:c0 + 64, :],
                                      kctm[c0:c0 + 64, ts(kv, 128)],
                                      sbM[c0:c0 + 64, 64 * (n // 2) + 16 + h:64 * (n // 2) + 17 + h],
                                      None, ALU.mult)
                ev1.after(ins)
            ctx.on("vector", f_kcw)
            kcw_done = ev1.n

            idx, apdh, waits = pool.acquire(256)
            def f_dhdb(e, h=h, kv=kv, c0=c0, v=kcw_done, ap=apdh, waits=waits):
                e.wait_ge(ev1.h, v)
                for s, vv in waits:
                    e.wait_ge(s.h, vv)
                kcw = sbM[:, 4032:4096].bitcast(BF16)[c0:c0 + 64, :]
                e.matmul(ap[:, 0:128], kcw, kctm[c0:c0 + 64, ts(kv, 128)],
                         start=True, stop=True)
                ins = e.matmul(ap[:, 128:256], kcw, vctm[c0:c0 + 64, ts(kv, 128)],
                               start=True, stop=True)
                mm1.after(ins)
            ctx.on("tensor", f_dhdb)
            dh_mm = mm1.n

            def f_upd(e, h=h, n=n, v=dh_mm, ap=apdh):
                e.wait_ge(mm1.h, v)
                dgcol = dgB(ctx)[:, 32 * h + n:32 * h + n + 1]
                sw(e, ev1, e.scalar_tensor_tensor(Hst(h), Hst(h), dgcol, ap[:, 0:128],
                                                  ALU.mult, ALU.add))
                ins = e.scalar_tensor_tensor(Bst(h), Bst(h), dgcol, ap[:, 128:256],
                                             ALU.mult, ALU.add)
                ev1.after(ins)
            ctx.on("vector", f_upd)
            pool.release(idx, [(ev1, ev1.n)])
        prep_all = ev1.n

        if getattr(ctx, "_dbg_chunk", None) == n:
            def f_dbg3(e, v=prep_all, base=dma2.n):
                e.wait_ge(ev1.h, v)
                e.dma_start(out=ctx.t["d_negA"], in_=negA(0)).then_inc(dma2.h, 16)
                e.dma_start(out=ctx.t["d_bb"], in_=bbuf(0)).then_inc(dma2.h, 16)
                e.dma_start(out=ctx.t["d_d0"], in_=dbuf(0)).then_inc(dma2.h, 16)
                e.dma_start(out=ctx.t["d_cBv"], in_=cBv).then_inc(dma2.h, 16)
                e.wait_ge(dma2.h, base + 64)
            ctx.on("gpsimd", f_dbg3)
            dma2.n += 64

        if getattr(ctx, "_cheb_serial", False):
            x_of = {}
            for h in range(HQ):
                bank = h % 4
                rr = psA[:, bank, 128 * (h // 4):128 * (h // 4) + 128]
                pt = cheb_bank_t[bank]

                def f_pe1(e, h=h, rr=rr, v=prep_all, pt=pt):
                    e.wait_ge(ev1.h, v)
                    if pt:
                        e.wait_ge(sc0.h, pt)
                    e.wait_ge(gp1.h, gp1.n)
                    mm1.after(e.matmul(rr, c_idbf16(ctx), bbuf(h), start=True, stop=False))
                ctx.on("tensor", f_pe1)
                for i in range(NUM_ITER - 1):
                    dprev = ev1.n

                    def f_mm_i(e, h=h, rr=rr, i=i, dp=dprev, gp=gp1.n):
                        e.wait_ge(ev1.h, dp)
                        e.wait_ge(gp1.h, gp)
                        mm1.after(e.matmul(rr, negA(h), dbuf(h), start=False,
                                           stop=(i == NUM_ITER - 2)))
                    ctx.on("tensor", f_mm_i)
                    mmn = mm1.n

                    def f_t_i(e, h=h, rr=rr, i=i, v=mmn):
                        e.wait_ge(mm1.h, v)
                        ins = e.activation(tbuf(h), rr, AF.Identity, scale=cb_c2(h, i))
                        sc0.after(ins)
                    ctx.on("scalar", f_t_i)
                    tn = sc0.n

                    def f_x_i(e, h=h, i=i, dp=dprev):
                        e.wait_ge(ev1.h, dp)
                        if i == 0:
                            ins = e.tensor_copy(xsol(h), dbuf(h))
                        else:
                            ins = e.tensor_tensor(xsol(h), xsol(h), dbuf(h), ALU.add)
                        gp1.after(ins)
                    ctx.on("gpsimd", f_x_i)
                    xn_ = gp1.n

                    def f_d_i(e, h=h, i=i, tn=tn, xn_=xn_):
                        e.wait_ge(sc0.h, tn)
                        e.wait_ge(gp1.h, xn_)
                        ins = e.scalar_tensor_tensor(dbuf(h), dbuf(h), cb_c1(h, i),
                                                     tbuf(h), ALU.mult, ALU.add)
                        ev1.after(ins)
                    ctx.on("vector", f_d_i)
                dlast = ev1.n

                def f_x_f(e, h=h, dl=dlast):
                    e.wait_ge(ev1.h, dl)
                    ins = e.tensor_tensor(xsol(h), xsol(h), dbuf(h), ALU.add)
                    gp1.after(ins)
                ctx.on("gpsimd", f_x_f)
                cheb_bank_t[bank] = sc0.n
            x_of = {(h, NUM_ITER - 1): gp1.n for h in range(HQ)}
            mm_of = {}
        else:
            emit_cheb_pipelined = True

        if getattr(ctx, "_cheb_serial", False):
            pass
        else:
          mm_of = {}
          t_of = {}
          d_of = {}
          x_of = {}

          for h in range(HQ):
              d_of[(h, 0)] = prep_all

          base_mm = mm1.n
          cnt = base_mm
          for h in range(HQ):
              cnt += 1
              mm_of[(h, -1)] = cnt
          for i in range(NUM_ITER - 1):
              for h in range(HQ):
                  cnt += 1
                  mm_of[(h, i)] = cnt
          base_sc = sc0.n
          scnt = base_sc
          for i in range(NUM_ITER - 1):
              for h in range(HQ):
                  scnt += 1
                  t_of[(h, i)] = scnt
          base_ev = ev1.n
          ecnt = base_ev
          for i in range(NUM_ITER - 1):
              for h in range(HQ):
                  ecnt += 1
                  d_of[(h, i + 1)] = ecnt
          base_gp = gp1.n
          gcnt = base_gp
          for i in range(NUM_ITER):
              for h in range(HQ):
                  gcnt += 1
                  x_of[(h, i)] = gcnt

          def f_pe(e, v=prep_all, cb=dict(cheb_bank_t)):
              e.wait_ge(ev1.h, v)
              for h in range(HQ):
                  bank = h % 4
                  if cb[bank]:
                      e.wait_ge(sc0.h, cb[bank])
                      cb[bank] = 0
                  rr = psA[:, bank, 128 * (h // 4):128 * (h // 4) + 128]
                  mm1.after(e.matmul(rr, c_idbf16(ctx), bbuf(h), start=True, stop=False))
              for i in range(NUM_ITER - 1):
                  for h in range(HQ):
                      bank = h % 4
                      e.wait_ge(ev1.h, d_of[(h, i)])
                      hp = h - 4 if h >= 4 else h + 12
                      ip = i if h >= 4 else i - 1
                      if (hp, ip) in t_of:
                          e.wait_ge(sc0.h, t_of[(hp, ip)])
                      rr = psA[:, bank, 128 * (h // 4):128 * (h // 4) + 128]
                      mm1.after(e.matmul(rr, negA(h), dbuf(h), start=False,
                                         stop=(i == NUM_ITER - 2)))
          ctx.on("tensor", f_pe)
          mm1.n = cnt

          def f_sc(e):
              for i in range(NUM_ITER - 1):
                  for h in range(HQ):
                      bank = h % 4
                      e.wait_ge(mm1.h, mm_of[(h, i)])
                      if i > 0 or h >= 4:
                          hp = h - 4 if h >= 4 else h + 12
                          ip = i if h >= 4 else i - 1
                          e.wait_ge(ev1.h, d_of[(hp, ip + 1)])
                      rr = psA[:, bank, 128 * (h // 4):128 * (h // 4) + 128]
                      ins = e.activation(tbuf(h), rr, AF.Identity, scale=cb_c2(h, i))
                      sc0.after(ins)
          ctx.on("scalar", f_sc)
          assert sc0.n == scnt, (sc0.n, scnt)

          def f_dv(e):
              for i in range(NUM_ITER - 1):
                  for h in range(HQ):
                      e.wait_ge(sc0.h, t_of[(h, i)])
                      e.wait_ge(gp1.h, x_of[(h, i)])
                      ins = e.scalar_tensor_tensor(dbuf(h), dbuf(h), cb_c1(h, i), tbuf(h),
                                                   ALU.mult, ALU.add)
                      ev1.after(ins)
          ctx.on("vector", f_dv)
          assert ev1.n == ecnt, (ev1.n, ecnt)

          def f_gp(e):
              for i in range(NUM_ITER):
                  for h in range(HQ):
                      e.wait_ge(ev1.h, d_of[(h, i)])
                      if i == 0:
                          ins = e.tensor_copy(xsol(h), dbuf(h))
                      else:
                          ins = e.tensor_tensor(xsol(h), xsol(h), dbuf(h), ALU.add)
                      gp1.after(ins)
          ctx.on("gpsimd", f_gp)
          assert gp1.n == gcnt, (gp1.n, gcnt)
          for b in range(4):
              hh = [h for h in range(HQ) if h % 4 == b]
              cheb_bank_t[b] = max(t_of[(h, NUM_ITER - 2)] for h in hh)
          cheb_done_ev = ecnt
          cheb_done_gp = gcnt

        if getattr(ctx, "_dbg_chunk", None) == n:
            def f_dbgx(e, base=dma2.n):
                for h in range(HQ):
                    e.wait_ge(gp1.h, x_of[(h, NUM_ITER - 1)])
                    e.dma_start(out=ctx.t["d_X"][ts(h, 128), :], in_=xsol(h)).then_inc(dma2.h, 16)
                e.wait_ge(dma2.h, base + 16 * 16)
            ctx.on("gpsimd", f_dbgx)
            dma2.n += 16 * 16

        prev_mmo = 0
        prev_mmA = 0
        prev_rmsd = 0
        for h in range(HQ):
            kv = h // 2

            def f_xbf(e, h=h, kv=kv, n=n, p=p, c0=c0, v=x_of[(h, NUM_ITER - 1)],
                      pm=prev_mmo, po=getattr(ctx, "_prev_oeps", 0)):
                e.wait_ge(gp1.h, v)
                if pm:
                    e.wait_ge(mm1.h, pm)
                sw(e, ev1, e.tensor_copy(Xbf, xsol(h)))
                if h % 2 == 0:
                    if po:
                        e.wait_ge(ev1.h, po)
                    sw(e, ev1, e.tensor_copy(vc0[0:64, :], vctm[c0:c0 + 64, ts(kv, 128)]))
                sw(e, ev1, e.tensor_copy(cols3[0:64, 0:1],
                                         sbM[c0:c0 + 64, 64 * p + h:64 * p + h + 1]))
                sw(e, ev1, e.tensor_copy(cols3[0:64, 1:2],
                                         sbM[c0:c0 + 64, 64 * p + 48 + h:64 * p + 49 + h]))
                ins = e.tensor_copy(cols3[0:64, 2:3],
                                    sbM[c0:c0 + 64, 64 * p + 32 + h:64 * p + 33 + h])
                ev1.after(ins)
            ctx.on("vector", f_xbf)
            xbf_done = ev1.n

            def f_exr(e, h=h, c0=c0, ld=pair_ld, pm=prev_mmA):
                e.wait_ge(dma1.h, ld)
                if pm:
                    e.wait_ge(mm1.h, pm)
                ins = e.activation(exprow, csrow[:, 128 * h + c0:128 * h + c0 + 64],
                                   AF.Exp, bias=c_lnqs(ctx)[0:1, :])
                sc0.after(ins)
            ctx.on("scalar", f_exr)
            exr_done = sc0.n

            iA, apA, wA = pool.acquire(192)
            def f_mmA(e, h=h, kv=kv, c0=c0, v=exr_done, ap=apA, waits=wA):
                for s, vv in waits:
                    e.wait_ge(s.h, vv)
                e.wait_ge(sc0.h, v)
                e.matmul(ap[0:128, 0:64], ctx.t["sbC"][0:1, 1792:1920],
                         csrow[:, 128 * h + c0:128 * h + c0 + 64], start=True, stop=True)
                e.matmul(ap[0:128, 64:128], ctx.t["sbC"][0:1, 1792:1920], exprow,
                         start=True, stop=True)
                ins = e.matmul(ap[0:64, 128:192], kpair[:, ts(kv, 128)][:, c0:c0 + 64],
                               qpair[:, ts(h, 128)][:, c0:c0 + 64], start=True, stop=True)
                mm1.after(ins)
            ctx.on("tensor", f_mmA)
            mmA = mm1.n

            def f_dmq(e, h=h, n=n, p=p, c0=c0, v=mmA, ap=apA, pm=prev_mmo):
                e.wait_ge(mm1.h, v)
                if pm:
                    e.wait_ge(mm1.h, pm)
                sw(e, ev1, e.tensor_scalar(dmat[0:64, :], ap[0:64, 0:64], cols3[0:64, 0:1],
                                           None, ALU.subtract))
                sw(e, ev1, e.tensor_tensor(dmat[0:64, :], dmat[0:64, :], c_mtri64(ctx)[0:64, :],
                                           ALU.add))
                ins = e.tensor_tensor(qexp[:, 0:64], qpair[:, ts(h, 128)][:, c0:c0 + 64],
                                      ap[0:128, 64:128][:, 0:64], ALU.mult)
                ev1.after(ins)
            ctx.on("vector", f_dmq)
            dmq = ev1.n

            def f_dexp(e, h=h, p=p, c0=c0, v=dmq):
                e.wait_ge(ev1.h, v)
                ins = e.activation(dmat[0:64, :], dmat[0:64, :], AF.Exp,
                                   bias=cols3[0:64, 1:2])
                sc0.after(ins)
            ctx.on("scalar", f_dexp)
            dexp = sc0.n

            def f_sct(e, h=h, v=dexp, ap=apA):
                e.wait_ge(sc0.h, v)
                ins = e.tensor_tensor(sct[0:64, :], dmat[0:64, :], ap[0:64, 128:192], ALU.mult)
                ev1.after(ins)
            ctx.on("vector", f_sct)
            sctd = ev1.n
            pool.release(iA, [(ev1, sctd)])

            iB, apB, wB = pool.acquire(128)
            def f_mmo(e, h=h, kv=kv, c0=c0, vx=xbf_done, vs=sctd, ap=apB, waits=wB):
                e.wait_ge(ev1.h, max(vx, vs))
                for s, vv in waits:
                    e.wait_ge(s.h, vv)
                e.matmul(ap[0:64, :], qexp[:, 0:64], Xbf, start=True, stop=False)
                ins = e.matmul(ap[0:64, :], sct[0:64, :],
                               vc0[0:64, :], start=False, stop=True)
                mm1.after(ins)
            ctx.on("tensor", f_mmo)
            mmo = mm1.n

            def f_oeps(e, h=h, kv=kv, n=n, p=p, c0=c0, v=mmo, ap=apB):
                e.wait_ge(mm1.h, v)
                ins = e.scalar_tensor_tensor(o1[0:64, :], vc0[0:64, :],
                                             cols3[0:64, 2:3], ap[0:64, :], ALU.mult, ALU.add)
                ev1.after(ins)
            ctx.on("vector", f_oeps)
            o1d = ev1.n
            ctx._prev_oeps = o1d
            pool.release(iB, [(ev1, o1d)])

            def f_osq(e, v=o1d, pr=prev_rmsd):
                e.wait_ge(ev1.h, v)
                if pr:
                    e.wait_ge(ev1.h, pr)
                ins = e.activation(osq[0:64, :], o1[0:64, :], AF.Square)
                sc0.after(ins)
            ctx.on("scalar", f_osq)
            osqd = sc0.n

            def f_rms(e, v=osqd):
                e.wait_ge(sc0.h, v)
                sw(e, ev1, e.reduce_sum(rstd_[0:64, :], osq[0:64, :], axis=AX))
                ins = e.tensor_scalar(rstd_[0:64, :], rstd_[0:64, :], 1.0 / HD, None, ALU.mult)
                ev1.after(ins)
            ctx.on("vector", f_rms)
            rmsd = ev1.n

            def f_rsq(e, h=h, c0=c0, v=rmsd):
                e.wait_ge(ev1.h, v)
                sw(e, sc0, e.activation(rstd_[0:64, :], rstd_[0:64, :], AF.Sqrt,
                                        bias=c_eps0(ctx)[0:64, :]))
                ins = e.activation(sgl[0:64, :], gcsl[c0:c0 + 64, ts(h, 128)], AF.Silu)
                sc0.after(ins)
            ctx.on("scalar", f_rsq)
            rsqd = sc0.n

            def f_ofin(e, h=h, c0=c0, v=rsqd, of=getattr(ctx, "_oslab_free", 0)):
                e.wait_ge(sc0.h, v)
                if of:
                    e.wait_ge(mm1.h, of)
                sw(e, ev1, e.reciprocal(rstd_[0:64, :], rstd_[0:64, :]))
                sw(e, ev1, e.scalar_tensor_tensor(o1[0:64, :], o1[0:64, :], rstd_[0:64, :],
                                                  c_normwB(ctx)[0:64, :], ALU.mult, ALU.mult))
                sw(e, ev1, e.tensor_tensor(o3b[0:64, :], o1[0:64, :], sgl[0:64, :], ALU.mult))
                ins = e.tensor_copy(oslab[c0:c0 + 64, ts(h, 128)], o3b[0:64, :])
                ev1.after(ins)
            ctx.on("vector", f_ofin)
            prev_mmo = mmo
            prev_mmA = mmA
            prev_rmsd = rmsd
        out_done = ev1.n

        if half == 1:
            tq = TQ(ctx, bf16=True)
            oTv = xn_view(ctx)
            for h in range(HQ):
                tq.add(oslab[:, ts(h, 128)], oTv[:, h, ts(p, 128)])
            tq.emit(wait_sems=[(ev1, out_done)])
            ctx._oslab_free = mm1.n
    barrier(ctx)
    ctx._oslab_free = 0


def emit_dequant_one(ctx, wp, ws, wout, nrows, ncols_b):
    t = ctx.t
    dma1, dma2, ev1, gp1 = ctx.sem("dma1"), ctx.sem("dma2"), ctx.sem("ev1"), ctx.sem("gp1")
    sbB, sbM = t["sbB"], t["sbM"]
    nblk = ncols_b // 16
    W = ncols_b
    assert W in (1024, 8192)
    if W == 1024:
        RAW = sbB[:, 0:1024].bitcast(I32)
        NIB = lambda j: sbB[:, 1024 + 1024 * j:2048 + 1024 * j].bitcast(I32)
        BITS = sbB[:, 3072:4096].bitcast(I32)
        U16 = sbB[:, 4096:4608].bitcast(mybir.dt.uint16)
        SC = sbB[:, 5120:5184]
        SCE = sbB[:, 6144:7168]
        OUT = sbB[:, 8192:9216].bitcast(BF16)
    else:
        RAW = sbB[:, 0:8192].bitcast(I32)
        NIB = None
    dgq = Sem(ctx.sems["dma3"].h)
    dma3 = ctx.sem("dma3")

    n_slabs = nrows // 128
    CHUNKS = W // 1024
    for s_ in range(n_slabs):
        prev = ev1.n

        def f_ld(e, s_=s_, pv=getattr(ctx, "_dq_store", 0)):
            e.wait_ge(dma2.h, pv)
            e.dma_start(out=sbB[:, 0:W].bitcast(I32), in_=wp[ts(s_, 128), :]) \
                .then_inc(dma1.h, 16)
            e.dma_start(out=sbB[:, 15360:15360 + nblk], in_=ws[ts(s_, 128), :]) \
                .then_inc(dma1.h, 16)
        ctx.on("sync", f_ld)
        dma1.n += 32
        ld = dma1.n

        for cc in range(CHUNKS):
            raw = sbB[:, cc * 1024:cc * 1024 + 1024].bitcast(I32)
            nib = lambda j: sbB[:, 8192 + 1024 * j:9216 + 1024 * j].bitcast(I32)
            bits = sbB[:, 10240:11264].bitcast(I32)
            u16 = sbB[:, 11264:11776].bitcast(mybir.dt.uint16)
            sce = sbB[:, 12288:13312]
            outb = sbB[:, 13312:14336].bitcast(BF16)

            def f_dq(e, cc=cc, ld=ld, raw=raw, nib=nib, bits=bits, u16=u16,
                     sce=sce, outb=outb, s_=s_):
                e.wait_ge(dma1.h, ld)
                for j in range(2):
                    n_ = nib(j)
                    if j == 0:
                        sw(e, ev1, e.tensor_single_scalar(n_, raw, 15, ALU.bitwise_and))
                    else:
                        sw(e, ev1, e.tensor_single_scalar(n_, raw, 4, ALU.logical_shift_right))
                        sw(e, ev1, e.tensor_single_scalar(n_, n_, 15, ALU.bitwise_and))
                    sw(e, ev1, e.tensor_single_scalar(bits, n_, 8, ALU.bitwise_and))
                    sw(e, ev1, e.tensor_single_scalar(n_, n_, 7, ALU.bitwise_and))
                    t1 = sbB[:, 14336:15360].bitcast(I32)
                    sw(e, ev1, e.tensor_single_scalar(t1, n_, 2, ALU.min))
                    sw(e, ev1, e.tensor_tensor(t1, t1, n_, ALU.add))
                    sw(e, ev1, e.tensor_scalar(n_, n_, 1, 16000, ALU.min, ALU.mult))
                    sw(e, ev1, e.scalar_tensor_tensor(t1, t1, 64, n_, ALU.mult, ALU.add))
                    sw(e, ev1, e.scalar_tensor_tensor(bits, bits, 4096, t1, ALU.mult, ALU.add))
                    sw(e, ev1, e.tensor_copy(u16, bits))
                    sw(e, ev1, e.tensor_copy(
                        outb.rearrange("p (a b) -> p a b", b=2)[:, :, j],
                        u16.bitcast(BF16)))
                for jj in range(16):
                    sw(e, ev1, e.tensor_copy(
                        sce.rearrange("p (a b) -> p a b", b=16)[:, :, jj],
                        sbB[:, 15360 + cc * 64:15360 + cc * 64 + 64]))
                ins = e.tensor_tensor(outb, outb,
                                      sce.rearrange("p (a b) -> p a b", b=1)[:, :, 0]
                                      if False else sce, ALU.mult) if False else None
                ins = e.tensor_tensor(outb.rearrange("p (a b) -> p a b", b=2)[:, :, 0],
                                      outb.rearrange("p (a b) -> p a b", b=2)[:, :, 0],
                                      sce, ALU.mult)
                sw(e, ev1, ins)
                ins = e.tensor_tensor(outb.rearrange("p (a b) -> p a b", b=2)[:, :, 1],
                                      outb.rearrange("p (a b) -> p a b", b=2)[:, :, 1],
                                      sce, ALU.mult)
                ev1.after(ins)
            ctx.on("vector", f_dq)
            dqd = ev1.n

            def f_st(e, s_=s_, cc=cc, v=dqd, outb=outb):
                e.wait_ge(ev1.h, v)
                e.dma_start(out=wout[ts(s_, 128), cc * 2048:cc * 2048 + 2048], in_=outb) \
                    .then_inc(dma2.h, 16)
            ctx.on("gpsimd", f_st)
            dma2.n += 16
        ctx._dq_store = dma2.n
    barrier(ctx)


def emit_dequant(ctx):
    t = ctx.t
    ctx._dq_store = 0
    emit_dequant_one(ctx, t["L0_ffn_gate_W_packed"], t["L0_ffn_gate_scales"],
                     t["Wg_dq"], MLP0, D // 2)
    ctx._dq_store = 0
    emit_dequant_one(ctx, t["L0_ffn_up_W_packed"], t["L0_ffn_up_scales"],
                     t["Wu_dq"], MLP0, D // 2)
    ctx._dq_store = 0
    emit_dequant_one(ctx, t["L0_ffn_down_W_packed"], t["L0_ffn_down_scales"],
                     t["Wd_dq"], D, MLP0 // 2)


def emit_ffn0_gateup(ctx):
    t = ctx.t
    dma1, dma2, mm1, ev1, sc0 = (ctx.sem("dma1"), ctx.sem("dma2"), ctx.sem("mm1"),
                                 ctx.sem("ev1"), ctx.sem("sc0"))
    sbB, sbM, psA = t["sbB"], t["sbM"], t["psA"]
    slab = lambda mat, buf: sbB[:, (mat * 2 + buf) * 4096:(mat * 2 + buf + 1) * 4096] \
        .bitcast(BF16).rearrange("p (k m) -> p k m", k=16)
    NMACRO = MLP0 // 512
    prev_use = {0: 0, 1: 0}
    ctx._ffn_evfree = 0
    ctx._ffn_gelu = 0
    ctx._ffn_comb = 0

    for mac in range(NMACRO):
        buf = mac % 2
        need = prev_use[buf]

        def f_ld(e, mac=mac, buf=buf, need=need):
            e.wait_ge(mm1.h, need)
            for mat, wd in ((0, t["Wg_dq"]), (1, t["Wu_dq"])):
                for k in range(16):
                    e.dma_start(out=slab(mat, buf)[:, k, :],
                                in_=wd[ts(mac, 512), ts(k, 128)], transpose=True) \
                        .then_inc(dma1.h, 16)
        ctx.on("sync", f_ld)
        dma1.n += 32 * 16
        ld = dma1.n

        for nt in range(4):
            for mat in range(2):

                gate_gate = ctx._ffn_gelu if mat == 1 else 0

                def f_mm(e, mac=mac, buf=buf, nt=nt, mat=mat, ld=ld, gg=gate_gate):
                    e.wait_ge(dma1.h, ld)
                    if mat == 0:
                        e.wait_ge(ev1.h, ctx._ffn_evfree)
                    else:
                        e.wait_ge(sc0.h, gg)
                    ins = None
                    for k in range(16):
                        for tb in range(4):
                            ins = e.matmul(psA[:, tb, :],
                                           slab(mat, buf)[:, k, ts(nt, 512)[0].__index__() if False else slice(nt * 128, nt * 128 + 128)][:, :] if False else
                                           slab(mat, buf)[:, k, nt * 128:nt * 128 + 128],
                                           xn_view(ctx)[:, k, ts(tb, 512)],
                                           start=(k == 0), stop=(k == 15))
                    mm1.after(ins)
                ctx.on("tensor", f_mm)
                mmn = mm1.n

                if mat == 0:
                    def f_gelu(e, v=mmn, pc=ctx._ffn_comb):
                        e.wait_ge(mm1.h, v)
                        e.wait_ge(ev1.h, pc)
                        ins = None
                        for tb in range(4):
                            ins = e.activation(sbM[:, 2048 + tb * 256:2048 + tb * 256 + 256]
                                               .bitcast(BF16),
                                               psA[:, tb, :], AF.Gelu_apprx_tanh)
                        sc0.after(ins)
                    ctx.on("scalar", f_gelu)
                    ctx._ffn_gelu = sc0.n
                else:
                    def f_comb(e, v=mmn, gv=ctx._ffn_gelu, pst=dma2.n):
                        e.wait_ge(mm1.h, v)
                        e.wait_ge(sc0.h, gv)
                        e.wait_ge(dma2.h, pst)
                        ins = None
                        for tb in range(4):
                            ins = e.tensor_tensor(sbM[:, 3072 + tb * 256:3072 + tb * 256 + 256]
                                                  .bitcast(BF16),
                                                  sbM[:, 2048 + tb * 256:2048 + tb * 256 + 256]
                                                  .bitcast(BF16),
                                                  psA[:, tb, :], ALU.mult)
                        ev1.after(ins)
                    ctx.on("vector", f_comb)
                    ctx._ffn_evfree = ev1.n
                    ctx._ffn_comb = ev1.n
                    cmb = ev1.n

                    def f_st(e, mac=mac, nt=nt, v=cmb):
                        e.wait_ge(ev1.h, v)
                        e.dma_start(out=t["inter0"][ts(mac * 4 + nt, 128), :],
                                    in_=sbM[:, 3072:4096].bitcast(BF16)).then_inc(dma2.h, 16)
                    ctx.on("gpsimd", f_st)
                    dma2.n += 16
        prev_use[buf] = mm1.n
    barrier(ctx)
    ctx._ffn_evfree = 0
    ctx._ffn_evfree2 = 0


def acc_view(ctx):
    return ctx.t["sbA"]


def emit_ffn_down(ctx, inter_dram, wdq_dram, fout_dram, NK, int8=False, sdown_col=None):
    t = ctx.t
    dma1, dma2, mm1, ev1, sc0, gp1 = (ctx.sem("dma1"), ctx.sem("dma2"), ctx.sem("mm1"),
                                      ctx.sem("ev1"), ctx.sem("sc0"), ctx.sem("gp1"))
    sbB, sbM, psA = t["sbB"], t["sbM"], t["psA"]
    GK = 4
    NG = NK // GK
    islab = lambda buf: sbB[:, buf * 2048:(buf + 1) * 2048].bitcast(BF16) \
        .rearrange("p (k m) -> p k m", k=GK)
    wslab_ = lambda buf: sbB[:, 4096 + buf * 4096:4096 + (buf + 1) * 4096].bitcast(BF16) \
        .rearrange("p (k m) -> p k m", k=GK)
    acc = acc_view(ctx)
    prev_use = {0: 0, 1: 0}
    ctx._dn_ev = {0: 0, 1: 0}

    for th in range(2):
        def f_z(e, pv=ev1.n, ps=sc0.n):
            e.wait_ge(ev1.h, pv)
            e.wait_ge(sc0.h, ps)
            ins = e.memset(acc.rearrange("p a b -> p (a b)"), 0.0)
            gp1.after(ins)
        ctx.on("gpsimd", f_z)
        zd = gp1.n

        for g in range(NG):
            buf = g % 2
            need = prev_use[buf]

            def f_ld(e, th=th, g=g, buf=buf, need=need):
                e.wait_ge(mm1.h, need)
                e.dma_start(out=islab(buf),
                            in_=inter_dram[g * GK * 128:(g * GK + GK) * 128, ts(th, 1024)]
                            .rearrange("(k p) m -> p k m", p=128)).then_inc(dma1.h, 16)
                for k in range(GK):
                    nk = g * GK + k
                    if int8:
                        e.dma_start(out=wslab_(buf)[:, k, :], in_=wdq_dram[ts(nk, 128), :]) \
                            .then_inc(dma1.h, 16)
                    else:
                        e.dma_start(out=wslab_(buf)[:, k, :], in_=wdq_dram[:, ts(nk, 128)],
                                    transpose=True).then_inc(dma1.h, 16)
            eng = "gpsimd" if int8 else "sync"
            ctx.on(eng, f_ld)
            dma1.n += 16 * (1 + GK)
            ld = dma1.n

            for h2t in range(16):
                bk = h2t % 2

                def f_mm(e, g=g, buf=buf, h2t=h2t, bk=bk, ld=ld,
                         ev_need=None):
                    e.wait_ge(dma1.h, ld)
                    e.wait_ge(ev1.h, ctx._dn_ev.get(bk, 0))
                    ins = None
                    for k in range(GK):
                        for tb in range(2):
                            ins = e.matmul(psA[:, 2 * bk + tb, :],
                                           wslab_(buf)[:, k, h2t * 128:h2t * 128 + 128],
                                           islab(buf)[:, k, tb * 512:tb * 512 + 512],
                                           start=(k == 0), stop=(k == GK - 1))
                    mm1.after(ins)
                ctx.on("tensor", f_mm)
                mmn = mm1.n

                def f_add(e, h2t=h2t, bk=bk, v=mmn, zd=zd):
                    e.wait_ge(mm1.h, v)
                    e.wait_ge(gp1.h, zd)
                    e.tensor_tensor(acc[:, h2t, 0:512], acc[:, h2t, 0:512],
                                    psA[:, 2 * bk, :], ALU.add)
                    ins = e.tensor_tensor(acc[:, h2t, 512:1024], acc[:, h2t, 512:1024],
                                          psA[:, 2 * bk + 1, :], ALU.add)
                    ev1.after(ins)
                ctx.on("vector", f_add)
                ctx._dn_ev[bk] = ev1.n
            prev_use[buf] = mm1.n
        all_add = ev1.n

        for h2t in range(16):
            def f_silu(e, th=th, h2t=h2t, v=all_add, pst=dma2.n):
                e.wait_ge(ev1.h, v)
                e.wait_ge(dma2.h, pst)
                if sdown_col is not None:
                    ins = e.activation(sbM[:, 2048:2560].bitcast(BF16), acc[:, h2t, :],
                                       AF.Silu, scale=sdown_col(ctx, h2t))
                else:
                    ins = e.activation(sbM[:, 2048:2560].bitcast(BF16), acc[:, h2t, :],
                                       AF.Silu)
                sc0.after(ins)
            ctx.on("scalar", f_silu)
            sd = sc0.n

            def f_st(e, th=th, h2t=h2t, v=sd):
                e.wait_ge(sc0.h, v)
                e.dma_start(out=fout_dram[ts(h2t, 128), ts(th, 1024)],
                            in_=sbM[:, 2048:2560].bitcast(BF16)).then_inc(dma2.h, 16)
            ctx.on("gpsimd", f_st)
            dma2.n += 16
    barrier(ctx)


def emit_dense_proj(ctx, wd, out_dram, n_tiles=16):
    t = ctx.t
    dma1, dma2, mm1, ev1 = ctx.sem("dma1"), ctx.sem("dma2"), ctx.sem("mm1"), ctx.sem("ev1")
    sbM = t["sbM"]
    ctx._slot_ev = {0: 0, 1: 0, 2: 0}
    ctx._slab_mm = [0, 0]
    ctx._stage_gate = None
    for m in range(n_tiles):
        stag = emit_proj_job(ctx, m, wd, ts(m, 128))

        def f_cast(e, sd=stag, pst=dma2.n):
            e.wait_ge(ev1.h, sd)
            e.wait_ge(dma2.h, pst)
            ins = e.tensor_copy(sbM[:, 4096:5120].bitcast(BF16), sbM[:, 0:2048])
            ev1.after(ins)
        ctx.on("vector", f_cast)
        cd = ev1.n
        ctx._stage_gate = (ev1, cd)

        def f_st(e, m=m, v=cd):
            e.wait_ge(ev1.h, v)
            e.dma_start(out=out_dram[ts(m, 128), :], in_=sbM[:, 4096:5120].bitcast(BF16)) \
                .then_inc(dma2.h, 16)
        ctx.on("gpsimd", f_st)
        dma2.n += 16
    barrier(ctx)


def emit_combine_ln(ctx, xin, sT, fT, xout, do_ln):
    t = ctx.t
    dma1, dma2, mm1, ev1, sc0 = (ctx.sem("dma1"), ctx.sem("dma2"), ctx.sem("mm1"),
                                 ctx.sem("ev1"), ctx.sem("sc0"))
    sbB, sbM, sbS = t["sbB"], t["sbM"], t["sbS"]
    xnv = xn_view(ctx)
    prev_tr_mm = mm1.n
    prev_sq = 0
    for i in range(16):
        j = i % 2
        xslot = sbB[:, j * 4096:j * 4096 + 2048]
        sslot = sbB[:, j * 4096 + 2048:j * 4096 + 3072].bitcast(BF16)
        fslot = sbB[:, j * 4096 + 3072:j * 4096 + 4096].bitcast(BF16)
        prev_done = ev1.n

        def f_ld(e, i=i, xslot=xslot, sslot=sslot, fslot=fslot, prev=prev_done):
            if i >= 2:
                e.wait_ge(ev1.h, prev)
            e.dma_start(out=xslot, in_=xin[ts(i, 128), :]).then_inc(dma1.h, 16)
            e.dma_start(out=sslot, in_=sT[:, ts(i, 128)], transpose=True).then_inc(dma1.h, 16)
            e.dma_start(out=fslot, in_=fT[:, ts(i, 128)], transpose=True).then_inc(dma1.h, 16)
        ctx.on("sync", f_ld)
        dma1.n += 48
        ld = dma1.n

        def f_x1(e, xslot=xslot, sslot=sslot, fslot=fslot, ld=ld):
            e.wait_ge(dma1.h, ld)
            sw(e, ev1, e.tensor_tensor(sbM[:, 0:2048], sslot, fslot, ALU.mult))
            ins = e.tensor_tensor(xslot, xslot, sbM[:, 0:2048], ALU.add)
            ev1.after(ins)
        ctx.on("vector", f_x1)
        x1d = ev1.n

        def f_sto(e, i=i, xslot=xslot, v=x1d):
            e.wait_ge(ev1.h, v)
            e.dma_start(out=xout[ts(i, 128), :], in_=xslot).then_inc(dma2.h, 16)
        ctx.on("gpsimd", f_sto)
        dma2.n += 16

        if not do_ln:
            continue

        def f_sq(e, xslot=xslot, v=x1d, psq=prev_sq):
            e.wait_ge(ev1.h, max(v, psq) if psq else v)
            ins = e.activation(sbM[:, 2048:4096], xslot, AF.Square)
            sc0.after(ins)
        ctx.on("scalar", f_sq)
        sq = sc0.n

        def f_stats(e, xslot=xslot, sq=sq):
            sw(e, ev1, e.reduce_sum(sbS[:, 200:201], xslot, axis=AX))
            e.wait_ge(sc0.h, sq)
            sw(e, ev1, e.reduce_sum(sbS[:, 201:202], sbM[:, 2048:4096], axis=AX))
            e.tensor_scalar(sbS[:, 200:201], sbS[:, 200:201], 1.0 / D, None, ALU.mult)
            sw(e, ev1, e.tensor_scalar(sbS[:, 201:202], sbS[:, 201:202], 1.0 / D, None, ALU.mult))
            sw(e, ev1, e.tensor_scalar(sbS[:, 202:203], sbS[:, 200:201], sbS[:, 200:201],
                                       None, ALU.mult))
            ins = e.tensor_tensor(sbS[:, 201:202], sbS[:, 201:202], sbS[:, 202:203],
                                  ALU.subtract)
            ev1.after(ins)
        ctx.on("vector", f_stats)
        vd = ev1.n

        def f_sqrt(e, v=vd):
            e.wait_ge(ev1.h, v)
            ins = e.activation(sbS[:, 202:203], sbS[:, 201:202], AF.Sqrt, bias=c_epsln(ctx))
            sc0.after(ins)
        ctx.on("scalar", f_sqrt)
        sqr = sc0.n

        def f_apply(e, xslot=xslot, v=sqr, ptr=prev_tr_mm):
            e.wait_ge(sc0.h, v)
            e.wait_ge(mm1.h, ptr)
            sw(e, ev1, e.reciprocal(sbS[:, 203:204], sbS[:, 202:203]))
            ins = e.tensor_scalar(sbM[:, 0:1024].bitcast(BF16), xslot,
                                  sbS[:, 200:201], sbS[:, 203:204],
                                  ALU.subtract, ALU.mult)
            ev1.after(ins)
        ctx.on("vector", f_apply)
        xnd = ev1.n

        tq = TQ(ctx, bf16=True)
        for kk in range(16):
            tq.add(sbM[:, 0:1024].bitcast(BF16)[:, ts(kk, 128)], xnv[:, kk, ts(i, 128)])
        tq.emit(wait_sems=[(ev1, xnd)])
        prev_tr_mm = mm1.n
        prev_sq = vd
    barrier(ctx)


def fam1_csT(ctx, i):  return ctx.t["sbM"][:, 128 * i + 0:128 * i + 32]
def fam1_wT(ctx, i):   return ctx.t["sbM"][:, 128 * i + 32:128 * i + 64]
def fam1_dtT(ctx, i):  return ctx.t["sbM"][:, 128 * i + 64:128 * i + 96]
def fam1_ecsT(ctx, i): return ctx.t["sbM"][:, 128 * i + 96:128 * i + 128]


def emit_proj1(ctx):
    t = ctx.t
    dma1, dma2, dma3, mm1, ev1, sc0, gp1 = (ctx.sem("dma1"), ctx.sem("dma2"), ctx.sem("dma3"),
                                            ctx.sem("mm1"), ctx.sem("ev1"), ctx.sem("sc0"),
                                            ctx.sem("gp1"))
    sbM, sbS, sbB, psA = t["sbM"], t["sbS"], t["sbB"], t["psA"]
    ctx._slot_ev = {0: 0, 1: 0, 2: 0}
    ctx._slab_mm = [0, 0]
    ctx._stage_gate = None

    def f_aux(e):
        e.dma_start(out=sbS[:, 300:364].rearrange("p (k j) -> p k j", j=4),
                    in_=t["L1_attn_conv_w"][0:2048].rearrange("(k p) j -> p k j", p=128)) \
            .then_inc(dma1.h, 16)
        e.dma_start(out=sbS[0:64, 364:368].rearrange("p (k j) -> p k j", j=4),
                    in_=t["L1_attn_conv_w"][2048:2112][None if False else slice(None), :]
                    .rearrange("(k p) j -> p k j", p=64)).then_inc(dma1.h, 16)
        e.dma_start(out=sbS[:, 368:384],
                    in_=t["L1_attn_conv_b"][0:2048].rearrange("(k p) -> p k", p=128),
                    allow_slow_non_contiguous=True).then_inc(dma1.h, 16)
        e.dma_start(out=sbS[0:64, 384:385],
                    in_=t["L1_attn_conv_b"][2048:2112][:, None]).then_inc(dma1.h, 16)
        e.dma_start(out=sbS[:, 385:417],
                    in_=t["L1_attn_b_in"][0:4096].rearrange("(k p) -> p k", p=128),
                    allow_slow_non_contiguous=True).then_inc(dma1.h, 16)
        e.dma_start(out=sbS[0:96, 417:418],
                    in_=t["L1_attn_b_in"][4096:4192][:, None]).then_inc(dma1.h, 16)
        e.dma_start(out=sbS[64:96, 428:429], in_=t["L1_attn_dt_bias"][:, None]) \
            .then_inc(dma1.h, 16)
        e.dma_start(out=sbS[64:96, 430:431], in_=t["L1_attn_A_log"][:, None]) \
            .then_inc(dma1.h, 16)
    ctx.on("sync", f_aux)
    dma1.n += 8 * 16
    aux_ld = dma1.n

    def f_expa(e, v=aux_ld):
        e.wait_ge(dma1.h, v)
        ins = e.activation(sbS[64:96, 429:430], sbS[64:96, 430:431], AF.Exp)
        sc0.after(ins)
    ctx.on("scalar", f_expa)

    for m in range(33):
        mrows = 128 if m < 32 else 96
        stag = emit_proj_job(ctx, m, t["L1_attn_W_in"], slice(m * 128, m * 128 + mrows),
                             mrows=128)

        if m < 16:
            def f_g(e, m=m, sd=stag, pst=dma2.n):
                e.wait_ge(ev1.h, sd)
                e.wait_ge(dma2.h, pst)
                ins = e.activation(sbM[:, 4096:5120].bitcast(BF16), sbM[:, 0:2048],
                                   AF.Identity, bias=sbS[:, 385 + m:386 + m])
                sc0.after(ins)
            ctx.on("scalar", f_g)
            gd = sc0.n
            ctx._stage_gate = (sc0, gd)

            def f_st(e, m=m, v=gd):
                e.wait_ge(sc0.h, v)
                e.dma_start(out=t["gate1T"][ts(m, 128), :],
                            in_=sbM[:, 4096:5120].bitcast(BF16)).then_inc(dma2.h, 16)
            ctx.on("gpsimd", f_st)
            dma2.n += 16
        else:
            j = m - 16
            nrows = 128 if j < 16 else 64
            tapc = 300 + 4 * j if j < 16 else 364
            bcol = (368, j) if j < 16 else (384, 0)

            def f_conv(e, m=m, j=j, nrows=nrows, tapc=tapc, bcol=bcol, sd=stag):
                e.wait_ge(ev1.h, sd)
                y = sbM[0:nrows, 0:2048]
                acc = sbM[0:nrows, 2048:4096]
                sw(e, ev1, e.tensor_scalar(y, y, sbS[0:nrows, 385 + m:386 + m], None,
                                           ALU.add))
                tap = lambda jj: sbS[0:nrows, tapc + jj:tapc + jj + 1]
                bc = sbS[0:nrows, bcol[0] + bcol[1]:bcol[0] + bcol[1] + 1]
                sw(e, ev1, e.tensor_scalar(acc, y, tap(3), bc, ALU.mult, ALU.add))
                sw(e, ev1, e.scalar_tensor_tensor(acc[:, 1:2048], y[:, 0:2047], tap(2),
                                                  acc[:, 1:2048], ALU.mult, ALU.add))
                sw(e, ev1, e.scalar_tensor_tensor(acc[:, 2:2048], y[:, 0:2046], tap(1),
                                                  acc[:, 2:2048], ALU.mult, ALU.add))
                ins = e.scalar_tensor_tensor(acc[:, 3:2048], y[:, 0:2045], tap(0),
                                             acc[:, 3:2048], ALU.mult, ALU.add)
                ev1.after(ins)
            ctx.on("vector", f_conv)
            cvd = ev1.n

            def f_silu(e, nrows=nrows, cd=cvd, pst=dma2.n):
                e.wait_ge(ev1.h, cd)
                e.wait_ge(dma2.h, pst)
                ins = e.activation(sbM[0:nrows, 4096:5120].bitcast(BF16),
                                   sbM[0:nrows, 2048:4096], AF.Silu)
                sc0.after(ins)
            ctx.on("scalar", f_silu)
            sld = sc0.n
            ctx._stage_gate = (sc0, sld)

            if j < 16:
                def f_st(e, j=j, v=sld):
                    e.wait_ge(sc0.h, v)
                    e.dma_start(out=t["xsT"][ts(j, 128), :],
                                in_=sbM[:, 4096:5120].bitcast(BF16)).then_inc(dma2.h, 16)
                ctx.on("gpsimd", f_st)
                dma2.n += 16
            else:
                def f_bc(e, v=sld):
                    e.wait_ge(sc0.h, v)
                    sw(e, ev1, e.tensor_copy(sbB[0:32, 2048:4096],
                                             sbM[0:32, 4096:5120].bitcast(BF16)))
                    ins = e.tensor_copy(sbB[0:32, 4096:6144],
                                        sbM[32:64, 4096:5120].bitcast(BF16))
                    ev1.after(ins)
                ctx.on("vector", f_bc)
                ctx._bc_done = ev1.n

                def f_bcs(e, v=ctx._bc_done, base=dma2.n):
                    e.wait_ge(ev1.h, v)
                    e.dma_start(out=t["bmcm"][0:32, :], in_=sbB[0:32, 2048:4096]) \
                        .then_inc(dma2.h, 16)
                    e.dma_start(out=t["bmcm"][32:64, :], in_=sbB[0:32, 4096:6144]) \
                        .then_inc(dma2.h, 16)
                ctx.on("gpsimd", f_bcs)
                dma2.n += 32
                def f_dt(e, sd=stag):
                    Z = sbM[64:96, 0:2048]
                    S = sbM[64:96, 2048:4096]
                    sw(e, ev1, e.tensor_scalar(Z, Z, sbS[64:96, 417:418], None, ALU.add))
                    ev1.after(e.engine_nop())
                ctx.on("vector", f_dt)
                dtz = ev1.n

                def f_dt2(e, v=dtz):
                    e.wait_ge(ev1.h, v)
                    Z = sbM[64:96, 0:2048]
                    S = sbM[64:96, 2048:4096]
                    sw(e, sc0, e.activation(Z, Z, AF.Identity, bias=sbS[64:96, 428:429]))
                    sw(e, sc0, e.activation(S, Z, AF.Abs))
                    sw(e, sc0, e.activation(S, S, AF.Exp, scale=-1.0))
                    ins = e.activation(S, S, AF.Ln, bias=c_onecol(ctx)[64:96, :])
                    sc0.after(ins)
                ctx.on("scalar", f_dt2)
                dts = sc0.n

                def f_dt3(e, v=dts):
                    e.wait_ge(sc0.h, v)
                    Z = sbM[64:96, 0:2048]
                    S = sbM[64:96, 2048:4096]
                    sw(e, ev1, e.tensor_scalar(Z, Z, 0.0, None, ALU.max))
                    sw(e, ev1, e.tensor_tensor(Z, Z, S, ALU.add))
                    ins = e.tensor_scalar(S, Z, sbS[64:96, 429:430], -1.0,
                                          ALU.mult, ALU.mult)
                    ev1.after(ins)
                ctx.on("vector", f_dt3)
                ctx._dt_done = ev1.n
    def f_zz(e):
        ins = e.memset(sbB[64:96, 15360:15488], 0.0)
        gp1.after(ins)
    ctx.on("gpsimd", f_zz)
    zzd = gp1.n

    def f_scan(e, v=ctx._dt_done, zz=zzd):
        e.wait_ge(ev1.h, v)
        e.wait_ge(gp1.h, zz)
        S = sbM[64:96, 2048:4096]
        ins = None
        for n in range(16):
            ins = e.tensor_tensor_scan(S[:, ts(n, 128)], S[:, ts(n, 128)],
                                       sbB[64:96, 15360:15488], 0.0, ALU.add, ALU.add)
        ev1.after(ins)
    ctx.on("vector", f_scan)
    csd = ev1.n

    def f_csout(e, v=csd):
        e.wait_ge(ev1.h, v)
        e.dma_start(out=t["csflat1"][:, :], in_=sbM[64:96, 2048:4096]).then_inc(dma2.h, 16)
    ctx.on("gpsimd", f_csout)
    dma2.n += 16

    def f_w1(e, v=csd):
        e.wait_ge(ev1.h, v)
        S = sbM[64:96, 2048:4096]
        W = sbM[64:96, 4096:6144]
        ins = None
        for n in range(16):
            gcol = S[:, 128 * n + 127:128 * n + 128]
            ins = e.activation(W[:, ts(n, 128)], S[:, ts(n, 128)], AF.Exp,
                               scale=-1.0, bias=gcol)
        sc0.after(ins)
    ctx.on("scalar", f_w1)
    w1s = sc0.n

    def f_w1b(e, v=w1s):
        e.wait_ge(sc0.h, v)
        W = sbM[64:96, 4096:6144]
        ins = e.tensor_tensor(W, W, sbM[64:96, 0:2048], ALU.mult)
        ev1.after(ins)
    ctx.on("vector", f_w1b)
    w1d = ev1.n

    def f_ecs(e, v=csd):
        e.wait_ge(ev1.h, v)
        ins = e.activation(sbB[64:96, 6144:8192], sbM[64:96, 2048:4096], AF.Exp)
        sc0.after(ins)
    ctx.on("scalar", f_ecs)
    ecsd = sc0.n

    def f_eg(e, v=csd):
        e.wait_ge(ev1.h, v)
        S = sbM[64:96, 2048:4096]
        ins = e.activation(sbS[64:96, 440:456], S[:, 127:2048:128], AF.Exp)
        sc0.after(ins)
    ctx.on("scalar", f_eg)
    egd = sc0.n

    def f_flat(e, v=egd, base=dma3.n):
        e.wait_ge(sc0.h, v)
        e.dma_start(out=t["rowbuf"][0:512], in_=sbS[64:96, 440:456]).then_inc(dma3.h, 16)
        e.wait_ge(dma3.h, base + 16)
        e.dma_start(out=sbB[0:1, 8192:8704], in_=t["rowbuf"][None, 0:512]).then_inc(dma3.h, 16)
    ctx.on("gpsimd", f_flat)
    dma3.n += 32
    fld = dma3.n

    def f_bc2(e, v=fld, pe=ev1.n):
        e.wait_ge(dma3.h, v)
        ins = row_bcast(ctx, e, psA[:, 0, 0:512], sbB[0:1, 8192:8704])
        mm1.after(ins)
    ctx.on("tensor", f_bc2)
    bcd = mm1.n

    def f_bce(e, v=bcd):
        e.wait_ge(mm1.h, v)
        ins = e.tensor_copy(sbM[:, 5120:5632], psA[:, 0, 0:512])
        ev1.after(ins)
    ctx.on("vector", f_bce)
    dgd = ev1.n

    def f_place(e, v=max(w1d, dgd, ecsd)):
        e.wait_ge(ev1.h, v)
        e.wait_ge(sc0.h, ecsd)
        sw(e, ev1, e.tensor_copy(sbB[0:32, 8704:10752], sbM[64:96, 2048:4096]))
        sw(e, ev1, e.tensor_copy(sbB[32:64, 8704:10752], sbM[64:96, 4096:6144]))
        sw(e, ev1, e.tensor_copy(sbB[64:96, 8704:10752], sbM[64:96, 0:2048]))
        ins = e.tensor_copy(sbB[0:32, 10752:12800], sbB[64:96, 6144:8192])
        ev1.after(ins)
    ctx.on("vector", f_place)
    pld = ev1.n

    tq = TQ(ctx, bf16=False)
    for i in range(16):
        tq.add_multi(sbB[:, 8704 + 128 * i:8832 + 128 * i], [
            (slice(0, 32), fam1_csT(ctx, i)),
            (slice(32, 64), fam1_wT(ctx, i)),
            (slice(64, 96), fam1_dtT(ctx, i)),
        ])
        tq.add_multi(sbB[:, 10752 + 128 * i:10880 + 128 * i], [
            (slice(0, 32), fam1_ecsT(ctx, i)),
        ])
    tq.emit(wait_sems=[(ev1, pld)])
    barrier(ctx)


def emit_scan1(ctx):
    t = ctx.t
    dma1, dma2, mm1, ev1, sc0, gp1 = (ctx.sem("dma1"), ctx.sem("dma2"), ctx.sem("mm1"),
                                      ctx.sem("ev1"), ctx.sem("sc0"), ctx.sem("gp1"))
    sbB, sbM, sbS, psA = t["sbB"], t["sbM"], t["sbS"], t["psA"]
    pool = PsumPool(ctx)
    hstate = lambda h: sbB[0:32, 64 * h:64 * (h + 1)]
    Bm = sbB[0:32, 2048:4096]
    Cm = sbB[0:32, 4096:6144]
    xs_tm = sbB[:, 12800:13824].bitcast(BF16)
    g_tm = sbB[:, 13824:14848].bitcast(BF16)
    csrow = sbB[0:1, 6144:10240]
    BmT = sbB[:, 14848:14880]
    Bw = sbB[:, 14880:14896].bitcast(BF16)
    basev = sbB[:, 14896:14960].bitcast(BF16)
    yslab = sbM[:, 2048:3072].bitcast(BF16)
    dmat = sbM[:, 3584:3712]
    sct1 = sbM[:, 3712:3776].bitcast(BF16)
    e1b = sbM[:, 3776:3840]
    e2b = sbM[:, 3904:3968]
    ysq = sbM[:, 3840:4352]
    ynstage = sbM[:, 4352:4864].bitcast(BF16)
    ynstage = sbM[:, 4352:5376].bitcast(BF16)
    rst = sbS[:, 216:217]

    def f_init(e):
        ins = e.memset(sbB[0:32, 0:2048], 0.0)
        gp1.after(ins)
    ctx.on("gpsimd", f_init)

    def f_bcld(e, base=dma1.n):
        e.dma_start(out=sbB[0:32, 2048:4096], in_=t["bmcm"][0:32, :]).then_inc(dma1.h, 16)
        e.dma_start(out=sbB[0:32, 4096:6144], in_=t["bmcm"][32:64, :]).then_inc(dma1.h, 16)
    ctx.on("sync", f_bcld)
    dma1.n += 32
    barrier(ctx)

    for n in range(16):
        pe_, pm_ = ev1.n, mm1.n

        def f_ld(e, n=n, pe=pe_, pm=pm_):
            e.wait_ge(ev1.h, pe)
            e.wait_ge(mm1.h, pm)
            e.dma_start(out=xs_tm, in_=t["xsT"][:, ts(n, 128)], transpose=True) \
                .then_inc(dma1.h, 16)
            e.dma_start(out=g_tm, in_=t["gate1T"][:, ts(n, 128)], transpose=True) \
                .then_inc(dma1.h, 16)
            e.dma_start(out=csrow.rearrange("a (h c) -> a h c", h=32),
                        in_=t["csflat1"][:, ts(n, 128)][None, :, :]).then_inc(dma1.h, 16)
        ctx.on("sync", f_ld)
        dma1.n += 48
        ld = dma1.n

        iT, apT, wT = pool.acquire(128)

        def f_bmt(e, n=n, pe=pe_, ap=apT, waits=wT):
            e.wait_ge(ev1.h, pe)
            for s_, v_ in waits:
                e.wait_ge(s_.h, v_)
            ins = e.transpose(ap[0:128, 0:32], Bm[:, ts(n, 128)], c_idf32(ctx)[0:32, 0:32])
            mm1.after(ins)
        ctx.on("tensor", f_bmt)
        bmtm = mm1.n

        def f_bmte(e, v=bmtm, ap=apT):
            e.wait_ge(mm1.h, v)
            ins = e.tensor_copy(BmT[0:128, 0:32], ap[0:128, 0:32])
            ev1.after(ins)
        ctx.on("vector", f_bmte)
        bmtd = ev1.n
        pool.release(iT, [(ev1, bmtd)])

        iP, apP, wP = pool.acquire(128)
        def f_base(e, n=n, ap=apP, waits=wP, ld=ld):
            e.wait_ge(dma1.h, ld)
            for s, v in waits:
                e.wait_ge(s.h, v)
            ins = e.matmul(ap[0:128, :], Bm[:, ts(n, 128)], Cm[:, ts(n, 128)],
                           start=True, stop=True)
            mm1.after(ins)
        ctx.on("tensor", f_base)
        bmm = mm1.n

        def f_basev(e, v=bmm, ap=apP):
            e.wait_ge(mm1.h, v)
            ins = e.tensor_copy(basev, ap[0:128, :])
            ev1.after(ins)
        ctx.on("vector", f_basev)
        bvd = ev1.n
        pool.release(iP, [(ev1, bvd)])

        prev_ymm = 0
        for h in range(H1):
            def f_exr(e, h=h, n=n, ld=ld, pm=prev_ymm):
                if pm:
                    e.wait_ge(mm1.h, pm)
                e.wait_ge(dma1.h, ld)
                ins = e.tensor_copy(sbB[0:1, 10240:10368], csrow[:, 128 * h:128 * h + 128])
                ev1.after(ins)
            ctx.on("vector", f_exr)
            exd = ev1.n

            iA, apA, wA = pool.acquire(128)
            def f_csb(e, v=exd, ap=apA, waits=wA):
                e.wait_ge(ev1.h, v)
                for s, vv in waits:
                    e.wait_ge(s.h, vv)
                ins = row_bcast(ctx, e, ap[0:128, :], sbB[0:1, 10240:10368])
                mm1.after(ins)
            ctx.on("tensor", f_csb)
            csbm = mm1.n

            def f_dm(e, h=h, n=n, v=csbm, ap=apA, bv=bvd):
                e.wait_ge(mm1.h, v)
                e.wait_ge(ev1.h, bv)
                sw(e, ev1, e.tensor_scalar(dmat, ap[0:128, :],
                                           fam1_csT(ctx, n)[:, h:h + 1], None, ALU.subtract))
                ins = e.tensor_tensor(dmat, dmat, c_mtri128(ctx), ALU.add)
                ev1.after(ins)
            ctx.on("vector", f_dm)
            dmd = ev1.n
            pool.release(iA, [(ev1, dmd)])

            def f_de(e, v=dmd):
                e.wait_ge(ev1.h, v)
                ins = e.activation(dmat, dmat, AF.Exp)
                sc0.after(ins)
            ctx.on("scalar", f_de)
            ded = sc0.n

            def f_sct(e, h=h, n=n, v=ded):
                e.wait_ge(sc0.h, v)
                sw(e, ev1, e.scalar_tensor_tensor(sct1, dmat, fam1_dtT(ctx, n)[:, h:h + 1],
                                                  basev, ALU.mult, ALU.mult))
                ins = e.tensor_scalar(Bw, BmT[:, 0:32], fam1_wT(ctx, n)[:, h:h + 1],
                                      None, ALU.mult)
                ev1.after(ins)
            ctx.on("vector", f_sct)
            sctd = ev1.n

            iB, apB, wB = pool.acquire(192)
            def f_ymm(e, h=h, n=n, v=sctd, ap=apB, waits=wB):
                e.wait_ge(ev1.h, v)
                for s, vv in waits:
                    e.wait_ge(s.h, vv)
                e.matmul(ap[0:128, 0:64], sct1, xs_tm[:, ts(h, 64)], start=True, stop=True)
                e.matmul(ap[0:128, 64:128], Cm[:, ts(n, 128)], hstate(h),
                         start=True, stop=True)
                ins = e.matmul(ap[0:32, 128:192], Bw, xs_tm[:, ts(h, 64)],
                               start=True, stop=True)
                mm1.after(ins)
            ctx.on("tensor", f_ymm)
            ymm = mm1.n
            prev_ymm = ymm

            def f_yeps(e, h=h, n=n, v=ymm, ap=apB):
                e.wait_ge(mm1.h, v)
                sw(e, ev1, e.tensor_scalar(e1b[:, 0:64], ap[0:128, 64:128],
                                           fam1_ecsT(ctx, n)[:, h:h + 1], None, ALU.mult))
                sw(e, ev1, e.scalar_tensor_tensor(e2b[:, 0:64], xs_tm[:, ts(h, 64)],
                                                  c_dbrow(ctx)[:, h:h + 1], ap[0:128, 0:64],
                                                  ALU.mult, ALU.add))
                sw(e, ev1, e.tensor_tensor(yslab[:, ts(h, 64)], e1b[:, 0:64], e2b[:, 0:64],
                                           ALU.add))
                ins = e.scalar_tensor_tensor(hstate(h), hstate(h),
                                             dgB(ctx)[0:32, 16 * h + n:16 * h + n + 1],
                                             ap[0:32, 128:192], ALU.mult, ALU.add)
                ev1.after(ins)
            ctx.on("vector", f_yeps)
            pool.release(iB, [(ev1, ev1.n)])
            if getattr(ctx, "_dbg_scan", None) == n and h == 0:
                def f_dh(e, v=ev1.n, base=ctx.sem("dma2").n):
                    e.wait_ge(ev1.h, v)
                    e.dma_start(out=ctx.t["d_dmat"], in_=dmat).then_inc(ctx.sem("dma2").h, 16)
                    e.dma_start(out=ctx.t["d_sct"], in_=sct1).then_inc(ctx.sem("dma2").h, 16)
                    e.dma_start(out=ctx.t["d_base"], in_=basev).then_inc(ctx.sem("dma2").h, 16)
                    e.dma_start(out=ctx.t["d_e1"], in_=e1b[:, 0:64]).then_inc(ctx.sem("dma2").h, 16)
                    e.dma_start(out=ctx.t["d_e2"], in_=e2b[:, 0:64]).then_inc(ctx.sem("dma2").h, 16)
                    e.wait_ge(ctx.sem("dma2").h, base + 80)
                ctx.on("gpsimd", f_dh)
                ctx.sem("dma2").n += 80
        yall = ev1.n

        if getattr(ctx, "_dbg_scan", None) == n:
            def f_dsc(e, v=yall, base=ctx.sem("dma2").n):
                e.wait_ge(ev1.h, v)
                e.dma_start(out=ctx.t["d_yslab"], in_=yslab).then_inc(ctx.sem("dma2").h, 16)
                e.dma_start(out=ctx.t["d_hst"], in_=sbB[0:32, 0:2048]).then_inc(ctx.sem("dma2").h, 16)
                e.wait_ge(ctx.sem("dma2").h, base + 32)
            ctx.on("gpsimd", f_dsc)
            ctx.sem("dma2").n += 32

        def f_sg(e, v=yall, pst=getattr(ctx, "_yn_used", 0)):
            e.wait_ge(ev1.h, v)
            if pst:
                e.wait_ge(mm1.h, pst)
            ins = e.activation(ynstage, g_tm, AF.Silu)
            sc0.after(ins)
        ctx.on("scalar", f_sg)
        sgd = sc0.n

        def f_gate(e, v=sgd, yv=yall):
            e.wait_ge(sc0.h, v)
            sw(e, ev1, e.tensor_tensor(yslab, yslab, ynstage, ALU.mult))
            ev1.after(e.engine_nop())
        ctx.on("vector", f_gate)
        gtd = ev1.n

        def f_ysq(e, v=gtd):
            e.wait_ge(ev1.h, v)
            ins = e.activation(ysq, yslab[:, 0:1024] if False else yslab, AF.Square,
                               accum_out=None) if False else None
            ins = e.activation(t["sbB"][:, 10368:12416], yslab, AF.Square)
            sc0.after(ins)
        ctx.on("scalar", f_ysq)
        ysd = sc0.n

        def f_rms(e, v=ysd):
            e.wait_ge(sc0.h, v)
            sw(e, ev1, e.reduce_sum(rst, t["sbB"][:, 10368:12416], axis=AX))
            ins = e.tensor_scalar(rst, rst, 1.0 / D, None, ALU.mult)
            ev1.after(ins)
        ctx.on("vector", f_rms)
        rmd = ev1.n

        def f_rsq(e, v=rmd):
            e.wait_ge(ev1.h, v)
            ins = e.activation(rst, rst, AF.Sqrt, bias=c_eps0(ctx))
            sc0.after(ins)
        ctx.on("scalar", f_rsq)
        rqd = sc0.n

        def f_yn(e, v=rqd):
            e.wait_ge(sc0.h, v)
            sw(e, ev1, e.reciprocal(rst, rst))
            sw(e, ev1, e.tensor_scalar(yslab, yslab, rst, None, ALU.mult))
            ins = e.tensor_tensor(ynstage, yslab, c_normw1B(ctx), ALU.mult)
            ev1.after(ins)
        ctx.on("vector", f_yn)
        ynd = ev1.n

        tq = TQ(ctx, bf16=True)
        ynv = xn_view(ctx)
        for kk in range(16):
            tq.add(ynstage[:, ts(kk, 128)], ynv[:, kk, ts(n, 128)])
        tq.emit(wait_sems=[(ev1, ynd)])
        ctx._yn_used = mm1.n
    barrier(ctx)
    ctx._yn_used = 0


def emit_ffn1_gateup(ctx):
    t = ctx.t
    dma1, dma2, mm1, ev1, sc0 = (ctx.sem("dma1"), ctx.sem("dma2"), ctx.sem("mm1"),
                                 ctx.sem("ev1"), ctx.sem("sc0"))
    sbB, sbM, psA = t["sbB"], t["sbM"], t["psA"]
    slab = lambda mat, buf: sbB[:, (mat * 2 + buf) * 4096:(mat * 2 + buf + 1) * 4096] \
        .bitcast(BF16).rearrange("p (k m) -> p k m", k=16)
    def f_sc(e):
        e.dma_start(out=t["sbS"][:, 300:364],
                    in_=t["L1_ffn_s_gate"].rearrange("(m p) -> p m", p=128),
                    allow_slow_non_contiguous=True).then_inc(dma1.h, 16)
        e.dma_start(out=t["sbS"][:, 364:428],
                    in_=t["L1_ffn_s_up"].rearrange("(m p) -> p m", p=128),
                    allow_slow_non_contiguous=True).then_inc(dma1.h, 16)
    ctx.on("sync", f_sc)
    dma1.n += 32
    scl = dma1.n

    NMACRO = MLP1 // 512
    prev_use = {0: 0, 1: 0}
    ctx._ffn_evfree = 0
    ctx._ffn_gelu = 0
    ctx._ffn_comb = 0
    for mac in range(NMACRO):
        buf = mac % 2
        need = prev_use[buf]

        def f_ld(e, mac=mac, buf=buf, need=need):
            e.wait_ge(mm1.h, need)
            for mat, wd in ((0, t["L1_ffn_w_gate"]), (1, t["L1_ffn_w_up"])):
                e.dma_start(out=slab(mat, buf),
                            in_=wd[:, ts(mac, 512)].rearrange("(k p) m -> p k m", p=128)) \
                    .then_inc(dma1.h, 16)
        ctx.on("gpsimd", f_ld)
        dma1.n += 32
        ld = dma1.n

        for nt in range(4):
            for mat in range(2):
                gate_gate = ctx._ffn_gelu if mat == 1 else 0

                def f_mm(e, mac=mac, buf=buf, nt=nt, mat=mat, ld=ld, gg=gate_gate):
                    e.wait_ge(dma1.h, ld)
                    if mat == 0:
                        e.wait_ge(ev1.h, ctx._ffn_evfree)
                    else:
                        e.wait_ge(sc0.h, gg)
                    ins = None
                    for k in range(16):
                        for tb in range(4):
                            ins = e.matmul(psA[:, tb, :],
                                           slab(mat, buf)[:, k, nt * 128:nt * 128 + 128],
                                           xn_view(ctx)[:, k, ts(tb, 512)],
                                           start=(k == 0), stop=(k == 15))
                    mm1.after(ins)
                ctx.on("tensor", f_mm)
                mmn = mm1.n

                if mat == 0:
                    def f_silu(e, mac=mac, nt=nt, v=mmn, pc=ctx._ffn_comb, scl=scl):
                        e.wait_ge(mm1.h, v)
                        e.wait_ge(ev1.h, pc)
                        e.wait_ge(dma1.h, scl)
                        ntile = mac * 4 + nt
                        scol = t["sbS"][:, 300 + ntile // 2:301 + ntile // 2]
                        scol = t["sbS"][:, 300 + ntile:301 + ntile]
                        ins = None
                        for tb in range(4):
                            ins = e.activation(sbM[:, 2048 + tb * 256:2048 + tb * 256 + 256]
                                               .bitcast(BF16), psA[:, tb, :], AF.Silu,
                                               scale=scol)
                        sc0.after(ins)
                    ctx.on("scalar", f_silu)
                    ctx._ffn_gelu = sc0.n
                else:
                    def f_comb(e, mac=mac, nt=nt, v=mmn, gv=ctx._ffn_gelu, pst=dma2.n, scl=scl):
                        e.wait_ge(mm1.h, v)
                        e.wait_ge(sc0.h, gv)
                        e.wait_ge(dma2.h, pst)
                        ntile = mac * 4 + nt
                        sucol = t["sbS"][:, 364 + ntile:365 + ntile]
                        ins = None
                        for tb in range(4):
                            ins = e.scalar_tensor_tensor(
                                sbM[:, 3072 + tb * 256:3072 + tb * 256 + 256].bitcast(BF16),
                                sbM[:, 2048 + tb * 256:2048 + tb * 256 + 256].bitcast(BF16),
                                sucol, psA[:, tb, :], ALU.mult, ALU.mult)
                        ev1.after(ins)
                    ctx.on("vector", f_comb)
                    ctx._ffn_evfree = ev1.n
                    ctx._ffn_comb = ev1.n
                    cmb = ev1.n

                    def f_st(e, mac=mac, nt=nt, v=cmb):
                        e.wait_ge(ev1.h, v)
                        e.dma_start(out=t["inter1"][ts(mac * 4 + nt, 128), :],
                                    in_=sbM[:, 3072:4096].bitcast(BF16)).then_inc(dma2.h, 16)
                    ctx.on("gpsimd", f_st)
                    dma2.n += 16
        prev_use[buf] = mm1.n
    barrier(ctx)


def sdown1_col(ctx, h2t):
    return ctx.t["sbS"][:, 428 + h2t:429 + h2t]


def emit_sdown1_load(ctx):
    t = ctx.t
    dma1 = ctx.sem("dma1")

    def f(e):
        e.dma_start(out=t["sbS"][:, 428:444],
                    in_=t["L1_ffn_s_down"].rearrange("(m p) -> p m", p=128),
                    allow_slow_non_contiguous=True).then_inc(dma1.h, 16)
    ctx.on("sync", f)
    dma1.n += 16
    barrier(ctx)


def emit_dump_sbA(ctx, dst):
    dma2 = ctx.sem("dma2")
    xnv = xn_view(ctx)

    def f(e, base=dma2.n):
        for kk in range(16):
            e.dma_start(out=dst[ts(kk, 128), :], in_=xnv[:, kk, :]).then_inc(dma2.h, 16)
        e.wait_ge(dma2.h, base + 256)
    ctx.on("gpsimd", f)
    dma2.n += 256
    barrier(ctx)


def emit_load_sbA(ctx, src):
    dma1 = ctx.sem("dma1")
    xnv = xn_view(ctx)

    def f(e, base=dma1.n):
        for kk in range(16):
            e.dma_start(out=xnv[:, kk, :], in_=src[ts(kk, 128), :]).then_inc(dma1.h, 16)
        e.wait_ge(dma1.h, base + 256)
    ctx.on("sync", f)
    dma1.n += 256
    barrier(ctx)


def build_progA():
    nc = bass.Bass("TRN2", target_bir_lowering=False)
    stack = ExitStack()
    over = {"Wg_dq": "ExternalOutput", "Wu_dq": "ExternalOutput", "Wd_dq": "ExternalOutput"}
    ctx = build_ctx(nc, stack, over)
    with stack:
        with nc.Block() as blk:
            ctx.blk = blk
            emit_consts(ctx)
            emit_dequant(ctx)
            dma2 = ctx.sem("dma2")

            def f_fin(e, v=dma2.n):
                e.wait_ge(dma2.h, v)
            ctx.on("gpsimd", f_fin)
    return nc


def build_progB():
    nc = bass.Bass("TRN2", target_bir_lowering=False)
    stack = ExitStack()
    over = {"Wg_dq": "ExternalInput", "Wu_dq": "ExternalInput", "Wd_dq": "ExternalInput",
            "x1b": "ExternalOutput", "xn1d": "ExternalOutput"}
    ctx = build_ctx(nc, stack, over)
    ctx._cheb_serial = True
    t = ctx.t
    with stack:
        with nc.Block() as blk:
            ctx.blk = blk
            emit_consts(ctx)
            emit_ln_token(ctx, t["L0_attn_hidden_states"][0])
            emit_proj0(ctx)
            emit_afam_post(ctx)
            emit_ffn0_gateup(ctx)
            emit_ffn_down(ctx, t["inter0"], t["Wd_dq"], t["f0T"], MLP0 // 128)
            emit_chunk_solve(ctx)
            emit_dense_proj(ctx, t["L0_attn_Wo"], t["s0T"])
            emit_combine_ln(ctx, t["L0_attn_hidden_states"][0], t["s0T"], t["f0T"],
                            t["x1b"], do_ln=True)
            emit_dump_sbA(ctx, t["xn1d"])
            dma2 = ctx.sem("dma2")

            def f_fin(e, v=dma2.n):
                e.wait_ge(dma2.h, v)
            ctx.on("gpsimd", f_fin)
    return nc


def build_progC():
    nc = bass.Bass("TRN2", target_bir_lowering=False)
    stack = ExitStack()
    over = {"x1b": "ExternalInput", "xn1d": "ExternalInput", "yb": "ExternalOutput"}
    ctx = build_ctx(nc, stack, over)
    t = ctx.t
    with stack:
        with nc.Block() as blk:
            ctx.blk = blk
            emit_consts(ctx)
            emit_load_sbA(ctx, t["xn1d"])
            emit_proj1(ctx)
            emit_ffn1_gateup(ctx)
            emit_scan1(ctx)
            emit_dense_proj(ctx, t["L1_attn_W_out"], t["s1T"])
            emit_sdown1_load(ctx)
            emit_ffn_down(ctx, t["inter1"], t["L1_ffn_w_down"], t["f1T"],
                          MLP1 // 128, int8=True, sdown_col=sdown1_col)
            emit_combine_ln(ctx, t["x1b"], t["s1T"], t["f1T"], t["yb"], do_ln=False)
            dma2 = ctx.sem("dma2")

            def f_fin(e, v=dma2.n):
                e.wait_ge(dma2.h, v)
            ctx.on("gpsimd", f_fin)
    return nc


_NC_CACHE = {}


def kernel(**inputs):
    n_cores = 8
    if "A" not in _NC_CACHE:
        _NC_CACHE["A"] = build_progA()
        _NC_CACHE["B"] = build_progB()
        _NC_CACHE["C"] = build_progC()
    x = np.ascontiguousarray(inputs["L0_attn_hidden_states"])
    base = dict(inputs)
    del base["L0_attn_hidden_states"]
    dummy_x = x[0:NB]

    mA = dict(base)
    mA["L0_attn_hidden_states"] = dummy_x
    resA = run_bass_kernel_spmd(_NC_CACHE["A"], [mA for _ in range(n_cores)],
                                core_ids=list(range(n_cores)))
    dq = {k: np.asarray(resA.results[0][k]) for k in ("Wg_dq", "Wu_dq", "Wd_dq")}

    outs = []
    for b0 in range(NB):
        mB = [dict(base, L0_attn_hidden_states=x[2 * c + b0:2 * c + b0 + 1].repeat(NB, 0)
                   if False else np.broadcast_to(x[2 * c + b0][None], (NB, L, D)).copy(),
                   **dq) for c in range(n_cores)]
        resB = run_bass_kernel_spmd(_NC_CACHE["B"], mB, core_ids=list(range(n_cores)))
        mC = [dict(base,
                   L0_attn_hidden_states=np.broadcast_to(x[2 * c + b0][None], (NB, L, D)).copy(),
                   x1b=np.asarray(resB.results[c]["x1b"]),
                   xn1d=np.asarray(resB.results[c]["xn1d"])) for c in range(n_cores)]
        resC = run_bass_kernel_spmd(_NC_CACHE["C"], mC, core_ids=list(range(n_cores)))
        outs.append(np.stack([np.asarray(resC.results[c]["yb"]) for c in range(n_cores)]))
    y = np.empty((2 * n_cores, L, D), np.float32)
    for b0 in range(NB):
        for c in range(n_cores):
            y[2 * c + b0] = outs[b0][c]
    return y


# version 08 (final)
# speedup vs baseline: 1.0015x; 1.0015x over previous
import math
from contextlib import ExitStack

import numpy as np
import concourse.bass as bass
import concourse.mybir as mybir
from concourse.bass_utils import run_bass_kernel_spmd

F32 = mybir.dt.float32
BF16 = mybir.dt.bfloat16
I32 = mybir.dt.int32
I8 = mybir.dt.int8
AF = mybir.ActivationFunctionType
ALU = mybir.AluOpType
AX = mybir.AxisListType.X

D = 2048; L = 2048; NB = 2
HD = 128; HQ = 16; HK = 8; CCH = 64; NCH = L // CCH; NPAIR = L // 128
NUM_ITER = 30; RIDGE = 0.02; EPS0 = 1e-6
H1 = 32; P1 = 64; N1 = 32; CONV_DIM = D + 2 * N1; PROJ1 = D + CONV_DIM + H1
MLP0 = 16384; MLP1 = 8192
LNEPS = 1e-5
LNQS = -0.5 * math.log(128.0)


def ts(i, n):
    return slice(i * n, (i + 1) * n)


class Sem:
    def __init__(self, h):
        self.h = h
        self.n = 0

    def after(self, ins, d=1):
        ins.then_inc(self.h, d)
        self.n += d
        return self.n


def sw(e, sem, ins):
    sem.after(ins)
    e.wait_ge(sem.h, sem.n)


class Ctx:
    def __init__(self, nc):
        self.nc = nc
        self.blk = None
        self.t = {}
        self.sems = {}

    def on(self, eng, fn):
        getattr(self.blk, eng)(fn)

    def sem(self, name):
        return self.sems[name]


def declare_dram(nc, over=None):
    over = over or {}
    t = {}
    def ei(name, shape, dt):
        t[name] = nc.dram_tensor(name, shape, dt, kind=over.get(name, "ExternalInput")).ap()
    ei("L0_attn_hidden_states", [NB, L, D], F32)
    ei("L0_attn_Wq", [D, D], F32)
    ei("L0_attn_Wk", [D, HK * HD], F32)
    ei("L0_attn_Wv", [D, HK * HD], F32)
    ei("L0_attn_conv_q", [D, 4], F32)
    ei("L0_attn_conv_k", [HK * HD, 4], F32)
    ei("L0_attn_conv_v", [HK * HD, 4], F32)
    ei("L0_attn_Wa", [D, HQ], F32)
    ei("L0_attn_A_log", [HQ], F32)
    ei("L0_attn_dt_bias", [HQ], F32)
    ei("L0_attn_Wb", [D, HQ], F32)
    ei("L0_attn_bb", [HQ], F32)
    ei("L0_attn_Walpha", [D, HQ], F32)
    ei("L0_attn_balpha", [HQ], F32)
    ei("L0_attn_Wg", [D, D], F32)
    ei("L0_attn_norm_w", [HD], F32)
    ei("L0_attn_Wo", [D, D], F32)
    ei("L0_ffn_gate_W_packed", [MLP0, D // 2], I32)
    ei("L0_ffn_gate_scales", [MLP0, D // 32], F32)
    ei("L0_ffn_up_W_packed", [MLP0, D // 2], I32)
    ei("L0_ffn_up_scales", [MLP0, D // 32], F32)
    ei("L0_ffn_down_W_packed", [D, MLP0 // 2], I32)
    ei("L0_ffn_down_scales", [D, MLP0 // 32], F32)
    ei("L1_attn_W_in", [D, PROJ1], F32)
    ei("L1_attn_b_in", [PROJ1], F32)
    ei("L1_attn_conv_w", [CONV_DIM, 4], F32)
    ei("L1_attn_conv_b", [CONV_DIM], F32)
    ei("L1_attn_A_log", [H1], F32)
    ei("L1_attn_dt_bias", [H1], F32)
    ei("L1_attn_D_param", [H1], F32)
    ei("L1_attn_norm_w", [D], F32)
    ei("L1_attn_W_out", [D, D], F32)
    ei("L1_ffn_w_gate", [D, MLP1], I8)
    ei("L1_ffn_w_up", [D, MLP1], I8)
    ei("L1_ffn_w_down", [MLP1, D], I8)
    ei("L1_ffn_s_gate", [MLP1], F32)
    ei("L1_ffn_s_up", [MLP1], F32)
    ei("L1_ffn_s_down", [D], F32)
    t["y"] = nc.dram_tensor("y", [NB, L, D], F32, kind="ExternalOutput").ap()

    def sc(name, shape, dt):
        t[name] = nc.dram_tensor(name, shape, dt, kind=over.get(name, "Internal")).ap()
    sc("xn1d", [D, L], BF16)
    sc("bmcm", [64, L], F32)
    sc("x1b", [L, D], F32)
    sc("yb", [L, D], F32)
    sc("x1", [NB, L, D], F32)
    sc("qT", [D, L], BF16)
    sc("kT", [HK * HD, L], BF16)
    sc("vT", [HK * HD, L], BF16)
    sc("gT", [D, L], BF16)
    sc("s0T", [D, L], BF16)
    sc("f0T", [D, L], BF16)
    sc("inter0", [MLP0, L], BF16)
    sc("Wg_dq", [MLP0, D], BF16)
    sc("Wu_dq", [MLP0, D], BF16)
    sc("Wd_dq", [D, MLP0], BF16)
    sc("gate1T", [D, L], BF16)
    sc("xsT", [D, L], BF16)
    sc("inter1", [MLP1, L], BF16)
    sc("s1T", [D, L], BF16)
    sc("f1T", [D, L], BF16)
    sc("rowbuf", [65536], F32)
    sc("csflat", [HQ, L], F32)
    sc("csflat1", [H1, L], F32)
    return t


SB_SPECS = [
    ("sbA", [128, 16, 1024], F32),
    ("sbB", [128, 16384], F32),
    ("sbW", [128, 4096], BF16),
    ("sbM", [128, 6144], F32),
    ("sbC", [128, 2048], F32),
    ("sbS", [128, 512], F32),
]
PS_SPECS = [
    ("psA", [128, 6, 512], F32),
    ("psTr", [128, 2, 1024], BF16),
]
SEM_NAMES = ["dma1", "dma2", "dma3", "mm1", "ev1", "sc0", "gp1"]


def build_ctx(nc, stack, over=None):
    ctx = Ctx(nc)
    ctx.t.update(declare_dram(nc, over))
    for name, shape, dt in SB_SPECS:
        ctx.t[name] = stack.enter_context(nc.sbuf_tensor(name, shape, dt)).ap()
    for name, shape, dt in PS_SPECS:
        ctx.t[name] = stack.enter_context(nc.psum_tensor(name, shape, dt)).ap()
    for name in SEM_NAMES:
        ctx.sems[name] = Sem(stack.enter_context(nc.semaphore(name)))
    return ctx


def psT(ctx, i):
    return ctx.t["psTr"][:, i, :]


def barrier(ctx):
    waits = [(s, s.n) for s in ctx.sems.values() if s.n > 0]
    for engname in ("sync", "tensor", "vector", "scalar", "gpsimd"):
        def _f(e, waits=waits):
            for s, v in waits:
                e.wait_ge(s.h, v)
        ctx.on(engname, _f)


def c_idf32(c): return c.t["sbC"][:, 0:128]
def c_mtri64(c): return c.t["sbC"][:, 128:192]
def c_mtri128(c): return c.t["sbC"][:, 192:320]
def c_negridgeI(c): return c.t["sbC"][:, 320:448]
def c_normwB(c): return c.t["sbC"][:, 448:576]
def c_epsln(c): return c.t["sbC"][:, 576:577]
def c_eps0(c): return c.t["sbC"][:, 577:578]
def c_onecol(c): return c.t["sbC"][:, 578:579]
def c_lnqs(c): return c.t["sbC"][:, 579:580]
def c_idbf16(c): return c.t["sbC"][:, 600:664].bitcast(BF16)
def c_dbrow(c): return c.t["sbC"][:, 728:760]
def c_normw1B(c): return c.t["sbC"][:, 768:1792].bitcast(BF16)


def emit_consts(ctx):
    t = ctx.t
    dma1, gp1, mm1, ev1 = ctx.sem("dma1"), ctx.sem("gp1"), ctx.sem("mm1"), ctx.sem("ev1")
    sbS, sbM, psA, c = t["sbS"], t["sbM"], t["psA"], t["sbC"]

    def f_sync(e):
        e.dma_start(out=sbS[0:1, 0:128], in_=t["L0_attn_norm_w"][None, :]).then_inc(dma1.h, 16)
        e.dma_start(out=sbS[0:1, 128:160], in_=t["L1_attn_D_param"][None, :]).then_inc(dma1.h, 16)
        e.dma_start(out=sbM[0:1, 0:D], in_=t["L1_attn_norm_w"][None, :]).then_inc(dma1.h, 16)
    ctx.on("sync", f_sync)
    dma1.n += 48
    ld = dma1.n

    def f_gp(e):
        e.memset(c[:, 0:128], 1.0)
        e.affine_select(c_idf32(ctx), c_idf32(ctx), [[1, 128]], ALU.is_equal, 0.0,
                        base=0, channel_multiplier=-1)
        e.memset(c[:, 128:192], 0.0)
        e.affine_select(c_mtri64(ctx), c_mtri64(ctx), [[1, 64]], ALU.is_ge, -1e30,
                        base=0, channel_multiplier=-1)
        e.memset(c[:, 192:320], 0.0)
        e.affine_select(c_mtri128(ctx), c_mtri128(ctx), [[1, 128]], ALU.is_ge, -1e30,
                        base=0, channel_multiplier=-1)
        e.memset(c[:, 320:448], -RIDGE)
        e.affine_select(c_negridgeI(ctx), c_negridgeI(ctx), [[1, 128]], ALU.is_equal, 0.0,
                        base=0, channel_multiplier=-1)
        e.memset(c_epsln(ctx), LNEPS)
        e.memset(c_eps0(ctx), EPS0)
        e.memset(c_onecol(ctx), 1.0)
        e.memset(c_lnqs(ctx), LNQS)
        e.memset(c_idbf16(ctx), 1.0)
        ins = e.affine_select(c_idbf16(ctx), c_idbf16(ctx), [[1, 128]], ALU.is_equal, 0.0,
                              base=0, channel_multiplier=-1)
        gp1.after(ins)
    ctx.on("gpsimd", f_gp)

    def f_vec0(e):
        ins = e.memset(c[0:1, 1792:1920], 1.0)
        ev1.after(ins)
    ctx.on("vector", f_vec0)
    od = ev1.n

    def f_mm(e, ld=ld, od=od):
        e.wait_ge(dma1.h, ld)
        e.wait_ge(ev1.h, od)
        e.matmul(psA[:, 0, 0:128], c[0:1, 1792:1920], sbS[0:1, 0:128], start=True, stop=True)
        e.matmul(psA[:, 1, 0:32], c[0:1, 1792:1920], sbS[0:1, 128:160], start=True, stop=True)
        for j in range(4):
            ins = e.matmul(psA[:, 2 + j, :], c[0:1, 1792:1920], sbM[0:1, ts(j, 512)],
                           start=True, stop=True)
        mm1.after(ins)
    ctx.on("tensor", f_mm)
    mmn = mm1.n

    def f_vec(e, mmn=mmn):
        e.wait_ge(mm1.h, mmn)
        e.tensor_copy(c_normwB(ctx), psA[:, 0, 0:128])
        e.tensor_copy(c_dbrow(ctx), psA[:, 1, 0:32])
        for j in range(4):
            ins = e.tensor_copy(c_normw1B(ctx)[:, ts(j, 512)], psA[:, 2 + j, :])
        ev1.after(ins)
    ctx.on("vector", f_vec)
    barrier(ctx)


def row_bcast(ctx, e_tensor, psum_slice, row_ap):
    return e_tensor.matmul(psum_slice, ctx.t["sbC"][0:1, 1792:1920], row_ap,
                           start=True, stop=True)


class TQ:
    def __init__(self, ctx, bf16=True):
        self.ctx = ctx
        self.bf16 = bf16
        self.reqs = []

    def add(self, src, dst):
        self.reqs.append((src, dst))

    def add_multi(self, src, dsts):
        self.reqs.append((src, dsts))

    def emit(self, wait_sems=()):
        ctx = self.ctx
        mm1, ev1 = ctx.sem("mm1"), ctx.sem("ev1")
        reqs = list(self.reqs)
        self.reqs = []
        base_mm = mm1.n
        bf = self.bf16
        evacs = [len(d) if isinstance(d, list) else 1 for _, d in reqs]
        cume = [0]
        for x in evacs:
            cume.append(cume[-1] + x)
        base_ev = ev1.n

        def f_t(e, reqs=reqs, ws=list(wait_sems), base_ev=base_ev, bf=bf, cume=cume):
            for s, v in ws:
                e.wait_ge(s.h, v)
            for i, (src, dst) in enumerate(reqs):
                if i >= 2:
                    e.wait_ge(ev1.h, base_ev + cume[i - 1])
                if bf:
                    ins = e.transpose(psT(ctx, i % 2)[:, 0:128], src, c_idbf16(ctx))
                else:
                    ins = e.transpose(ctx.t["psA"][:, 4 + i % 2, 0:128], src, c_idf32(ctx))
                mm1.after(ins)
        ctx.on("tensor", f_t)

        def f_v(e, reqs=reqs, base_mm=base_mm, bf=bf):
            for i, (src, dst) in enumerate(reqs):
                e.wait_ge(mm1.h, base_mm + i + 1)
                p = psT(ctx, i % 2) if bf else ctx.t["psA"][:, 4 + i % 2, 0:128]
                if isinstance(dst, list):
                    for cols, d in dst:
                        ins = e.tensor_copy(d, p[0:d.shape[0], cols])
                        ev1.after(ins)
                else:
                    ins = e.tensor_copy(dst, p[0:dst.shape[0], 0:dst.shape[1]])
                    ev1.after(ins)
        ctx.on("vector", f_v)


def xn_view(ctx):
    return ctx.t["sbA"].rearrange("p a b -> p (a b)").bitcast(BF16) \
        .rearrange("p (k t) -> p k t", k=16)


def emit_ln_token(ctx, x_src_2d):
    t = ctx.t
    dma1, mm1, ev1, sc0 = ctx.sem("dma1"), ctx.sem("mm1"), ctx.sem("ev1"), ctx.sem("sc0")
    sbB, sbM, sbS = t["sbB"], t["sbM"], t["sbS"]
    xnv = xn_view(ctx)

    prev_tr_mm = mm1.n
    prev_sqred_ev = 0
    for i in range(16):
        j = i % 2
        xslot = sbB[:, ts(j, 2048)]
        prev_done = ev1.n

        def f_ld(e, i=i, xslot=xslot, prev=prev_done):
            if i >= 2:
                e.wait_ge(ev1.h, prev)
            e.dma_start(out=xslot, in_=x_src_2d[ts(i, 128), :]).then_inc(dma1.h, 16)
        ctx.on("sync", f_ld)
        dma1.n += 16
        ld = dma1.n

        def f_sq(e, xslot=xslot, ld=ld, psq=prev_sqred_ev):
            e.wait_ge(dma1.h, ld)
            e.wait_ge(ev1.h, psq)
            ins = e.activation(sbM[:, 2048:4096], xslot, AF.Square)
            sc0.after(ins)
        ctx.on("scalar", f_sq)
        sq = sc0.n

        def f_stats(e, xslot=xslot, ld=ld, sq=sq):
            e.wait_ge(dma1.h, ld)
            sw(e, ev1, e.reduce_sum(sbS[:, 200:201], xslot, axis=AX))
            e.wait_ge(sc0.h, sq)
            sw(e, ev1, e.reduce_sum(sbS[:, 201:202], sbM[:, 2048:4096], axis=AX))
            e.tensor_scalar(sbS[:, 200:201], sbS[:, 200:201], 1.0 / D, None, ALU.mult)
            sw(e, ev1, e.tensor_scalar(sbS[:, 201:202], sbS[:, 201:202], 1.0 / D, None, ALU.mult))
            sw(e, ev1, e.tensor_scalar(sbS[:, 202:203], sbS[:, 200:201], sbS[:, 200:201],
                                       None, ALU.mult))
            ins = e.tensor_tensor(sbS[:, 201:202], sbS[:, 201:202], sbS[:, 202:203], ALU.subtract)
            ev1.after(ins)
        ctx.on("vector", f_stats)
        var_done = ev1.n

        def f_sqrt(e, v=var_done):
            e.wait_ge(ev1.h, v)
            ins = e.activation(sbS[:, 202:203], sbS[:, 201:202], AF.Sqrt, bias=c_epsln(ctx))
            sc0.after(ins)
        ctx.on("scalar", f_sqrt)
        sqr = sc0.n

        def f_apply(e, xslot=xslot, v=sqr, ptr=prev_tr_mm):
            e.wait_ge(sc0.h, v)
            e.wait_ge(mm1.h, ptr)
            sw(e, ev1, e.reciprocal(sbS[:, 203:204], sbS[:, 202:203]))
            ins = e.tensor_scalar(sbM[:, 0:1024].bitcast(BF16), xslot,
                                  sbS[:, 200:201], sbS[:, 203:204],
                                  ALU.subtract, ALU.mult)
            ev1.after(ins)
        ctx.on("vector", f_apply)
        xn_done = ev1.n

        tq = TQ(ctx, bf16=True)
        xn_tm = sbM[:, 0:1024].bitcast(BF16)
        for kk in range(16):
            tq.add(xn_tm[:, ts(kk, 128)], xnv[:, kk, ts(i, 128)])
        tq.emit(wait_sems=[(ev1, xn_done)])
        prev_tr_mm = mm1.n
        prev_sqred_ev = var_done
    barrier(ctx)


def wslab(ctx, buf):
    return ctx.t["sbW"].rearrange("p (u k m) -> p u k m", u=2, k=16)[:, buf]


def emit_proj_job(ctx, ji, wd, mcols, kt=16, mrows=128):
    t = ctx.t
    dma1, mm1, ev1 = ctx.sem("dma1"), ctx.sem("mm1"), ctx.sem("ev1")
    psA, sbM = t["psA"], t["sbM"]
    buf = ji % 2
    need_mm = ctx._slab_mm[buf]
    mw = mcols.stop - mcols.start

    def f_ld(e, wd=wd, mcols=mcols, buf=buf, need=need_mm, kt=kt, mw=mw):
        e.wait_ge(mm1.h, need)
        wv = wslab(ctx, buf)
        src = wd[:, mcols].rearrange("(k p) m -> p k m", p=128)
        e.dma_start(out=wv[:, 0:kt, 0:mw], in_=src).then_inc(dma1.h, 16)
    ctx.on("gpsimd", f_ld)
    dma1.n += 16
    slab_ld = dma1.n

    done = 0
    for th in range(2):
        slot = (2 * ji + th) % 3
        ev_need = ctx._slot_ev[slot]
        wv = wslab(ctx, buf)

        def f_mm(e, th=th, slot=slot, ld=slab_ld, ev_need=ev_need, wv=wv, kt=kt,
                 mw=mw):
            e.wait_ge(dma1.h, ld)
            if ev_need:
                e.wait_ge(ev1.h, ev_need)
            ins = None
            for k in range(kt):
                for tb in range(2):
                    ins = e.matmul(psA[0:mw, 2 * slot + tb, :], wv[:, k, 0:mw],
                                   xn_view(ctx)[:, k, ts(2 * th + tb, 512)],
                                   start=(k == 0), stop=(k == kt - 1))
            mm1.after(ins)
        ctx.on("tensor", f_mm)
        mmn = mm1.n

        gate = ctx._stage_gate if th == 0 else None

        def f_ev(e, th=th, slot=slot, mmn=mmn, gate=gate, mw=mw):
            e.wait_ge(mm1.h, mmn)
            if gate:
                e.wait_ge(gate[0].h, gate[1])
            e.tensor_copy(sbM[0:mw, th * 1024:th * 1024 + 512], psA[0:mw, 2 * slot, :])
            ins = e.tensor_copy(sbM[0:mw, th * 1024 + 512:(th + 1) * 1024],
                                psA[0:mw, 2 * slot + 1, :])
            ev1.after(ins)
        ctx.on("vector", f_ev)
        ctx._slot_ev[slot] = ev1.n
        done = ev1.n
    ctx._slab_mm[buf] = mm1.n
    return done


def emit_proj0(ctx):
    t = ctx.t
    dma1, dma2, mm1, ev1, sc0, gp1 = (ctx.sem("dma1"), ctx.sem("dma2"), ctx.sem("mm1"),
                                      ctx.sem("ev1"), ctx.sem("sc0"), ctx.sem("gp1"))
    sbM, sbS = t["sbM"], t["sbS"]
    ctx._slot_ev = {0: 0, 1: 0, 2: 0}
    ctx._slab_mm = [0, 0]
    ctx._stage_gate = None

    def f_taps(e):
        e.dma_start(out=sbS[:, 300:364].rearrange("p (k j) -> p k j", j=4),
                    in_=t["L0_attn_conv_q"].rearrange("(k p) j -> p k j", p=128)).then_inc(dma1.h, 16)
        e.dma_start(out=sbS[:, 364:396].rearrange("p (k j) -> p k j", j=4),
                    in_=t["L0_attn_conv_k"].rearrange("(k p) j -> p k j", p=128)).then_inc(dma1.h, 16)
        e.dma_start(out=sbS[:, 396:428].rearrange("p (k j) -> p k j", j=4),
                    in_=t["L0_attn_conv_v"].rearrange("(k p) j -> p k j", p=128)).then_inc(dma1.h, 16)
        e.dma_start(out=sbS[0:16, 428:429], in_=t["L0_attn_dt_bias"][:, None]).then_inc(dma1.h, 16)
        e.dma_start(out=sbS[32:48, 428:429], in_=t["L0_attn_bb"][:, None]).then_inc(dma1.h, 16)
        e.dma_start(out=sbS[64:80, 428:429], in_=t["L0_attn_balpha"][:, None]).then_inc(dma1.h, 16)
        e.dma_start(out=sbS[0:16, 430:431], in_=t["L0_attn_A_log"][:, None]).then_inc(dma1.h, 16)
    ctx.on("sync", f_taps)
    dma1.n += 7 * 16
    taps_ld = dma1.n

    def f_expa(e, v=taps_ld):
        e.wait_ge(dma1.h, v)
        ins = e.activation(sbS[0:16, 429:430], sbS[0:16, 430:431], AF.Exp)
        sc0.after(ins)
    ctx.on("scalar", f_expa)

    jobs = ([("q", t["L0_attn_Wq"], m, t["qT"], 300) for m in range(16)]
            + [("k", t["L0_attn_Wk"], m, t["kT"], 364) for m in range(8)]
            + [("v", t["L0_attn_Wv"], m, t["vT"], 396) for m in range(8)]
            + [("g", t["L0_attn_Wg"], m, t["gT"], None) for m in range(16)])

    for ji, (kind, wd, m, dst, tap0) in enumerate(jobs):
        stag_done = emit_proj_job(ctx, ji, wd, ts(m, 128))

        if kind != "g":
            def f_conv(e, tap0=tap0, m=m, sd=stag_done):
                e.wait_ge(ev1.h, sd)
                y = sbM[:, 0:2048]
                acc = sbM[:, 2048:4096]
                tap = lambda j: sbS[:, tap0 + 4 * m + j:tap0 + 4 * m + j + 1]
                sw(e, ev1, e.tensor_scalar(acc, y, tap(3), None, ALU.mult))
                sw(e, ev1, e.scalar_tensor_tensor(acc[:, 1:2048], y[:, 0:2047], tap(2),
                                                  acc[:, 1:2048], ALU.mult, ALU.add))
                sw(e, ev1, e.scalar_tensor_tensor(acc[:, 2:2048], y[:, 0:2046], tap(1),
                                                  acc[:, 2:2048], ALU.mult, ALU.add))
                ins = e.scalar_tensor_tensor(acc[:, 3:2048], y[:, 0:2045], tap(0),
                                             acc[:, 3:2048], ALU.mult, ALU.add)
                ev1.after(ins)
            ctx.on("vector", f_conv)
            conv_done = ev1.n

            def f_silu(e, cd=conv_done, pst=dma2.n):
                e.wait_ge(ev1.h, cd)
                e.wait_ge(dma2.h, pst)
                ins = e.activation(sbM[:, 4096 - 1024 + 1024:5120].bitcast(BF16),
                                   sbM[:, 2048:4096], AF.Silu)
                sc0.after(ins)
            ctx.on("scalar", f_silu)
            sdone = sc0.n

            def f_st(e, dst=dst, m=m, sd=sdone):
                e.wait_ge(sc0.h, sd)
                e.dma_start(out=dst[ts(m, 128), :], in_=sbM[:, 4096:5120].bitcast(BF16)) \
                    .then_inc(dma2.h, 16)
            ctx.on("gpsimd", f_st)
            dma2.n += 16
            ctx._stage_gate = (sc0, sdone)
        else:
            def f_cast(e, sd=stag_done, pst=dma2.n):
                e.wait_ge(ev1.h, sd)
                e.wait_ge(dma2.h, pst)
                ins = e.tensor_copy(sbM[:, 4096:5120].bitcast(BF16), sbM[:, 0:2048])
                ev1.after(ins)
            ctx.on("vector", f_cast)
            cdone = ev1.n

            def f_st(e, dst=dst, m=m, cd=cdone):
                e.wait_ge(ev1.h, cd)
                e.dma_start(out=dst[ts(m, 128), :], in_=sbM[:, 4096:5120].bitcast(BF16)) \
                    .then_inc(dma2.h, 16)
            ctx.on("gpsimd", f_st)
            dma2.n += 16
            ctx._stage_gate = (ev1, cdone)

    ji = len(jobs)
    buf = ji % 2
    need_mm = ctx._slab_mm[buf]

    def f_zero(e, buf=buf, need=need_mm):
        e.wait_ge(mm1.h, need)
        ins = e.memset(wslab(ctx, buf)[:, :, 0:80], 0.0)
        gp1.after(ins)
    ctx.on("gpsimd", f_zero)
    zd = gp1.n

    def f_lda(e, buf=buf, zd=zd):
        e.wait_ge(gp1.h, zd)
        wv = wslab(ctx, buf)
        e.dma_start(out=wv[:, 0:16, 0:16],
                    in_=t["L0_attn_Wa"].rearrange("(k p) m -> p k m", p=128)).then_inc(dma1.h, 16)
        e.dma_start(out=wv[:, 0:16, 32:48],
                    in_=t["L0_attn_Wb"].rearrange("(k p) m -> p k m", p=128)).then_inc(dma1.h, 16)
        e.dma_start(out=wv[:, 0:16, 64:80],
                    in_=t["L0_attn_Walpha"].rearrange("(k p) m -> p k m", p=128)).then_inc(dma1.h, 16)
    ctx.on("gpsimd", f_lda)
    dma1.n += 48
    slab_ld = dma1.n

    wv = wslab(ctx, buf)
    for th in range(2):
        slot = th
        ev_need = ctx._slot_ev[slot]

        def f_mm(e, th=th, slot=slot, ld=slab_ld, ev_need=ev_need, wv=wv):
            e.wait_ge(dma1.h, ld)
            if ev_need:
                e.wait_ge(ev1.h, ev_need)
            ins = None
            for k in range(16):
                for tb in range(2):
                    ins = e.matmul(ctx.t["psA"][0:80, 2 * slot + tb, :], wv[:, k, 0:80],
                                   xn_view(ctx)[:, k, ts(2 * th + tb, 512)],
                                   start=(k == 0), stop=(k == 15))
            mm1.after(ins)
        ctx.on("tensor", f_mm)
        mmn = mm1.n

        def f_ev(e, th=th, slot=slot, mmn=mmn):
            e.wait_ge(mm1.h, mmn)
            e.tensor_copy(sbM[0:80, 4096 + th * 1024:4096 + th * 1024 + 512],
                          ctx.t["psA"][0:80, 2 * slot, :])
            ins = e.tensor_copy(sbM[0:80, 4096 + th * 1024 + 512:4096 + (th + 1) * 1024],
                                ctx.t["psA"][0:80, 2 * slot + 1, :])
            ev1.after(ins)
        ctx.on("vector", f_ev)
        ctx._slot_ev[slot] = ev1.n
    barrier(ctx)


def emit_afam_post(ctx):
    t = ctx.t
    dma2, dma3, mm1, ev1, sc0, gp1 = (ctx.sem("dma2"), ctx.sem("dma3"), ctx.sem("mm1"),
                                      ctx.sem("ev1"), ctx.sem("sc0"), ctx.sem("gp1"))
    sbM, sbS, sbB, psA = t["sbM"], t["sbS"], t["sbB"], t["psA"]
    z = sbM[:, 4096:6144]
    za, zb, zal = z[0:16, :], z[32:48, :], z[64:80, :]
    CS = sbB[0:16, 0:2048]
    SCR = sbB[0:16, 2048:4096]
    LNQ = sbB[0:16, 4096:6144]
    BET = sbB[0:16, 6144:8192]
    WEN = sbB[0:16, 8192:10240]
    ALP = sbB[0:16, 10240:12288]
    FAM = sbM[:, 1024:3072]
    LNB = sbM[:, 3072:5120]
    biascol = lambda r0: sbS[r0:r0 + 16, 428:429]

    def f_zeros(e):
        ins = e.memset(sbS[0:16, 440:504], 0.0)
        gp1.after(ins)
    ctx.on("gpsimd", f_zeros)
    zzd = gp1.n

    def f_sc1(e):
        sw(e, sc0, e.activation(ALP, zal, AF.Sigmoid, bias=biascol(64)))
        sw(e, sc0, e.activation(BET, zb, AF.Sigmoid, bias=biascol(32)))
        sw(e, sc0, e.activation(CS, za, AF.Identity, bias=biascol(0)))
        sw(e, sc0, e.activation(SCR, CS, AF.Abs))
        sw(e, sc0, e.activation(SCR, SCR, AF.Exp, scale=-1.0))
        ins = e.activation(SCR, SCR, AF.Ln, bias=c_onecol(ctx)[0:16, :])
        sc0.after(ins)
    ctx.on("scalar", f_sc1)
    spd = sc0.n

    def f_glog(e, v=spd):
        e.wait_ge(sc0.h, v)
        sw(e, ev1, e.tensor_scalar(CS, CS, 0.0, None, ALU.max))
        sw(e, ev1, e.tensor_tensor(CS, CS, SCR, ALU.add))
        ins = e.tensor_scalar(CS, CS, sbS[0:16, 429:430], -1.0, ALU.mult, ALU.mult)
        ev1.after(ins)
    ctx.on("vector", f_glog)
    gld = ev1.n

    def f_scan(e, v=gld, zz=zzd):
        e.wait_ge(ev1.h, v)
        e.wait_ge(gp1.h, zz)
        ins = None
        for n in range(NCH):
            ins = e.tensor_tensor_scan(CS[:, ts(n, 64)], CS[:, ts(n, 64)],
                                       sbS[0:16, 440:504], 0.0, ALU.add, ALU.add)
        ev1.after(ins)
    ctx.on("vector", f_scan)
    csd = ev1.n

    def f_csout(e, v=csd):
        e.wait_ge(ev1.h, v)
        e.dma_start(out=t["csflat"][:, :], in_=CS).then_inc(dma2.h, 16)
    ctx.on("gpsimd", f_csout)
    dma2.n += 16

    def f_sc2(e, v=csd):
        e.wait_ge(ev1.h, v)
        sw(e, sc0, e.activation(LNQ, zb, AF.Identity, bias=biascol(32)))
        sw(e, sc0, e.activation(SCR, LNQ, AF.Abs))
        sw(e, sc0, e.activation(SCR, SCR, AF.Exp, scale=-1.0))
        ins = e.activation(SCR, SCR, AF.Ln, bias=c_onecol(ctx)[0:16, :])
        sc0.after(ins)
    ctx.on("scalar", f_sc2)
    lsd = sc0.n

    def f_lnbq(e, v=lsd):
        e.wait_ge(sc0.h, v)
        sw(e, ev1, e.tensor_scalar(LNQ, LNQ, 0.0, LNQS, ALU.min, ALU.add))
        ins = e.tensor_tensor(LNQ, LNQ, SCR, ALU.subtract)
        ev1.after(ins)
    ctx.on("vector", f_lnbq)
    lbd = ev1.n

    def f_wpre(e, v=csd):
        e.wait_ge(ev1.h, v)
        ins = None
        for n in range(NCH):
            gcol = CS[:, 64 * n + 63:64 * n + 64]
            ins = e.activation(WEN[:, ts(n, 64)], CS[:, ts(n, 64)], AF.Exp,
                               scale=-1.0, bias=gcol)
        sc0.after(ins)
    ctx.on("scalar", f_wpre)
    wpd = sc0.n

    def f_wend(e, v=wpd):
        e.wait_ge(sc0.h, v)
        ins = e.tensor_tensor(WEN, WEN, BET, ALU.mult)
        ev1.after(ins)
    ctx.on("vector", f_wend)
    wed = ev1.n

    def f_expg(e, v=csd):
        e.wait_ge(ev1.h, v)
        ins = e.activation(sbS[0:16, 440:472], CS[:, 63:2048:64], AF.Exp)
        sc0.after(ins)
    ctx.on("scalar", f_expg)
    egd = sc0.n

    def f_flat(e, v=egd, base=dma3.n):
        e.wait_ge(sc0.h, v)
        e.dma_start(out=t["rowbuf"][0:512], in_=sbS[0:16, 440:472]).then_inc(dma3.h, 16)
        e.wait_ge(dma3.h, base + 16)
        e.dma_start(out=sbB[0:1, 12288:12800], in_=t["rowbuf"][None, 0:512]).then_inc(dma3.h, 16)
    ctx.on("gpsimd", f_flat)
    dma3.n += 32
    fld = dma3.n

    def f_bc(e, v=fld):
        e.wait_ge(dma3.h, v)
        ins = row_bcast(ctx, e, psA[:, 0, 0:512], sbB[0:1, 12288:12800])
        mm1.after(ins)
    ctx.on("tensor", f_bc)
    bcd = mm1.n

    def f_bcev(e, v=bcd):
        e.wait_ge(mm1.h, v)
        ins = e.tensor_copy(sbM[:, 5120:5632], psA[:, 0, 0:512])
        ev1.after(ins)
    ctx.on("vector", f_bcev)
    dgd = ev1.n

    def f_place(e, v=max(wed, lbd, dgd, csd)):
        e.wait_ge(ev1.h, v)
        sw(e, ev1, e.tensor_copy(FAM[0:16, :], CS))
        sw(e, ev1, e.tensor_copy(FAM[32:48, :], WEN))
        sw(e, ev1, e.tensor_copy(FAM[64:80, :], ALP))
        ins = e.tensor_copy(LNB[0:16, :], LNQ)
        ev1.after(ins)
    ctx.on("vector", f_place)
    pld = ev1.n

    tq = TQ(ctx, bf16=False)
    famT = sbM[:, 0:1024]
    for i in range(16):
        tq.add_multi(FAM[:, ts(i, 128)], [
            (slice(0, 16), famT[:, 64 * i + 0:64 * i + 16]),
            (slice(32, 48), famT[:, 64 * i + 16:64 * i + 32]),
            (slice(64, 80), famT[:, 64 * i + 32:64 * i + 48]),
        ])
        tq.add_multi(LNB[:, ts(i, 128)], [
            (slice(0, 16), famT[:, 64 * i + 48:64 * i + 64]),
        ])
    tq.emit(wait_sems=[(ev1, pld)])
    barrier(ctx)


def fam_csT(ctx, i):   return ctx.t["sbM"][:, 64 * i + 0:64 * i + 16]
def fam_wendT(ctx, i): return ctx.t["sbM"][:, 64 * i + 16:64 * i + 32]
def fam_alphaT(ctx, i): return ctx.t["sbM"][:, 64 * i + 32:64 * i + 48]
def fam_lnbqT(ctx, i): return ctx.t["sbM"][:, 64 * i + 48:64 * i + 64]
def dgB(ctx):          return ctx.t["sbM"][:, 5120:5632]


class PsumPool:
    SLOTS = [(4, 0, 512), (5, 0, 512)]

    def __init__(self, ctx):
        self.ctx = ctx
        self.i = 0
        self.marks = [[] for _ in range(2)]

    def acquire(self, width):
        idx = self.i % 2
        self.i += 1
        bank, lo, hi = self.SLOTS[idx]
        assert width <= hi - lo
        waits = []
        for j, (b, _, _) in enumerate(self.SLOTS):
            if b == bank:
                waits.extend(self.marks[j])
                self.marks[j] = [] if j == idx else self.marks[j]
        ap = self.ctx.t["psA"][:, bank, lo:lo + width]
        return idx, ap, waits

    def release(self, idx, marks):
        self.marks[idx] = list(marks)


def emit_chunk_solve(ctx):
    t = ctx.t
    dma1, dma2, dma3 = ctx.sem("dma1"), ctx.sem("dma2"), ctx.sem("dma3")
    mm1, ev1, sc0, gp1 = ctx.sem("mm1"), ctx.sem("ev1"), ctx.sem("sc0"), ctx.sem("gp1")
    sbB, sbM, sbS, psA = t["sbB"], t["sbM"], t["sbS"], t["psA"]
    pool = PsumPool(ctx)
    ctx._oslab_free = 0

    Hst = lambda h: sbB[:, 128 * h:128 * (h + 1)]
    Bst = lambda h: sbB[:, 2048 + 128 * h:2048 + 128 * (h + 1)]
    negA = lambda h: sbB[:, 4096 + 64 * h:4096 + 64 * (h + 1)].bitcast(BF16)
    bbuf = lambda h: sbB[:, 5120 + 64 * h:5120 + 64 * (h + 1)].bitcast(BF16)
    dbuf = lambda h: sbB[:, 6144 + 64 * h:6144 + 64 * (h + 1)].bitcast(BF16)
    xsol = lambda h: sbB[:, 7168 + 128 * h:7168 + 128 * (h + 1)]
    kctm = sbB[:, 9216:9728].bitcast(BF16)
    vctm = sbB[:, 9728:10240].bitcast(BF16)
    qpair = sbB[:, 10240:11264].bitcast(BF16)
    gcsl = sbB[:, 11264:12288].bitcast(BF16)
    kpair = sbB[:, 12288:12800].bitcast(BF16)
    csrow = sbB[0:1, 12800:14848]
    exprow = sbB[0:1, 14848:14912]
    rn2 = sbB[:, 14912:14928]
    crow = sbB[0:1, 15360:16384]
    coef = sbM[:, 2048:2112]
    cBv = sbM[:, 2560:3584]
    oslab = sbM[:, 1024:2048].bitcast(BF16)
    Xbf = sbM[:, 3584:3648].bitcast(BF16)
    qexp = sbM[:, 3648:3680].bitcast(BF16)
    dmat = sbM[:, 3680:3744]
    sct = sbM[:, 3744:3776].bitcast(BF16)
    o1 = sbM[:, 3776:3904]
    osq = sbM[:, 3904:4032]
    sgl = sbM[:, 4032:4096].bitcast(BF16)
    tbufs = sbM[:, 4096 - 0 + 512 - 512:4096]
    tbuf = lambda h: sbM[:, 4352 + 64 * (h % 4):4352 + 64 * (h % 4) + 64].bitcast(BF16)
    vc0 = sbM[:, 4608:4672].bitcast(BF16)
    o3b = sbM[:, 4672:4736].bitcast(BF16)
    cols3 = sbS[:, 213:216]
    trc = sbS[:, 504:505]
    ones64 = sbS[:, 505:506]
    expG = sbS[0:16, 440:472]
    rstd_ = sbS[:, 216:217]

    def f_init(e):
        ins = e.memset(sbB[:, 0:4096], 0.0)
        gp1.after(ins)
    ctx.on("gpsimd", f_init)

    def f_init2(e):
        sw(e, ev1, e.memset(trc, 0.0))
        ins = e.memset(ones64, 1.0)
        ev1.after(ins)
    ctx.on("vector", f_init2)

    barrier(ctx)
    cheb_bank_t = {0: 0, 1: 0, 2: 0, 3: 0}

    for n in range(NCH):
        p, half = n // 2, n % 2
        c0 = half * 64

        if half == 0:
            pe_, pm_ = ev1.n, mm1.n

            def f_pl(e, p=p, pe=pe_, pm=pm_):
                e.wait_ge(ev1.h, pe)
                e.wait_ge(mm1.h, pm)
                e.dma_start(out=kctm, in_=t["kT"][:, ts(p, 128)], transpose=True) \
                    .then_inc(dma1.h, 16)
                e.dma_start(out=vctm, in_=t["vT"][:, ts(p, 128)], transpose=True) \
                    .then_inc(dma1.h, 16)
                e.dma_start(out=gcsl, in_=t["gT"][:, ts(p, 128)], transpose=True) \
                    .then_inc(dma1.h, 16)
                e.dma_start(out=qpair.rearrange("p (k c) -> p k c", k=16),
                            in_=t["qT"][:, ts(p, 128)].rearrange("(k q) c -> q k c", q=128)) \
                    .then_inc(dma1.h, 16)
                e.dma_start(out=kpair.rearrange("p (k c) -> p k c", k=8),
                            in_=t["kT"][:, ts(p, 128)].rearrange("(k q) c -> q k c", q=128)) \
                    .then_inc(dma1.h, 16)
                e.dma_start(out=csrow.rearrange("a (h c) -> a h c", h=16),
                            in_=t["csflat"][:, ts(p, 128)][None, :, :]) \
                    .then_inc(dma1.h, 16)
            ctx.on("sync", f_pl)
            dma1.n += 96
        pair_ld = dma1.n

        base_sq = sc0.n
        base_red = ev1.n
        sq_of = {kv: base_sq + kv + 1 for kv in range(8)}
        red_of = {kv: base_red + 2 * (kv + 1) for kv in range(8)}
        sqbuf = lambda kv: (osq if kv % 2 == 0 else o1)

        def f_rn2_sc(e, c0=c0, ld=pair_ld, pv=base_red):
            e.wait_ge(dma1.h, ld)
            e.wait_ge(ev1.h, pv)
            for kv in range(8):
                if kv >= 2:
                    e.wait_ge(ev1.h, red_of[kv - 2])
                ins = e.activation(sqbuf(kv)[0:64, :], kctm[c0:c0 + 64, ts(kv, 128)],
                                   AF.Square)
                sc0.after(ins)
        ctx.on("scalar", f_rn2_sc)
        assert sc0.n == base_sq + 8

        def f_rn2_v(e, c0=c0):
            for kv in range(8):
                e.wait_ge(sc0.h, sq_of[kv])
                sw(e, ev1, e.reduce_sum(rn2[c0:c0 + 64, 2 * kv:2 * kv + 1],
                                        sqbuf(kv)[0:64, :], axis=AX))
                ins = e.tensor_copy(rn2[c0:c0 + 64, 2 * kv + 1:2 * kv + 2],
                                    rn2[c0:c0 + 64, 2 * kv:2 * kv + 1])
                ev1.after(ins)
        ctx.on("vector", f_rn2_v)
        assert ev1.n == base_red + 16
        rn2_done = ev1.n

        def f_wr(e, n=n, c0=c0, v=rn2_done):
            e.wait_ge(ev1.h, v)
            ins = e.tensor_tensor(sbM[:, 2112:2128][c0:c0 + 64, :],
                                  sbM[c0:c0 + 64, 64 * (n // 2) + 16:64 * (n // 2) + 32],
                                  rn2[c0:c0 + 64, :], ALU.mult)
            ev1.after(ins)
        ctx.on("vector", f_wr)
        wrd = ev1.n

        idx, ap_trd, waits = pool.acquire(16)
        def f_trd(e, ap=ap_trd, v=wrd, waits=waits, c0=c0):
            e.wait_ge(ev1.h, v)
            for s, vv in waits:
                e.wait_ge(s.h, vv)
            ins = e.matmul(ap[0:16, 0:1], sbM[:, 2112:2128][c0:c0 + 64, :],
                           ones64[c0:c0 + 64, :], start=True, stop=True)
            mm1.after(ins)
        ctx.on("tensor", f_trd)
        trd_mm = mm1.n

        def f_coef(e, n=n, v=trd_mm, ap=ap_trd):
            e.wait_ge(mm1.h, v)
            C = coef
            th_ = sbS[:, 506:507]; dl_ = sbS[:, 507:508]; sg1 = sbS[:, 508:509]
            rho = sbS[:, 509:510]; tmp = sbS[:, 510:511]; i2d = sbS[:, 511:512]
            sw(e, ev1, e.tensor_scalar(th_[0:16, :], trc[0:16, :], 128 * RIDGE + RIDGE, 0.5,
                                       ALU.add, ALU.mult))
            sw(e, ev1, e.tensor_scalar(dl_[0:16, :], trc[0:16, :], 128 * RIDGE - RIDGE, 0.5,
                                       ALU.add, ALU.mult))
            sw(e, ev1, e.scalar_tensor_tensor(trc[0:16, :], trc[0:16, :], expG[:, n:n + 1],
                                              ap[0:16, 0:1], ALU.mult, ALU.add))
            sw(e, ev1, e.reciprocal(tmp[0:16, :], dl_[0:16, :]))
            sw(e, ev1, e.tensor_scalar(i2d[0:16, :], tmp[0:16, :], 2.0, None, ALU.mult))
            sw(e, ev1, e.tensor_scalar(sg1[0:16, :], tmp[0:16, :], th_[0:16, :], None, ALU.mult))
            sw(e, ev1, e.reciprocal(tmp[0:16, :], th_[0:16, :]))
            sw(e, ev1, e.tensor_copy(C[0:16, 63:64], tmp[0:16, :]))
            sw(e, ev1, e.tensor_scalar(rho[0:16, :], tmp[0:16, :], dl_[0:16, :], None, ALU.mult))
            for i in range(NUM_ITER - 1):
                sw(e, ev1, e.scalar_tensor_tensor(tmp[0:16, :], sg1[0:16, :], 2.0,
                                                  rho[0:16, :], ALU.mult, ALU.subtract))
                sw(e, ev1, e.reciprocal(tmp[0:16, :], tmp[0:16, :]))
                sw(e, ev1, e.tensor_scalar(C[0:16, i:i + 1], tmp[0:16, :], rho[0:16, :],
                                           None, ALU.mult))
                sw(e, ev1, e.tensor_scalar(C[0:16, 32 + i:33 + i], tmp[0:16, :],
                                           i2d[0:16, :], None, ALU.mult))
                sw(e, ev1, e.tensor_copy(rho[0:16, :], tmp[0:16, :]))
            ev1.after(e.engine_nop())
        ctx.on("vector", f_coef)
        coef_done = ev1.n
        pool.release(idx, [(ev1, coef_done)])

        def f_cflat(e, v=coef_done, base=dma3.n):
            e.wait_ge(ev1.h, v)
            e.dma_start(out=t["rowbuf"][0:1024], in_=coef[0:16, :]).then_inc(dma3.h, 16)
            e.wait_ge(dma3.h, base + 16)
            e.dma_start(out=crow, in_=t["rowbuf"][None, 0:1024]).then_inc(dma3.h, 16)
        ctx.on("gpsimd", f_cflat)
        dma3.n += 32
        cfl = dma3.n

        i1, apc1, w1 = pool.acquire(512)
        i2, apc2, w2 = pool.acquire(512)
        def f_cbc(e, v=cfl, apc1=apc1, apc2=apc2, w1=w1, w2=w2):
            e.wait_ge(dma3.h, v)
            for s, vv in w1 + w2:
                e.wait_ge(s.h, vv)
            e.matmul(apc1, ctx.t["sbC"][0:1, 1792:1920], crow[:, 0:512],
                     start=True, stop=True)
            ins = e.matmul(apc2, ctx.t["sbC"][0:1, 1792:1920], crow[:, 512:1024],
                           start=True, stop=True)
            mm1.after(ins)
        ctx.on("tensor", f_cbc)
        cbcd = mm1.n

        def f_cbev(e, v=cbcd, apc1=apc1, apc2=apc2):
            e.wait_ge(mm1.h, v)
            e.tensor_copy(cBv[:, 0:512], apc1)
            ins = e.tensor_copy(cBv[:, 512:1024], apc2)
            ev1.after(ins)
        ctx.on("vector", f_cbev)
        cbd = ev1.n
        pool.release(i1, [(ev1, cbd)])
        pool.release(i2, [(ev1, cbd)])
        cb_c1 = lambda h, i: cBv[:, 64 * h + i:64 * h + i + 1]
        cb_c2 = lambda h, i: cBv[:, 64 * h + 32 + i:64 * h + 33 + i]
        cb_ith = lambda h: cBv[:, 64 * h + 63:64 * h + 64]

        if getattr(ctx, "_dbg_chunk", None) == n:
            def f_dbgst(e, v=cbd, base=dma2.n):
                e.wait_ge(ev1.h, v)
                for h in range(HQ):
                    e.dma_start(out=ctx.t["d_Hst"][ts(h, 128), :], in_=Hst(h)).then_inc(dma2.h, 16)
                    e.dma_start(out=ctx.t["d_Bst"][ts(h, 128), :], in_=Bst(h)).then_inc(dma2.h, 16)
                e.dma_start(out=ctx.t["d_coef"], in_=coef[0:16, :]).then_inc(dma2.h, 16)
                e.wait_ge(dma2.h, base + 33 * 16)
            ctx.on("gpsimd", f_dbgst)
            dma2.n += 33 * 16
            dbg_hold = dma2.n

            def f_hold(e, v=dbg_hold):
                e.wait_ge(dma2.h, v)
            ctx.on("vector", f_hold)

        upd_marks = []
        for h in range(HQ):
            kv = h // 2

            def f_prep(e, h=h, v=cbd):
                e.wait_ge(ev1.h, v)
                sw(e, ev1, e.scalar_tensor_tensor(negA(h), Hst(h), -1.0, c_negridgeI(ctx),
                                                  ALU.mult, ALU.add))
                sw(e, ev1, e.tensor_copy(bbuf(h), Bst(h)))
                ins = e.tensor_scalar(dbuf(h), Bst(h), cb_ith(h), None, ALU.mult)
                ev1.after(ins)
            ctx.on("vector", f_prep)
            prep_done = ev1.n

            def f_kcw(e, h=h, kv=kv, n=n, c0=c0, v=prep_done):
                ins = e.tensor_scalar(sbM[:, 4032:4096].bitcast(BF16)[c0:c0 + 64, :],
                                      kctm[c0:c0 + 64, ts(kv, 128)],
                                      sbM[c0:c0 + 64, 64 * (n // 2) + 16 + h:64 * (n // 2) + 17 + h],
                                      None, ALU.mult)
                ev1.after(ins)
            ctx.on("vector", f_kcw)
            kcw_done = ev1.n

            idx, apdh, waits = pool.acquire(256)
            def f_dhdb(e, h=h, kv=kv, c0=c0, v=kcw_done, ap=apdh, waits=waits):
                e.wait_ge(ev1.h, v)
                for s, vv in waits:
                    e.wait_ge(s.h, vv)
                kcw = sbM[:, 4032:4096].bitcast(BF16)[c0:c0 + 64, :]
                e.matmul(ap[:, 0:128], kcw, kctm[c0:c0 + 64, ts(kv, 128)],
                         start=True, stop=True)
                ins = e.matmul(ap[:, 128:256], kcw, vctm[c0:c0 + 64, ts(kv, 128)],
                               start=True, stop=True)
                mm1.after(ins)
            ctx.on("tensor", f_dhdb)
            dh_mm = mm1.n

            def f_upd(e, h=h, n=n, v=dh_mm, ap=apdh):
                e.wait_ge(mm1.h, v)
                dgcol = dgB(ctx)[:, 32 * h + n:32 * h + n + 1]
                sw(e, ev1, e.scalar_tensor_tensor(Hst(h), Hst(h), dgcol, ap[:, 0:128],
                                                  ALU.mult, ALU.add))
                ins = e.scalar_tensor_tensor(Bst(h), Bst(h), dgcol, ap[:, 128:256],
                                             ALU.mult, ALU.add)
                ev1.after(ins)
            ctx.on("vector", f_upd)
            pool.release(idx, [(ev1, ev1.n)])
        prep_all = ev1.n

        if getattr(ctx, "_dbg_chunk", None) == n:
            def f_dbg3(e, v=prep_all, base=dma2.n):
                e.wait_ge(ev1.h, v)
                e.dma_start(out=ctx.t["d_negA"], in_=negA(0)).then_inc(dma2.h, 16)
                e.dma_start(out=ctx.t["d_bb"], in_=bbuf(0)).then_inc(dma2.h, 16)
                e.dma_start(out=ctx.t["d_d0"], in_=dbuf(0)).then_inc(dma2.h, 16)
                e.dma_start(out=ctx.t["d_cBv"], in_=cBv).then_inc(dma2.h, 16)
                e.wait_ge(dma2.h, base + 64)
            ctx.on("gpsimd", f_dbg3)
            dma2.n += 64

        if getattr(ctx, "_cheb_serial", False):
            x_of = {}
            for h in range(HQ):
                bank = h % 4
                rr = psA[:, bank, 128 * (h // 4):128 * (h // 4) + 128]
                pt = cheb_bank_t[bank]

                def f_pe1(e, h=h, rr=rr, v=prep_all, pt=pt):
                    e.wait_ge(ev1.h, v)
                    if pt:
                        e.wait_ge(sc0.h, pt)
                    e.wait_ge(gp1.h, gp1.n)
                    mm1.after(e.matmul(rr, c_idbf16(ctx), bbuf(h), start=True, stop=False))
                ctx.on("tensor", f_pe1)
                for i in range(NUM_ITER - 1):
                    dprev = ev1.n

                    def f_mm_i(e, h=h, rr=rr, i=i, dp=dprev, gp=gp1.n):
                        e.wait_ge(ev1.h, dp)
                        e.wait_ge(gp1.h, gp)
                        mm1.after(e.matmul(rr, negA(h), dbuf(h), start=False,
                                           stop=(i == NUM_ITER - 2)))
                    ctx.on("tensor", f_mm_i)
                    mmn = mm1.n

                    def f_t_i(e, h=h, rr=rr, i=i, v=mmn):
                        e.wait_ge(mm1.h, v)
                        ins = e.activation(tbuf(h), rr, AF.Identity, scale=cb_c2(h, i))
                        sc0.after(ins)
                    ctx.on("scalar", f_t_i)
                    tn = sc0.n

                    def f_x_i(e, h=h, i=i, dp=dprev):
                        e.wait_ge(ev1.h, dp)
                        if i == 0:
                            ins = e.tensor_copy(xsol(h), dbuf(h))
                        else:
                            ins = e.tensor_tensor(xsol(h), xsol(h), dbuf(h), ALU.add)
                        gp1.after(ins)
                    ctx.on("gpsimd", f_x_i)
                    xn_ = gp1.n

                    def f_d_i(e, h=h, i=i, tn=tn, xn_=xn_):
                        e.wait_ge(sc0.h, tn)
                        e.wait_ge(gp1.h, xn_)
                        ins = e.scalar_tensor_tensor(dbuf(h), dbuf(h), cb_c1(h, i),
                                                     tbuf(h), ALU.mult, ALU.add)
                        ev1.after(ins)
                    ctx.on("vector", f_d_i)
                dlast = ev1.n

                def f_x_f(e, h=h, dl=dlast):
                    e.wait_ge(ev1.h, dl)
                    ins = e.tensor_tensor(xsol(h), xsol(h), dbuf(h), ALU.add)
                    gp1.after(ins)
                ctx.on("gpsimd", f_x_f)
                cheb_bank_t[bank] = sc0.n
            x_of = {(h, NUM_ITER - 1): gp1.n for h in range(HQ)}
            mm_of = {}
        else:
            emit_cheb_pipelined = True

        if getattr(ctx, "_cheb_serial", False):
            pass
        else:
          mm_of = {}
          t_of = {}
          d_of = {}
          x_of = {}

          for h in range(HQ):
              d_of[(h, 0)] = prep_all

          base_mm = mm1.n
          cnt = base_mm
          for h in range(HQ):
              cnt += 1
              mm_of[(h, -1)] = cnt
          for i in range(NUM_ITER - 1):
              for h in range(HQ):
                  cnt += 1
                  mm_of[(h, i)] = cnt
          base_sc = sc0.n
          scnt = base_sc
          for i in range(NUM_ITER - 1):
              for h in range(HQ):
                  scnt += 1
                  t_of[(h, i)] = scnt
          base_ev = ev1.n
          ecnt = base_ev
          for i in range(NUM_ITER - 1):
              for h in range(HQ):
                  ecnt += 1
                  d_of[(h, i + 1)] = ecnt
          base_gp = gp1.n
          gcnt = base_gp
          for i in range(NUM_ITER):
              for h in range(HQ):
                  gcnt += 1
                  x_of[(h, i)] = gcnt

          def f_pe(e, v=prep_all, cb=dict(cheb_bank_t)):
              e.wait_ge(ev1.h, v)
              for h in range(HQ):
                  bank = h % 4
                  if cb[bank]:
                      e.wait_ge(sc0.h, cb[bank])
                      cb[bank] = 0
                  rr = psA[:, bank, 128 * (h // 4):128 * (h // 4) + 128]
                  mm1.after(e.matmul(rr, c_idbf16(ctx), bbuf(h), start=True, stop=False))
              for i in range(NUM_ITER - 1):
                  for h in range(HQ):
                      bank = h % 4
                      e.wait_ge(ev1.h, d_of[(h, i)])
                      hp = h - 4 if h >= 4 else h + 12
                      ip = i if h >= 4 else i - 1
                      if (hp, ip) in t_of:
                          e.wait_ge(sc0.h, t_of[(hp, ip)])
                      rr = psA[:, bank, 128 * (h // 4):128 * (h // 4) + 128]
                      mm1.after(e.matmul(rr, negA(h), dbuf(h), start=False,
                                         stop=(i == NUM_ITER - 2)))
          ctx.on("tensor", f_pe)
          mm1.n = cnt

          def f_sc(e):
              for i in range(NUM_ITER - 1):
                  for h in range(HQ):
                      bank = h % 4
                      e.wait_ge(mm1.h, mm_of[(h, i)])
                      if i > 0 or h >= 4:
                          hp = h - 4 if h >= 4 else h + 12
                          ip = i if h >= 4 else i - 1
                          e.wait_ge(ev1.h, d_of[(hp, ip + 1)])
                      rr = psA[:, bank, 128 * (h // 4):128 * (h // 4) + 128]
                      ins = e.activation(tbuf(h), rr, AF.Identity, scale=cb_c2(h, i))
                      sc0.after(ins)
          ctx.on("scalar", f_sc)
          assert sc0.n == scnt, (sc0.n, scnt)

          def f_dv(e):
              for i in range(NUM_ITER - 1):
                  for h in range(HQ):
                      e.wait_ge(sc0.h, t_of[(h, i)])
                      e.wait_ge(gp1.h, x_of[(h, i)])
                      ins = e.scalar_tensor_tensor(dbuf(h), dbuf(h), cb_c1(h, i), tbuf(h),
                                                   ALU.mult, ALU.add)
                      ev1.after(ins)
          ctx.on("vector", f_dv)
          assert ev1.n == ecnt, (ev1.n, ecnt)

          def f_gp(e):
              for i in range(NUM_ITER):
                  for h in range(HQ):
                      e.wait_ge(ev1.h, d_of[(h, i)])
                      if i == 0:
                          ins = e.tensor_copy(xsol(h), dbuf(h))
                      else:
                          ins = e.tensor_tensor(xsol(h), xsol(h), dbuf(h), ALU.add)
                      gp1.after(ins)
          ctx.on("gpsimd", f_gp)
          assert gp1.n == gcnt, (gp1.n, gcnt)
          for b in range(4):
              hh = [h for h in range(HQ) if h % 4 == b]
              cheb_bank_t[b] = max(t_of[(h, NUM_ITER - 2)] for h in hh)
          cheb_done_ev = ecnt
          cheb_done_gp = gcnt

        if getattr(ctx, "_dbg_chunk", None) == n:
            def f_dbgx(e, base=dma2.n):
                for h in range(HQ):
                    e.wait_ge(gp1.h, x_of[(h, NUM_ITER - 1)])
                    e.dma_start(out=ctx.t["d_X"][ts(h, 128), :], in_=xsol(h)).then_inc(dma2.h, 16)
                e.wait_ge(dma2.h, base + 16 * 16)
            ctx.on("gpsimd", f_dbgx)
            dma2.n += 16 * 16

        prev_mmo = 0
        prev_mmA = 0
        prev_rmsd = 0
        for h in range(HQ):
            kv = h // 2

            def f_xbf(e, h=h, kv=kv, n=n, p=p, c0=c0, v=x_of[(h, NUM_ITER - 1)],
                      pm=prev_mmo, po=getattr(ctx, "_prev_oeps", 0)):
                e.wait_ge(gp1.h, v)
                if pm:
                    e.wait_ge(mm1.h, pm)
                sw(e, ev1, e.tensor_copy(Xbf, xsol(h)))
                if h % 2 == 0:
                    if po:
                        e.wait_ge(ev1.h, po)
                    sw(e, ev1, e.tensor_copy(vc0[0:64, :], vctm[c0:c0 + 64, ts(kv, 128)]))
                sw(e, ev1, e.tensor_copy(cols3[0:64, 0:1],
                                         sbM[c0:c0 + 64, 64 * p + h:64 * p + h + 1]))
                sw(e, ev1, e.tensor_copy(cols3[0:64, 1:2],
                                         sbM[c0:c0 + 64, 64 * p + 48 + h:64 * p + 49 + h]))
                ins = e.tensor_copy(cols3[0:64, 2:3],
                                    sbM[c0:c0 + 64, 64 * p + 32 + h:64 * p + 33 + h])
                ev1.after(ins)
            ctx.on("vector", f_xbf)
            xbf_done = ev1.n

            def f_exr(e, h=h, c0=c0, ld=pair_ld, pm=prev_mmA):
                e.wait_ge(dma1.h, ld)
                if pm:
                    e.wait_ge(mm1.h, pm)
                ins = e.activation(exprow, csrow[:, 128 * h + c0:128 * h + c0 + 64],
                                   AF.Exp, bias=c_lnqs(ctx)[0:1, :])
                sc0.after(ins)
            ctx.on("scalar", f_exr)
            exr_done = sc0.n

            iA, apA, wA = pool.acquire(192)
            def f_mmA(e, h=h, kv=kv, c0=c0, v=exr_done, ap=apA, waits=wA):
                for s, vv in waits:
                    e.wait_ge(s.h, vv)
                e.wait_ge(sc0.h, v)
                e.matmul(ap[0:128, 0:64], ctx.t["sbC"][0:1, 1792:1920],
                         csrow[:, 128 * h + c0:128 * h + c0 + 64], start=True, stop=True)
                e.matmul(ap[0:128, 64:128], ctx.t["sbC"][0:1, 1792:1920], exprow,
                         start=True, stop=True)
                ins = e.matmul(ap[0:64, 128:192], kpair[:, ts(kv, 128)][:, c0:c0 + 64],
                               qpair[:, ts(h, 128)][:, c0:c0 + 64], start=True, stop=True)
                mm1.after(ins)
            ctx.on("tensor", f_mmA)
            mmA = mm1.n

            def f_dmq(e, h=h, n=n, p=p, c0=c0, v=mmA, ap=apA, pm=prev_mmo):
                e.wait_ge(mm1.h, v)
                if pm:
                    e.wait_ge(mm1.h, pm)
                sw(e, ev1, e.tensor_scalar(dmat[0:64, :], ap[0:64, 0:64], cols3[0:64, 0:1],
                                           None, ALU.subtract))
                sw(e, ev1, e.tensor_tensor(dmat[0:64, :], dmat[0:64, :], c_mtri64(ctx)[0:64, :],
                                           ALU.add))
                ins = e.tensor_tensor(qexp[:, 0:64], qpair[:, ts(h, 128)][:, c0:c0 + 64],
                                      ap[0:128, 64:128][:, 0:64], ALU.mult)
                ev1.after(ins)
            ctx.on("vector", f_dmq)
            dmq = ev1.n

            def f_dexp(e, h=h, p=p, c0=c0, v=dmq):
                e.wait_ge(ev1.h, v)
                ins = e.activation(dmat[0:64, :], dmat[0:64, :], AF.Exp,
                                   bias=cols3[0:64, 1:2])
                sc0.after(ins)
            ctx.on("scalar", f_dexp)
            dexp = sc0.n

            def f_sct(e, h=h, v=dexp, ap=apA):
                e.wait_ge(sc0.h, v)
                ins = e.tensor_tensor(sct[0:64, :], dmat[0:64, :], ap[0:64, 128:192], ALU.mult)
                ev1.after(ins)
            ctx.on("vector", f_sct)
            sctd = ev1.n
            pool.release(iA, [(ev1, sctd)])

            iB, apB, wB = pool.acquire(128)
            def f_mmo(e, h=h, kv=kv, c0=c0, vx=xbf_done, vs=sctd, ap=apB, waits=wB):
                e.wait_ge(ev1.h, max(vx, vs))
                for s, vv in waits:
                    e.wait_ge(s.h, vv)
                e.matmul(ap[0:64, :], qexp[:, 0:64], Xbf, start=True, stop=False)
                ins = e.matmul(ap[0:64, :], sct[0:64, :],
                               vc0[0:64, :], start=False, stop=True)
                mm1.after(ins)
            ctx.on("tensor", f_mmo)
            mmo = mm1.n

            def f_oeps(e, h=h, kv=kv, n=n, p=p, c0=c0, v=mmo, ap=apB):
                e.wait_ge(mm1.h, v)
                ins = e.scalar_tensor_tensor(o1[0:64, :], vc0[0:64, :],
                                             cols3[0:64, 2:3], ap[0:64, :], ALU.mult, ALU.add)
                ev1.after(ins)
            ctx.on("vector", f_oeps)
            o1d = ev1.n
            ctx._prev_oeps = o1d
            pool.release(iB, [(ev1, o1d)])

            def f_osq(e, v=o1d, pr=prev_rmsd):
                e.wait_ge(ev1.h, v)
                if pr:
                    e.wait_ge(ev1.h, pr)
                ins = e.activation(osq[0:64, :], o1[0:64, :], AF.Square)
                sc0.after(ins)
            ctx.on("scalar", f_osq)
            osqd = sc0.n

            def f_rms(e, v=osqd):
                e.wait_ge(sc0.h, v)
                sw(e, ev1, e.reduce_sum(rstd_[0:64, :], osq[0:64, :], axis=AX))
                ins = e.tensor_scalar(rstd_[0:64, :], rstd_[0:64, :], 1.0 / HD, None, ALU.mult)
                ev1.after(ins)
            ctx.on("vector", f_rms)
            rmsd = ev1.n

            def f_rsq(e, h=h, c0=c0, v=rmsd):
                e.wait_ge(ev1.h, v)
                sw(e, sc0, e.activation(rstd_[0:64, :], rstd_[0:64, :], AF.Sqrt,
                                        bias=c_eps0(ctx)[0:64, :]))
                ins = e.activation(sgl[0:64, :], gcsl[c0:c0 + 64, ts(h, 128)], AF.Silu)
                sc0.after(ins)
            ctx.on("scalar", f_rsq)
            rsqd = sc0.n

            def f_ofin(e, h=h, c0=c0, v=rsqd, of=getattr(ctx, "_oslab_free", 0)):
                e.wait_ge(sc0.h, v)
                if of:
                    e.wait_ge(mm1.h, of)
                sw(e, ev1, e.reciprocal(rstd_[0:64, :], rstd_[0:64, :]))
                sw(e, ev1, e.scalar_tensor_tensor(o1[0:64, :], o1[0:64, :], rstd_[0:64, :],
                                                  c_normwB(ctx)[0:64, :], ALU.mult, ALU.mult))
                sw(e, ev1, e.tensor_tensor(o3b[0:64, :], o1[0:64, :], sgl[0:64, :], ALU.mult))
                ins = e.tensor_copy(oslab[c0:c0 + 64, ts(h, 128)], o3b[0:64, :])
                ev1.after(ins)
            ctx.on("vector", f_ofin)
            prev_mmo = mmo
            prev_mmA = mmA
            prev_rmsd = rmsd
        out_done = ev1.n

        if half == 1:
            tq = TQ(ctx, bf16=True)
            oTv = xn_view(ctx)
            for h in range(HQ):
                tq.add(oslab[:, ts(h, 128)], oTv[:, h, ts(p, 128)])
            tq.emit(wait_sems=[(ev1, out_done)])
            ctx._oslab_free = mm1.n
    barrier(ctx)
    ctx._oslab_free = 0


def emit_dequant_one(ctx, wp, ws, wout, nrows, ncols_b):
    t = ctx.t
    dma1, dma2, ev1, gp1 = ctx.sem("dma1"), ctx.sem("dma2"), ctx.sem("ev1"), ctx.sem("gp1")
    sbB, sbM = t["sbB"], t["sbM"]
    nblk = ncols_b // 16
    W = ncols_b
    assert W in (1024, 8192)
    if W == 1024:
        RAW = sbB[:, 0:1024].bitcast(I32)
        NIB = lambda j: sbB[:, 1024 + 1024 * j:2048 + 1024 * j].bitcast(I32)
        BITS = sbB[:, 3072:4096].bitcast(I32)
        U16 = sbB[:, 4096:4608].bitcast(mybir.dt.uint16)
        SC = sbB[:, 5120:5184]
        SCE = sbB[:, 6144:7168]
        OUT = sbB[:, 8192:9216].bitcast(BF16)
    else:
        RAW = sbB[:, 0:8192].bitcast(I32)
        NIB = None
    dgq = Sem(ctx.sems["dma3"].h)
    dma3 = ctx.sem("dma3")

    n_slabs = nrows // 128
    CHUNKS = W // 1024
    for s_ in range(n_slabs):
        prev = ev1.n

        def f_ld(e, s_=s_, pv=getattr(ctx, "_dq_store", 0)):
            e.wait_ge(dma2.h, pv)
            e.dma_start(out=sbB[:, 0:W].bitcast(I32), in_=wp[ts(s_, 128), :]) \
                .then_inc(dma1.h, 16)
            e.dma_start(out=sbB[:, 15360:15360 + nblk], in_=ws[ts(s_, 128), :]) \
                .then_inc(dma1.h, 16)
        ctx.on("sync", f_ld)
        dma1.n += 32
        ld = dma1.n

        for cc in range(CHUNKS):
            raw = sbB[:, cc * 1024:cc * 1024 + 1024].bitcast(I32)
            nib = lambda j: sbB[:, 8192 + 1024 * j:9216 + 1024 * j].bitcast(I32)
            bits = sbB[:, 10240:11264].bitcast(I32)
            u16 = sbB[:, 11264:11776].bitcast(mybir.dt.uint16)
            sce = sbB[:, 12288:13312]
            outb = sbB[:, 13312:14336].bitcast(BF16)

            def f_dq(e, cc=cc, ld=ld, raw=raw, nib=nib, bits=bits, u16=u16,
                     sce=sce, outb=outb, s_=s_):
                e.wait_ge(dma1.h, ld)
                for j in range(2):
                    n_ = nib(j)
                    if j == 0:
                        sw(e, ev1, e.tensor_single_scalar(n_, raw, 15, ALU.bitwise_and))
                    else:
                        sw(e, ev1, e.tensor_single_scalar(n_, raw, 4, ALU.logical_shift_right))
                        sw(e, ev1, e.tensor_single_scalar(n_, n_, 15, ALU.bitwise_and))
                    sw(e, ev1, e.tensor_single_scalar(bits, n_, 8, ALU.bitwise_and))
                    sw(e, ev1, e.tensor_single_scalar(n_, n_, 7, ALU.bitwise_and))
                    t1 = sbB[:, 14336:15360].bitcast(I32)
                    sw(e, ev1, e.tensor_single_scalar(t1, n_, 2, ALU.min))
                    sw(e, ev1, e.tensor_tensor(t1, t1, n_, ALU.add))
                    sw(e, ev1, e.tensor_scalar(n_, n_, 1, 16000, ALU.min, ALU.mult))
                    sw(e, ev1, e.scalar_tensor_tensor(t1, t1, 64, n_, ALU.mult, ALU.add))
                    sw(e, ev1, e.scalar_tensor_tensor(bits, bits, 4096, t1, ALU.mult, ALU.add))
                    sw(e, ev1, e.tensor_copy(u16, bits))
                    sw(e, ev1, e.tensor_copy(
                        outb.rearrange("p (a b) -> p a b", b=2)[:, :, j],
                        u16.bitcast(BF16)))
                for jj in range(16):
                    sw(e, ev1, e.tensor_copy(
                        sce.rearrange("p (a b) -> p a b", b=16)[:, :, jj],
                        sbB[:, 15360 + cc * 64:15360 + cc * 64 + 64]))
                ins = e.tensor_tensor(outb, outb,
                                      sce.rearrange("p (a b) -> p a b", b=1)[:, :, 0]
                                      if False else sce, ALU.mult) if False else None
                ins = e.tensor_tensor(outb.rearrange("p (a b) -> p a b", b=2)[:, :, 0],
                                      outb.rearrange("p (a b) -> p a b", b=2)[:, :, 0],
                                      sce, ALU.mult)
                sw(e, ev1, ins)
                ins = e.tensor_tensor(outb.rearrange("p (a b) -> p a b", b=2)[:, :, 1],
                                      outb.rearrange("p (a b) -> p a b", b=2)[:, :, 1],
                                      sce, ALU.mult)
                ev1.after(ins)
            ctx.on("vector", f_dq)
            dqd = ev1.n

            def f_st(e, s_=s_, cc=cc, v=dqd, outb=outb):
                e.wait_ge(ev1.h, v)
                e.dma_start(out=wout[ts(s_, 128), cc * 2048:cc * 2048 + 2048], in_=outb) \
                    .then_inc(dma2.h, 16)
            ctx.on("gpsimd", f_st)
            dma2.n += 16
        ctx._dq_store = dma2.n
    barrier(ctx)


def emit_dequant(ctx):
    t = ctx.t
    ctx._dq_store = 0
    emit_dequant_one(ctx, t["L0_ffn_gate_W_packed"], t["L0_ffn_gate_scales"],
                     t["Wg_dq"], MLP0, D // 2)
    ctx._dq_store = 0
    emit_dequant_one(ctx, t["L0_ffn_up_W_packed"], t["L0_ffn_up_scales"],
                     t["Wu_dq"], MLP0, D // 2)
    ctx._dq_store = 0
    emit_dequant_one(ctx, t["L0_ffn_down_W_packed"], t["L0_ffn_down_scales"],
                     t["Wd_dq"], D, MLP0 // 2)


def emit_ffn0_gateup(ctx):
    t = ctx.t
    dma1, dma2, mm1, ev1, sc0 = (ctx.sem("dma1"), ctx.sem("dma2"), ctx.sem("mm1"),
                                 ctx.sem("ev1"), ctx.sem("sc0"))
    sbB, sbM, psA = t["sbB"], t["sbM"], t["psA"]
    slab = lambda mat, buf: sbB[:, (mat * 2 + buf) * 4096:(mat * 2 + buf + 1) * 4096] \
        .bitcast(BF16).rearrange("p (k m) -> p k m", k=16)
    NMACRO = MLP0 // 512
    prev_use = {0: 0, 1: 0}
    ctx._ffn_evfree = 0
    ctx._ffn_gelu = 0
    ctx._ffn_comb = 0

    for mac in range(NMACRO):
        buf = mac % 2
        need = prev_use[buf]

        def f_ld(e, mac=mac, buf=buf, need=need):
            e.wait_ge(mm1.h, need)
            for mat, wd in ((0, t["Wg_dq"]), (1, t["Wu_dq"])):
                for k in range(16):
                    e.dma_start(out=slab(mat, buf)[:, k, :],
                                in_=wd[ts(mac, 512), ts(k, 128)], transpose=True) \
                        .then_inc(dma1.h, 16)
        ctx.on("sync", f_ld)
        dma1.n += 32 * 16
        ld = dma1.n

        for nt in range(4):
            for mat in range(2):

                gate_gate = ctx._ffn_gelu if mat == 1 else 0

                def f_mm(e, mac=mac, buf=buf, nt=nt, mat=mat, ld=ld, gg=gate_gate):
                    e.wait_ge(dma1.h, ld)
                    if mat == 0:
                        e.wait_ge(ev1.h, ctx._ffn_evfree)
                    else:
                        e.wait_ge(sc0.h, gg)
                    ins = None
                    for k in range(16):
                        for tb in range(4):
                            ins = e.matmul(psA[:, tb, :],
                                           slab(mat, buf)[:, k, ts(nt, 512)[0].__index__() if False else slice(nt * 128, nt * 128 + 128)][:, :] if False else
                                           slab(mat, buf)[:, k, nt * 128:nt * 128 + 128],
                                           xn_view(ctx)[:, k, ts(tb, 512)],
                                           start=(k == 0), stop=(k == 15))
                    mm1.after(ins)
                ctx.on("tensor", f_mm)
                mmn = mm1.n

                if mat == 0:
                    def f_gelu(e, v=mmn, pc=ctx._ffn_comb):
                        e.wait_ge(mm1.h, v)
                        e.wait_ge(ev1.h, pc)
                        ins = None
                        for tb in range(4):
                            ins = e.activation(sbM[:, 2048 + tb * 256:2048 + tb * 256 + 256]
                                               .bitcast(BF16),
                                               psA[:, tb, :], AF.Gelu_apprx_tanh)
                        sc0.after(ins)
                    ctx.on("scalar", f_gelu)
                    ctx._ffn_gelu = sc0.n
                else:
                    def f_comb(e, v=mmn, gv=ctx._ffn_gelu, pst=dma2.n):
                        e.wait_ge(mm1.h, v)
                        e.wait_ge(sc0.h, gv)
                        e.wait_ge(dma2.h, pst)
                        ins = None
                        for tb in range(4):
                            ins = e.tensor_tensor(sbM[:, 3072 + tb * 256:3072 + tb * 256 + 256]
                                                  .bitcast(BF16),
                                                  sbM[:, 2048 + tb * 256:2048 + tb * 256 + 256]
                                                  .bitcast(BF16),
                                                  psA[:, tb, :], ALU.mult)
                        ev1.after(ins)
                    ctx.on("vector", f_comb)
                    ctx._ffn_evfree = ev1.n
                    ctx._ffn_comb = ev1.n
                    cmb = ev1.n

                    def f_st(e, mac=mac, nt=nt, v=cmb):
                        e.wait_ge(ev1.h, v)
                        e.dma_start(out=t["inter0"][ts(mac * 4 + nt, 128), :],
                                    in_=sbM[:, 3072:4096].bitcast(BF16)).then_inc(dma2.h, 16)
                    ctx.on("gpsimd", f_st)
                    dma2.n += 16
        prev_use[buf] = mm1.n
    barrier(ctx)
    ctx._ffn_evfree = 0
    ctx._ffn_evfree2 = 0


def acc_view(ctx):
    return ctx.t["sbA"]


def emit_ffn_down(ctx, inter_dram, wdq_dram, fout_dram, NK, int8=False, sdown_col=None):
    t = ctx.t
    dma1, dma2, mm1, ev1, sc0, gp1 = (ctx.sem("dma1"), ctx.sem("dma2"), ctx.sem("mm1"),
                                      ctx.sem("ev1"), ctx.sem("sc0"), ctx.sem("gp1"))
    sbB, sbM, psA = t["sbB"], t["sbM"], t["psA"]
    GK = 4
    NG = NK // GK
    islab = lambda buf: sbB[:, buf * 2048:(buf + 1) * 2048].bitcast(BF16) \
        .rearrange("p (k m) -> p k m", k=GK)
    wslab_ = lambda buf: sbB[:, 4096 + buf * 4096:4096 + (buf + 1) * 4096].bitcast(BF16) \
        .rearrange("p (k m) -> p k m", k=GK)
    acc = acc_view(ctx)
    prev_use = {0: 0, 1: 0}
    ctx._dn_ev = {0: 0, 1: 0}

    for th in range(2):
        def f_z(e, pv=ev1.n, ps=sc0.n):
            e.wait_ge(ev1.h, pv)
            e.wait_ge(sc0.h, ps)
            ins = e.memset(acc.rearrange("p a b -> p (a b)"), 0.0)
            gp1.after(ins)
        ctx.on("gpsimd", f_z)
        zd = gp1.n

        for g in range(NG):
            buf = g % 2
            need = prev_use[buf]

            def f_ld(e, th=th, g=g, buf=buf, need=need):
                e.wait_ge(mm1.h, need)
                e.dma_start(out=islab(buf),
                            in_=inter_dram[g * GK * 128:(g * GK + GK) * 128, ts(th, 1024)]
                            .rearrange("(k p) m -> p k m", p=128)).then_inc(dma1.h, 16)
                for k in range(GK):
                    nk = g * GK + k
                    if int8:
                        e.dma_start(out=wslab_(buf)[:, k, :], in_=wdq_dram[ts(nk, 128), :]) \
                            .then_inc(dma1.h, 16)
                    else:
                        e.dma_start(out=wslab_(buf)[:, k, :], in_=wdq_dram[:, ts(nk, 128)],
                                    transpose=True).then_inc(dma1.h, 16)
            eng = "gpsimd" if int8 else "sync"
            ctx.on(eng, f_ld)
            dma1.n += 16 * (1 + GK)
            ld = dma1.n

            for h2t in range(16):
                bk = h2t % 2

                def f_mm(e, g=g, buf=buf, h2t=h2t, bk=bk, ld=ld,
                         ev_need=None):
                    e.wait_ge(dma1.h, ld)
                    e.wait_ge(ev1.h, ctx._dn_ev.get(bk, 0))
                    ins = None
                    for k in range(GK):
                        for tb in range(2):
                            ins = e.matmul(psA[:, 2 * bk + tb, :],
                                           wslab_(buf)[:, k, h2t * 128:h2t * 128 + 128],
                                           islab(buf)[:, k, tb * 512:tb * 512 + 512],
                                           start=(k == 0), stop=(k == GK - 1))
                    mm1.after(ins)
                ctx.on("tensor", f_mm)
                mmn = mm1.n

                def f_add(e, h2t=h2t, bk=bk, v=mmn, zd=zd):
                    e.wait_ge(mm1.h, v)
                    e.wait_ge(gp1.h, zd)
                    e.tensor_tensor(acc[:, h2t, 0:512], acc[:, h2t, 0:512],
                                    psA[:, 2 * bk, :], ALU.add)
                    ins = e.tensor_tensor(acc[:, h2t, 512:1024], acc[:, h2t, 512:1024],
                                          psA[:, 2 * bk + 1, :], ALU.add)
                    ev1.after(ins)
                ctx.on("vector", f_add)
                ctx._dn_ev[bk] = ev1.n
            prev_use[buf] = mm1.n
        all_add = ev1.n

        for h2t in range(16):
            def f_silu(e, th=th, h2t=h2t, v=all_add, pst=dma2.n):
                e.wait_ge(ev1.h, v)
                e.wait_ge(dma2.h, pst)
                if sdown_col is not None:
                    ins = e.activation(sbM[:, 2048:2560].bitcast(BF16), acc[:, h2t, :],
                                       AF.Silu, scale=sdown_col(ctx, h2t))
                else:
                    ins = e.activation(sbM[:, 2048:2560].bitcast(BF16), acc[:, h2t, :],
                                       AF.Silu)
                sc0.after(ins)
            ctx.on("scalar", f_silu)
            sd = sc0.n

            def f_st(e, th=th, h2t=h2t, v=sd):
                e.wait_ge(sc0.h, v)
                e.dma_start(out=fout_dram[ts(h2t, 128), ts(th, 1024)],
                            in_=sbM[:, 2048:2560].bitcast(BF16)).then_inc(dma2.h, 16)
            ctx.on("gpsimd", f_st)
            dma2.n += 16
    barrier(ctx)


def emit_dense_proj(ctx, wd, out_dram, n_tiles=16):
    t = ctx.t
    dma1, dma2, mm1, ev1 = ctx.sem("dma1"), ctx.sem("dma2"), ctx.sem("mm1"), ctx.sem("ev1")
    sbM = t["sbM"]
    ctx._slot_ev = {0: 0, 1: 0, 2: 0}
    ctx._slab_mm = [0, 0]
    ctx._stage_gate = None
    for m in range(n_tiles):
        stag = emit_proj_job(ctx, m, wd, ts(m, 128))

        def f_cast(e, sd=stag, pst=dma2.n):
            e.wait_ge(ev1.h, sd)
            e.wait_ge(dma2.h, pst)
            ins = e.tensor_copy(sbM[:, 4096:5120].bitcast(BF16), sbM[:, 0:2048])
            ev1.after(ins)
        ctx.on("vector", f_cast)
        cd = ev1.n
        ctx._stage_gate = (ev1, cd)

        def f_st(e, m=m, v=cd):
            e.wait_ge(ev1.h, v)
            e.dma_start(out=out_dram[ts(m, 128), :], in_=sbM[:, 4096:5120].bitcast(BF16)) \
                .then_inc(dma2.h, 16)
        ctx.on("gpsimd", f_st)
        dma2.n += 16
    barrier(ctx)


def emit_combine_ln(ctx, xin, sT, fT, xout, do_ln):
    t = ctx.t
    dma1, dma2, mm1, ev1, sc0 = (ctx.sem("dma1"), ctx.sem("dma2"), ctx.sem("mm1"),
                                 ctx.sem("ev1"), ctx.sem("sc0"))
    sbB, sbM, sbS = t["sbB"], t["sbM"], t["sbS"]
    xnv = xn_view(ctx)
    prev_tr_mm = mm1.n
    prev_sq = 0
    for i in range(16):
        j = i % 2
        xslot = sbB[:, j * 4096:j * 4096 + 2048]
        sslot = sbB[:, j * 4096 + 2048:j * 4096 + 3072].bitcast(BF16)
        fslot = sbB[:, j * 4096 + 3072:j * 4096 + 4096].bitcast(BF16)
        prev_done = ev1.n

        def f_ld(e, i=i, xslot=xslot, sslot=sslot, fslot=fslot, prev=prev_done):
            if i >= 2:
                e.wait_ge(ev1.h, prev)
            e.dma_start(out=xslot, in_=xin[ts(i, 128), :]).then_inc(dma1.h, 16)
            e.dma_start(out=sslot, in_=sT[:, ts(i, 128)], transpose=True).then_inc(dma1.h, 16)
            e.dma_start(out=fslot, in_=fT[:, ts(i, 128)], transpose=True).then_inc(dma1.h, 16)
        ctx.on("sync", f_ld)
        dma1.n += 48
        ld = dma1.n

        def f_x1(e, xslot=xslot, sslot=sslot, fslot=fslot, ld=ld):
            e.wait_ge(dma1.h, ld)
            sw(e, ev1, e.tensor_tensor(sbM[:, 0:2048], sslot, fslot, ALU.mult))
            ins = e.tensor_tensor(xslot, xslot, sbM[:, 0:2048], ALU.add)
            ev1.after(ins)
        ctx.on("vector", f_x1)
        x1d = ev1.n

        def f_sto(e, i=i, xslot=xslot, v=x1d):
            e.wait_ge(ev1.h, v)
            e.dma_start(out=xout[ts(i, 128), :], in_=xslot).then_inc(dma2.h, 16)
        ctx.on("gpsimd", f_sto)
        dma2.n += 16

        if not do_ln:
            continue

        def f_sq(e, xslot=xslot, v=x1d, psq=prev_sq):
            e.wait_ge(ev1.h, max(v, psq) if psq else v)
            ins = e.activation(sbM[:, 2048:4096], xslot, AF.Square)
            sc0.after(ins)
        ctx.on("scalar", f_sq)
        sq = sc0.n

        def f_stats(e, xslot=xslot, sq=sq):
            sw(e, ev1, e.reduce_sum(sbS[:, 200:201], xslot, axis=AX))
            e.wait_ge(sc0.h, sq)
            sw(e, ev1, e.reduce_sum(sbS[:, 201:202], sbM[:, 2048:4096], axis=AX))
            e.tensor_scalar(sbS[:, 200:201], sbS[:, 200:201], 1.0 / D, None, ALU.mult)
            sw(e, ev1, e.tensor_scalar(sbS[:, 201:202], sbS[:, 201:202], 1.0 / D, None, ALU.mult))
            sw(e, ev1, e.tensor_scalar(sbS[:, 202:203], sbS[:, 200:201], sbS[:, 200:201],
                                       None, ALU.mult))
            ins = e.tensor_tensor(sbS[:, 201:202], sbS[:, 201:202], sbS[:, 202:203],
                                  ALU.subtract)
            ev1.after(ins)
        ctx.on("vector", f_stats)
        vd = ev1.n

        def f_sqrt(e, v=vd):
            e.wait_ge(ev1.h, v)
            ins = e.activation(sbS[:, 202:203], sbS[:, 201:202], AF.Sqrt, bias=c_epsln(ctx))
            sc0.after(ins)
        ctx.on("scalar", f_sqrt)
        sqr = sc0.n

        def f_apply(e, xslot=xslot, v=sqr, ptr=prev_tr_mm):
            e.wait_ge(sc0.h, v)
            e.wait_ge(mm1.h, ptr)
            sw(e, ev1, e.reciprocal(sbS[:, 203:204], sbS[:, 202:203]))
            ins = e.tensor_scalar(sbM[:, 0:1024].bitcast(BF16), xslot,
                                  sbS[:, 200:201], sbS[:, 203:204],
                                  ALU.subtract, ALU.mult)
            ev1.after(ins)
        ctx.on("vector", f_apply)
        xnd = ev1.n

        tq = TQ(ctx, bf16=True)
        for kk in range(16):
            tq.add(sbM[:, 0:1024].bitcast(BF16)[:, ts(kk, 128)], xnv[:, kk, ts(i, 128)])
        tq.emit(wait_sems=[(ev1, xnd)])
        prev_tr_mm = mm1.n
        prev_sq = vd
    barrier(ctx)


def fam1_csT(ctx, i):  return ctx.t["sbM"][:, 128 * i + 0:128 * i + 32]
def fam1_wT(ctx, i):   return ctx.t["sbM"][:, 128 * i + 32:128 * i + 64]
def fam1_dtT(ctx, i):  return ctx.t["sbM"][:, 128 * i + 64:128 * i + 96]
def fam1_ecsT(ctx, i): return ctx.t["sbM"][:, 128 * i + 96:128 * i + 128]


def emit_proj1(ctx):
    t = ctx.t
    dma1, dma2, dma3, mm1, ev1, sc0, gp1 = (ctx.sem("dma1"), ctx.sem("dma2"), ctx.sem("dma3"),
                                            ctx.sem("mm1"), ctx.sem("ev1"), ctx.sem("sc0"),
                                            ctx.sem("gp1"))
    sbM, sbS, sbB, psA = t["sbM"], t["sbS"], t["sbB"], t["psA"]
    ctx._slot_ev = {0: 0, 1: 0, 2: 0}
    ctx._slab_mm = [0, 0]
    ctx._stage_gate = None

    def f_aux(e):
        e.dma_start(out=sbS[:, 300:364].rearrange("p (k j) -> p k j", j=4),
                    in_=t["L1_attn_conv_w"][0:2048].rearrange("(k p) j -> p k j", p=128)) \
            .then_inc(dma1.h, 16)
        e.dma_start(out=sbS[0:64, 364:368].rearrange("p (k j) -> p k j", j=4),
                    in_=t["L1_attn_conv_w"][2048:2112][None if False else slice(None), :]
                    .rearrange("(k p) j -> p k j", p=64)).then_inc(dma1.h, 16)
        e.dma_start(out=sbS[:, 368:384],
                    in_=t["L1_attn_conv_b"][0:2048].rearrange("(k p) -> p k", p=128),
                    allow_slow_non_contiguous=True).then_inc(dma1.h, 16)
        e.dma_start(out=sbS[0:64, 384:385],
                    in_=t["L1_attn_conv_b"][2048:2112][:, None]).then_inc(dma1.h, 16)
        e.dma_start(out=sbS[:, 385:417],
                    in_=t["L1_attn_b_in"][0:4096].rearrange("(k p) -> p k", p=128),
                    allow_slow_non_contiguous=True).then_inc(dma1.h, 16)
        e.dma_start(out=sbS[0:96, 417:418],
                    in_=t["L1_attn_b_in"][4096:4192][:, None]).then_inc(dma1.h, 16)
        e.dma_start(out=sbS[64:96, 428:429], in_=t["L1_attn_dt_bias"][:, None]) \
            .then_inc(dma1.h, 16)
        e.dma_start(out=sbS[64:96, 430:431], in_=t["L1_attn_A_log"][:, None]) \
            .then_inc(dma1.h, 16)
    ctx.on("sync", f_aux)
    dma1.n += 8 * 16
    aux_ld = dma1.n

    def f_expa(e, v=aux_ld):
        e.wait_ge(dma1.h, v)
        ins = e.activation(sbS[64:96, 429:430], sbS[64:96, 430:431], AF.Exp)
        sc0.after(ins)
    ctx.on("scalar", f_expa)

    for m in range(33):
        mrows = 128 if m < 32 else 96
        stag = emit_proj_job(ctx, m, t["L1_attn_W_in"], slice(m * 128, m * 128 + mrows),
                             mrows=128)

        if m < 16:
            def f_g(e, m=m, sd=stag, pst=dma2.n):
                e.wait_ge(ev1.h, sd)
                e.wait_ge(dma2.h, pst)
                ins = e.activation(sbM[:, 4096:5120].bitcast(BF16), sbM[:, 0:2048],
                                   AF.Identity, bias=sbS[:, 385 + m:386 + m])
                sc0.after(ins)
            ctx.on("scalar", f_g)
            gd = sc0.n
            ctx._stage_gate = (sc0, gd)

            def f_st(e, m=m, v=gd):
                e.wait_ge(sc0.h, v)
                e.dma_start(out=t["gate1T"][ts(m, 128), :],
                            in_=sbM[:, 4096:5120].bitcast(BF16)).then_inc(dma2.h, 16)
            ctx.on("gpsimd", f_st)
            dma2.n += 16
        else:
            j = m - 16
            nrows = 128 if j < 16 else 64
            tapc = 300 + 4 * j if j < 16 else 364
            bcol = (368, j) if j < 16 else (384, 0)

            def f_conv(e, m=m, j=j, nrows=nrows, tapc=tapc, bcol=bcol, sd=stag):
                e.wait_ge(ev1.h, sd)
                y = sbM[0:nrows, 0:2048]
                acc = sbM[0:nrows, 2048:4096]
                sw(e, ev1, e.tensor_scalar(y, y, sbS[0:nrows, 385 + m:386 + m], None,
                                           ALU.add))
                tap = lambda jj: sbS[0:nrows, tapc + jj:tapc + jj + 1]
                bc = sbS[0:nrows, bcol[0] + bcol[1]:bcol[0] + bcol[1] + 1]
                sw(e, ev1, e.tensor_scalar(acc, y, tap(3), bc, ALU.mult, ALU.add))
                sw(e, ev1, e.scalar_tensor_tensor(acc[:, 1:2048], y[:, 0:2047], tap(2),
                                                  acc[:, 1:2048], ALU.mult, ALU.add))
                sw(e, ev1, e.scalar_tensor_tensor(acc[:, 2:2048], y[:, 0:2046], tap(1),
                                                  acc[:, 2:2048], ALU.mult, ALU.add))
                ins = e.scalar_tensor_tensor(acc[:, 3:2048], y[:, 0:2045], tap(0),
                                             acc[:, 3:2048], ALU.mult, ALU.add)
                ev1.after(ins)
            ctx.on("vector", f_conv)
            cvd = ev1.n

            def f_silu(e, nrows=nrows, cd=cvd, pst=dma2.n):
                e.wait_ge(ev1.h, cd)
                e.wait_ge(dma2.h, pst)
                ins = e.activation(sbM[0:nrows, 4096:5120].bitcast(BF16),
                                   sbM[0:nrows, 2048:4096], AF.Silu)
                sc0.after(ins)
            ctx.on("scalar", f_silu)
            sld = sc0.n
            ctx._stage_gate = (sc0, sld)

            if j < 16:
                def f_st(e, j=j, v=sld):
                    e.wait_ge(sc0.h, v)
                    e.dma_start(out=t["xsT"][ts(j, 128), :],
                                in_=sbM[:, 4096:5120].bitcast(BF16)).then_inc(dma2.h, 16)
                ctx.on("gpsimd", f_st)
                dma2.n += 16
            else:
                def f_bc(e, v=sld):
                    e.wait_ge(sc0.h, v)
                    sw(e, ev1, e.tensor_copy(sbB[0:32, 2048:4096],
                                             sbM[0:32, 4096:5120].bitcast(BF16)))
                    ins = e.tensor_copy(sbB[0:32, 4096:6144],
                                        sbM[32:64, 4096:5120].bitcast(BF16))
                    ev1.after(ins)
                ctx.on("vector", f_bc)
                ctx._bc_done = ev1.n

                def f_bcs(e, v=ctx._bc_done, base=dma2.n):
                    e.wait_ge(ev1.h, v)
                    e.dma_start(out=t["bmcm"][0:32, :], in_=sbB[0:32, 2048:4096]) \
                        .then_inc(dma2.h, 16)
                    e.dma_start(out=t["bmcm"][32:64, :], in_=sbB[0:32, 4096:6144]) \
                        .then_inc(dma2.h, 16)
                ctx.on("gpsimd", f_bcs)
                dma2.n += 32
                def f_dt(e, sd=stag):
                    Z = sbM[64:96, 0:2048]
                    S = sbM[64:96, 2048:4096]
                    sw(e, ev1, e.tensor_scalar(Z, Z, sbS[64:96, 417:418], None, ALU.add))
                    ev1.after(e.engine_nop())
                ctx.on("vector", f_dt)
                dtz = ev1.n

                def f_dt2(e, v=dtz):
                    e.wait_ge(ev1.h, v)
                    Z = sbM[64:96, 0:2048]
                    S = sbM[64:96, 2048:4096]
                    sw(e, sc0, e.activation(Z, Z, AF.Identity, bias=sbS[64:96, 428:429]))
                    sw(e, sc0, e.activation(S, Z, AF.Abs))
                    sw(e, sc0, e.activation(S, S, AF.Exp, scale=-1.0))
                    ins = e.activation(S, S, AF.Ln, bias=c_onecol(ctx)[64:96, :])
                    sc0.after(ins)
                ctx.on("scalar", f_dt2)
                dts = sc0.n

                def f_dt3(e, v=dts):
                    e.wait_ge(sc0.h, v)
                    Z = sbM[64:96, 0:2048]
                    S = sbM[64:96, 2048:4096]
                    sw(e, ev1, e.tensor_scalar(Z, Z, 0.0, None, ALU.max))
                    sw(e, ev1, e.tensor_tensor(Z, Z, S, ALU.add))
                    ins = e.tensor_scalar(S, Z, sbS[64:96, 429:430], -1.0,
                                          ALU.mult, ALU.mult)
                    ev1.after(ins)
                ctx.on("vector", f_dt3)
                ctx._dt_done = ev1.n
    def f_zz(e):
        ins = e.memset(sbB[64:96, 15360:15488], 0.0)
        gp1.after(ins)
    ctx.on("gpsimd", f_zz)
    zzd = gp1.n

    def f_scan(e, v=ctx._dt_done, zz=zzd):
        e.wait_ge(ev1.h, v)
        e.wait_ge(gp1.h, zz)
        S = sbM[64:96, 2048:4096]
        ins = None
        for n in range(16):
            ins = e.tensor_tensor_scan(S[:, ts(n, 128)], S[:, ts(n, 128)],
                                       sbB[64:96, 15360:15488], 0.0, ALU.add, ALU.add)
        ev1.after(ins)
    ctx.on("vector", f_scan)
    csd = ev1.n

    def f_csout(e, v=csd):
        e.wait_ge(ev1.h, v)
        e.dma_start(out=t["csflat1"][:, :], in_=sbM[64:96, 2048:4096]).then_inc(dma2.h, 16)
    ctx.on("gpsimd", f_csout)
    dma2.n += 16

    def f_w1(e, v=csd):
        e.wait_ge(ev1.h, v)
        S = sbM[64:96, 2048:4096]
        W = sbM[64:96, 4096:6144]
        ins = None
        for n in range(16):
            gcol = S[:, 128 * n + 127:128 * n + 128]
            ins = e.activation(W[:, ts(n, 128)], S[:, ts(n, 128)], AF.Exp,
                               scale=-1.0, bias=gcol)
        sc0.after(ins)
    ctx.on("scalar", f_w1)
    w1s = sc0.n

    def f_w1b(e, v=w1s):
        e.wait_ge(sc0.h, v)
        W = sbM[64:96, 4096:6144]
        ins = e.tensor_tensor(W, W, sbM[64:96, 0:2048], ALU.mult)
        ev1.after(ins)
    ctx.on("vector", f_w1b)
    w1d = ev1.n

    def f_ecs(e, v=csd):
        e.wait_ge(ev1.h, v)
        ins = e.activation(sbB[64:96, 6144:8192], sbM[64:96, 2048:4096], AF.Exp)
        sc0.after(ins)
    ctx.on("scalar", f_ecs)
    ecsd = sc0.n

    def f_eg(e, v=csd):
        e.wait_ge(ev1.h, v)
        S = sbM[64:96, 2048:4096]
        ins = e.activation(sbS[64:96, 440:456], S[:, 127:2048:128], AF.Exp)
        sc0.after(ins)
    ctx.on("scalar", f_eg)
    egd = sc0.n

    def f_flat(e, v=egd, base=dma3.n):
        e.wait_ge(sc0.h, v)
        e.dma_start(out=t["rowbuf"][0:512], in_=sbS[64:96, 440:456]).then_inc(dma3.h, 16)
        e.wait_ge(dma3.h, base + 16)
        e.dma_start(out=sbB[0:1, 8192:8704], in_=t["rowbuf"][None, 0:512]).then_inc(dma3.h, 16)
    ctx.on("gpsimd", f_flat)
    dma3.n += 32
    fld = dma3.n

    def f_bc2(e, v=fld, pe=ev1.n):
        e.wait_ge(dma3.h, v)
        ins = row_bcast(ctx, e, psA[:, 0, 0:512], sbB[0:1, 8192:8704])
        mm1.after(ins)
    ctx.on("tensor", f_bc2)
    bcd = mm1.n

    def f_bce(e, v=bcd):
        e.wait_ge(mm1.h, v)
        ins = e.tensor_copy(sbM[:, 5120:5632], psA[:, 0, 0:512])
        ev1.after(ins)
    ctx.on("vector", f_bce)
    dgd = ev1.n

    def f_place(e, v=max(w1d, dgd, ecsd)):
        e.wait_ge(ev1.h, v)
        e.wait_ge(sc0.h, ecsd)
        sw(e, ev1, e.tensor_copy(sbB[0:32, 8704:10752], sbM[64:96, 2048:4096]))
        sw(e, ev1, e.tensor_copy(sbB[32:64, 8704:10752], sbM[64:96, 4096:6144]))
        sw(e, ev1, e.tensor_copy(sbB[64:96, 8704:10752], sbM[64:96, 0:2048]))
        ins = e.tensor_copy(sbB[0:32, 10752:12800], sbB[64:96, 6144:8192])
        ev1.after(ins)
    ctx.on("vector", f_place)
    pld = ev1.n

    tq = TQ(ctx, bf16=False)
    for i in range(16):
        tq.add_multi(sbB[:, 8704 + 128 * i:8832 + 128 * i], [
            (slice(0, 32), fam1_csT(ctx, i)),
            (slice(32, 64), fam1_wT(ctx, i)),
            (slice(64, 96), fam1_dtT(ctx, i)),
        ])
        tq.add_multi(sbB[:, 10752 + 128 * i:10880 + 128 * i], [
            (slice(0, 32), fam1_ecsT(ctx, i)),
        ])
    tq.emit(wait_sems=[(ev1, pld)])
    barrier(ctx)


def emit_scan1(ctx):
    t = ctx.t
    dma1, dma2, mm1, ev1, sc0, gp1 = (ctx.sem("dma1"), ctx.sem("dma2"), ctx.sem("mm1"),
                                      ctx.sem("ev1"), ctx.sem("sc0"), ctx.sem("gp1"))
    sbB, sbM, sbS, psA = t["sbB"], t["sbM"], t["sbS"], t["psA"]
    pool = PsumPool(ctx)
    hstate = lambda h: sbB[0:32, 64 * h:64 * (h + 1)]
    Bm = sbB[0:32, 2048:4096]
    Cm = sbB[0:32, 4096:6144]
    xs_tm = sbB[:, 12800:13824].bitcast(BF16)
    g_tm = sbB[:, 13824:14848].bitcast(BF16)
    csrow = sbB[0:1, 6144:10240]
    BmT = sbB[:, 14848:14880]
    Bw = sbB[:, 14880:14896].bitcast(BF16)
    basev = sbB[:, 14896:14960].bitcast(BF16)
    yslab = sbM[:, 2048:3072].bitcast(BF16)
    dmat = sbM[:, 3584:3712]
    sct1 = sbM[:, 3712:3776].bitcast(BF16)
    e1b = sbM[:, 3776:3840]
    e2b = sbM[:, 3904:3968]
    ysq = sbM[:, 3840:4352]
    ynstage = sbM[:, 4352:4864].bitcast(BF16)
    ynstage = sbM[:, 4352:5376].bitcast(BF16)
    rst = sbS[:, 216:217]

    def f_init(e):
        ins = e.memset(sbB[0:32, 0:2048], 0.0)
        gp1.after(ins)
    ctx.on("gpsimd", f_init)

    def f_bcld(e, base=dma1.n):
        e.dma_start(out=sbB[0:32, 2048:4096], in_=t["bmcm"][0:32, :]).then_inc(dma1.h, 16)
        e.dma_start(out=sbB[0:32, 4096:6144], in_=t["bmcm"][32:64, :]).then_inc(dma1.h, 16)
    ctx.on("sync", f_bcld)
    dma1.n += 32
    barrier(ctx)

    for n in range(16):
        pe_, pm_ = ev1.n, mm1.n

        def f_ld(e, n=n, pe=pe_, pm=pm_):
            e.wait_ge(ev1.h, pe)
            e.wait_ge(mm1.h, pm)
            e.dma_start(out=xs_tm, in_=t["xsT"][:, ts(n, 128)], transpose=True) \
                .then_inc(dma1.h, 16)
            e.dma_start(out=g_tm, in_=t["gate1T"][:, ts(n, 128)], transpose=True) \
                .then_inc(dma1.h, 16)
            e.dma_start(out=csrow.rearrange("a (h c) -> a h c", h=32),
                        in_=t["csflat1"][:, ts(n, 128)][None, :, :]).then_inc(dma1.h, 16)
        ctx.on("sync", f_ld)
        dma1.n += 48
        ld = dma1.n

        iT, apT, wT = pool.acquire(128)

        def f_bmt(e, n=n, pe=pe_, ap=apT, waits=wT):
            e.wait_ge(ev1.h, pe)
            for s_, v_ in waits:
                e.wait_ge(s_.h, v_)
            ins = e.transpose(ap[0:128, 0:32], Bm[:, ts(n, 128)], c_idf32(ctx)[0:32, 0:32])
            mm1.after(ins)
        ctx.on("tensor", f_bmt)
        bmtm = mm1.n

        def f_bmte(e, v=bmtm, ap=apT):
            e.wait_ge(mm1.h, v)
            ins = e.tensor_copy(BmT[0:128, 0:32], ap[0:128, 0:32])
            ev1.after(ins)
        ctx.on("vector", f_bmte)
        bmtd = ev1.n
        pool.release(iT, [(ev1, bmtd)])

        iP, apP, wP = pool.acquire(128)
        def f_base(e, n=n, ap=apP, waits=wP, ld=ld):
            e.wait_ge(dma1.h, ld)
            for s, v in waits:
                e.wait_ge(s.h, v)
            ins = e.matmul(ap[0:128, :], Bm[:, ts(n, 128)], Cm[:, ts(n, 128)],
                           start=True, stop=True)
            mm1.after(ins)
        ctx.on("tensor", f_base)
        bmm = mm1.n

        def f_basev(e, v=bmm, ap=apP):
            e.wait_ge(mm1.h, v)
            ins = e.tensor_copy(basev, ap[0:128, :])
            ev1.after(ins)
        ctx.on("vector", f_basev)
        bvd = ev1.n
        pool.release(iP, [(ev1, bvd)])

        prev_ymm = 0
        for h in range(H1):
            def f_exr(e, h=h, n=n, ld=ld, pm=prev_ymm):
                if pm:
                    e.wait_ge(mm1.h, pm)
                e.wait_ge(dma1.h, ld)
                ins = e.tensor_copy(sbB[0:1, 10240:10368], csrow[:, 128 * h:128 * h + 128])
                ev1.after(ins)
            ctx.on("vector", f_exr)
            exd = ev1.n

            iA, apA, wA = pool.acquire(128)
            def f_csb(e, v=exd, ap=apA, waits=wA):
                e.wait_ge(ev1.h, v)
                for s, vv in waits:
                    e.wait_ge(s.h, vv)
                ins = row_bcast(ctx, e, ap[0:128, :], sbB[0:1, 10240:10368])
                mm1.after(ins)
            ctx.on("tensor", f_csb)
            csbm = mm1.n

            def f_dm(e, h=h, n=n, v=csbm, ap=apA, bv=bvd):
                e.wait_ge(mm1.h, v)
                e.wait_ge(ev1.h, bv)
                sw(e, ev1, e.tensor_scalar(dmat, ap[0:128, :],
                                           fam1_csT(ctx, n)[:, h:h + 1], None, ALU.subtract))
                ins = e.tensor_tensor(dmat, dmat, c_mtri128(ctx), ALU.add)
                ev1.after(ins)
            ctx.on("vector", f_dm)
            dmd = ev1.n
            pool.release(iA, [(ev1, dmd)])

            def f_de(e, v=dmd):
                e.wait_ge(ev1.h, v)
                ins = e.activation(dmat, dmat, AF.Exp)
                sc0.after(ins)
            ctx.on("scalar", f_de)
            ded = sc0.n

            def f_sct(e, h=h, n=n, v=ded):
                e.wait_ge(sc0.h, v)
                sw(e, ev1, e.scalar_tensor_tensor(sct1, dmat, fam1_dtT(ctx, n)[:, h:h + 1],
                                                  basev, ALU.mult, ALU.mult))
                ins = e.tensor_scalar(Bw, BmT[:, 0:32], fam1_wT(ctx, n)[:, h:h + 1],
                                      None, ALU.mult)
                ev1.after(ins)
            ctx.on("vector", f_sct)
            sctd = ev1.n

            iB, apB, wB = pool.acquire(192)
            def f_ymm(e, h=h, n=n, v=sctd, ap=apB, waits=wB):
                e.wait_ge(ev1.h, v)
                for s, vv in waits:
                    e.wait_ge(s.h, vv)
                e.matmul(ap[0:128, 0:64], sct1, xs_tm[:, ts(h, 64)], start=True, stop=True)
                e.matmul(ap[0:128, 64:128], Cm[:, ts(n, 128)], hstate(h),
                         start=True, stop=True)
                ins = e.matmul(ap[0:32, 128:192], Bw, xs_tm[:, ts(h, 64)],
                               start=True, stop=True)
                mm1.after(ins)
            ctx.on("tensor", f_ymm)
            ymm = mm1.n
            prev_ymm = ymm

            def f_yeps(e, h=h, n=n, v=ymm, ap=apB):
                e.wait_ge(mm1.h, v)
                sw(e, ev1, e.tensor_scalar(e1b[:, 0:64], ap[0:128, 64:128],
                                           fam1_ecsT(ctx, n)[:, h:h + 1], None, ALU.mult))
                sw(e, ev1, e.scalar_tensor_tensor(e2b[:, 0:64], xs_tm[:, ts(h, 64)],
                                                  c_dbrow(ctx)[:, h:h + 1], ap[0:128, 0:64],
                                                  ALU.mult, ALU.add))
                sw(e, ev1, e.tensor_tensor(yslab[:, ts(h, 64)], e1b[:, 0:64], e2b[:, 0:64],
                                           ALU.add))
                ins = e.scalar_tensor_tensor(hstate(h), hstate(h),
                                             dgB(ctx)[0:32, 16 * h + n:16 * h + n + 1],
                                             ap[0:32, 128:192], ALU.mult, ALU.add)
                ev1.after(ins)
            ctx.on("vector", f_yeps)
            pool.release(iB, [(ev1, ev1.n)])
            if getattr(ctx, "_dbg_scan", None) == n and h == 0:
                def f_dh(e, v=ev1.n, base=ctx.sem("dma2").n):
                    e.wait_ge(ev1.h, v)
                    e.dma_start(out=ctx.t["d_dmat"], in_=dmat).then_inc(ctx.sem("dma2").h, 16)
                    e.dma_start(out=ctx.t["d_sct"], in_=sct1).then_inc(ctx.sem("dma2").h, 16)
                    e.dma_start(out=ctx.t["d_base"], in_=basev).then_inc(ctx.sem("dma2").h, 16)
                    e.dma_start(out=ctx.t["d_e1"], in_=e1b[:, 0:64]).then_inc(ctx.sem("dma2").h, 16)
                    e.dma_start(out=ctx.t["d_e2"], in_=e2b[:, 0:64]).then_inc(ctx.sem("dma2").h, 16)
                    e.wait_ge(ctx.sem("dma2").h, base + 80)
                ctx.on("gpsimd", f_dh)
                ctx.sem("dma2").n += 80
        yall = ev1.n

        if getattr(ctx, "_dbg_scan", None) == n:
            def f_dsc(e, v=yall, base=ctx.sem("dma2").n):
                e.wait_ge(ev1.h, v)
                e.dma_start(out=ctx.t["d_yslab"], in_=yslab).then_inc(ctx.sem("dma2").h, 16)
                e.dma_start(out=ctx.t["d_hst"], in_=sbB[0:32, 0:2048]).then_inc(ctx.sem("dma2").h, 16)
                e.wait_ge(ctx.sem("dma2").h, base + 32)
            ctx.on("gpsimd", f_dsc)
            ctx.sem("dma2").n += 32

        def f_sg(e, v=yall, pst=getattr(ctx, "_yn_used", 0)):
            e.wait_ge(ev1.h, v)
            if pst:
                e.wait_ge(mm1.h, pst)
            ins = e.activation(ynstage, g_tm, AF.Silu)
            sc0.after(ins)
        ctx.on("scalar", f_sg)
        sgd = sc0.n

        def f_gate(e, v=sgd, yv=yall):
            e.wait_ge(sc0.h, v)
            sw(e, ev1, e.tensor_tensor(yslab, yslab, ynstage, ALU.mult))
            ev1.after(e.engine_nop())
        ctx.on("vector", f_gate)
        gtd = ev1.n

        def f_ysq(e, v=gtd):
            e.wait_ge(ev1.h, v)
            ins = e.activation(ysq, yslab[:, 0:1024] if False else yslab, AF.Square,
                               accum_out=None) if False else None
            ins = e.activation(t["sbB"][:, 10368:12416], yslab, AF.Square)
            sc0.after(ins)
        ctx.on("scalar", f_ysq)
        ysd = sc0.n

        def f_rms(e, v=ysd):
            e.wait_ge(sc0.h, v)
            sw(e, ev1, e.reduce_sum(rst, t["sbB"][:, 10368:12416], axis=AX))
            ins = e.tensor_scalar(rst, rst, 1.0 / D, None, ALU.mult)
            ev1.after(ins)
        ctx.on("vector", f_rms)
        rmd = ev1.n

        def f_rsq(e, v=rmd):
            e.wait_ge(ev1.h, v)
            ins = e.activation(rst, rst, AF.Sqrt, bias=c_eps0(ctx))
            sc0.after(ins)
        ctx.on("scalar", f_rsq)
        rqd = sc0.n

        def f_yn(e, v=rqd):
            e.wait_ge(sc0.h, v)
            sw(e, ev1, e.reciprocal(rst, rst))
            sw(e, ev1, e.tensor_scalar(yslab, yslab, rst, None, ALU.mult))
            ins = e.tensor_tensor(ynstage, yslab, c_normw1B(ctx), ALU.mult)
            ev1.after(ins)
        ctx.on("vector", f_yn)
        ynd = ev1.n

        tq = TQ(ctx, bf16=True)
        ynv = xn_view(ctx)
        for kk in range(16):
            tq.add(ynstage[:, ts(kk, 128)], ynv[:, kk, ts(n, 128)])
        tq.emit(wait_sems=[(ev1, ynd)])
        ctx._yn_used = mm1.n
    barrier(ctx)
    ctx._yn_used = 0


def emit_ffn1_gateup(ctx):
    t = ctx.t
    dma1, dma2, mm1, ev1, sc0 = (ctx.sem("dma1"), ctx.sem("dma2"), ctx.sem("mm1"),
                                 ctx.sem("ev1"), ctx.sem("sc0"))
    sbB, sbM, psA = t["sbB"], t["sbM"], t["psA"]
    slab = lambda mat, buf: sbB[:, (mat * 2 + buf) * 4096:(mat * 2 + buf + 1) * 4096] \
        .bitcast(BF16).rearrange("p (k m) -> p k m", k=16)
    def f_sc(e):
        e.dma_start(out=t["sbS"][:, 300:364],
                    in_=t["L1_ffn_s_gate"].rearrange("(m p) -> p m", p=128),
                    allow_slow_non_contiguous=True).then_inc(dma1.h, 16)
        e.dma_start(out=t["sbS"][:, 364:428],
                    in_=t["L1_ffn_s_up"].rearrange("(m p) -> p m", p=128),
                    allow_slow_non_contiguous=True).then_inc(dma1.h, 16)
    ctx.on("sync", f_sc)
    dma1.n += 32
    scl = dma1.n

    NMACRO = MLP1 // 512
    prev_use = {0: 0, 1: 0}
    ctx._ffn_evfree = 0
    ctx._ffn_gelu = 0
    ctx._ffn_comb = 0
    for mac in range(NMACRO):
        buf = mac % 2
        need = prev_use[buf]

        def f_ld(e, mac=mac, buf=buf, need=need):
            e.wait_ge(mm1.h, need)
            for mat, wd in ((0, t["L1_ffn_w_gate"]), (1, t["L1_ffn_w_up"])):
                e.dma_start(out=slab(mat, buf),
                            in_=wd[:, ts(mac, 512)].rearrange("(k p) m -> p k m", p=128)) \
                    .then_inc(dma1.h, 16)
        ctx.on("gpsimd", f_ld)
        dma1.n += 32
        ld = dma1.n

        for nt in range(4):
            for mat in range(2):
                gate_gate = ctx._ffn_gelu if mat == 1 else 0

                def f_mm(e, mac=mac, buf=buf, nt=nt, mat=mat, ld=ld, gg=gate_gate):
                    e.wait_ge(dma1.h, ld)
                    if mat == 0:
                        e.wait_ge(ev1.h, ctx._ffn_evfree)
                    else:
                        e.wait_ge(sc0.h, gg)
                    ins = None
                    for k in range(16):
                        for tb in range(4):
                            ins = e.matmul(psA[:, tb, :],
                                           slab(mat, buf)[:, k, nt * 128:nt * 128 + 128],
                                           xn_view(ctx)[:, k, ts(tb, 512)],
                                           start=(k == 0), stop=(k == 15))
                    mm1.after(ins)
                ctx.on("tensor", f_mm)
                mmn = mm1.n

                if mat == 0:
                    def f_silu(e, mac=mac, nt=nt, v=mmn, pc=ctx._ffn_comb, scl=scl):
                        e.wait_ge(mm1.h, v)
                        e.wait_ge(ev1.h, pc)
                        e.wait_ge(dma1.h, scl)
                        ntile = mac * 4 + nt
                        scol = t["sbS"][:, 300 + ntile // 2:301 + ntile // 2]
                        scol = t["sbS"][:, 300 + ntile:301 + ntile]
                        ins = None
                        for tb in range(4):
                            ins = e.activation(sbM[:, 2048 + tb * 256:2048 + tb * 256 + 256]
                                               .bitcast(BF16), psA[:, tb, :], AF.Silu,
                                               scale=scol)
                        sc0.after(ins)
                    ctx.on("scalar", f_silu)
                    ctx._ffn_gelu = sc0.n
                else:
                    def f_comb(e, mac=mac, nt=nt, v=mmn, gv=ctx._ffn_gelu, pst=dma2.n, scl=scl):
                        e.wait_ge(mm1.h, v)
                        e.wait_ge(sc0.h, gv)
                        e.wait_ge(dma2.h, pst)
                        ntile = mac * 4 + nt
                        sucol = t["sbS"][:, 364 + ntile:365 + ntile]
                        ins = None
                        for tb in range(4):
                            ins = e.scalar_tensor_tensor(
                                sbM[:, 3072 + tb * 256:3072 + tb * 256 + 256].bitcast(BF16),
                                sbM[:, 2048 + tb * 256:2048 + tb * 256 + 256].bitcast(BF16),
                                sucol, psA[:, tb, :], ALU.mult, ALU.mult)
                        ev1.after(ins)
                    ctx.on("vector", f_comb)
                    ctx._ffn_evfree = ev1.n
                    ctx._ffn_comb = ev1.n
                    cmb = ev1.n

                    def f_st(e, mac=mac, nt=nt, v=cmb):
                        e.wait_ge(ev1.h, v)
                        e.dma_start(out=t["inter1"][ts(mac * 4 + nt, 128), :],
                                    in_=sbM[:, 3072:4096].bitcast(BF16)).then_inc(dma2.h, 16)
                    ctx.on("gpsimd", f_st)
                    dma2.n += 16
        prev_use[buf] = mm1.n
    barrier(ctx)


def sdown1_col(ctx, h2t):
    return ctx.t["sbS"][:, 428 + h2t:429 + h2t]


def emit_sdown1_load(ctx):
    t = ctx.t
    dma1 = ctx.sem("dma1")

    def f(e):
        e.dma_start(out=t["sbS"][:, 428:444],
                    in_=t["L1_ffn_s_down"].rearrange("(m p) -> p m", p=128),
                    allow_slow_non_contiguous=True).then_inc(dma1.h, 16)
    ctx.on("sync", f)
    dma1.n += 16
    barrier(ctx)


def emit_dump_sbA(ctx, dst):
    dma2 = ctx.sem("dma2")
    xnv = xn_view(ctx)

    def f(e, base=dma2.n):
        for kk in range(16):
            e.dma_start(out=dst[ts(kk, 128), :], in_=xnv[:, kk, :]).then_inc(dma2.h, 16)
        e.wait_ge(dma2.h, base + 256)
    ctx.on("gpsimd", f)
    dma2.n += 256
    barrier(ctx)


def emit_load_sbA(ctx, src):
    dma1 = ctx.sem("dma1")
    xnv = xn_view(ctx)

    def f(e, base=dma1.n):
        for kk in range(16):
            e.dma_start(out=xnv[:, kk, :], in_=src[ts(kk, 128), :]).then_inc(dma1.h, 16)
        e.wait_ge(dma1.h, base + 256)
    ctx.on("sync", f)
    dma1.n += 256
    barrier(ctx)


def build_progA():
    nc = bass.Bass("TRN2", target_bir_lowering=False)
    stack = ExitStack()
    over = {"Wg_dq": "ExternalOutput", "Wu_dq": "ExternalOutput", "Wd_dq": "ExternalOutput"}
    ctx = build_ctx(nc, stack, over)
    with stack:
        with nc.Block() as blk:
            ctx.blk = blk
            emit_consts(ctx)
            emit_dequant(ctx)
            dma2 = ctx.sem("dma2")

            def f_fin(e, v=dma2.n):
                e.wait_ge(dma2.h, v)
            ctx.on("gpsimd", f_fin)
    return nc


def build_progB():
    nc = bass.Bass("TRN2", target_bir_lowering=False)
    stack = ExitStack()
    over = {"Wg_dq": "ExternalInput", "Wu_dq": "ExternalInput", "Wd_dq": "ExternalInput",
            "x1b": "ExternalOutput", "xn1d": "ExternalOutput"}
    ctx = build_ctx(nc, stack, over)
    ctx._cheb_serial = True
    t = ctx.t
    with stack:
        with nc.Block() as blk:
            ctx.blk = blk
            emit_consts(ctx)
            emit_ln_token(ctx, t["L0_attn_hidden_states"][0])
            emit_proj0(ctx)
            emit_afam_post(ctx)
            emit_ffn0_gateup(ctx)
            emit_ffn_down(ctx, t["inter0"], t["Wd_dq"], t["f0T"], MLP0 // 128)
            emit_chunk_solve(ctx)
            emit_dense_proj(ctx, t["L0_attn_Wo"], t["s0T"])
            emit_combine_ln(ctx, t["L0_attn_hidden_states"][0], t["s0T"], t["f0T"],
                            t["x1b"], do_ln=True)
            emit_dump_sbA(ctx, t["xn1d"])
            dma2 = ctx.sem("dma2")

            def f_fin(e, v=dma2.n):
                e.wait_ge(dma2.h, v)
            ctx.on("gpsimd", f_fin)
    return nc


def build_progC():
    nc = bass.Bass("TRN2", target_bir_lowering=False)
    stack = ExitStack()
    over = {"x1b": "ExternalInput", "xn1d": "ExternalInput", "yb": "ExternalOutput"}
    ctx = build_ctx(nc, stack, over)
    t = ctx.t
    with stack:
        with nc.Block() as blk:
            ctx.blk = blk
            emit_consts(ctx)
            emit_load_sbA(ctx, t["xn1d"])
            emit_proj1(ctx)
            emit_ffn1_gateup(ctx)
            emit_scan1(ctx)
            emit_dense_proj(ctx, t["L1_attn_W_out"], t["s1T"])
            emit_sdown1_load(ctx)
            emit_ffn_down(ctx, t["inter1"], t["L1_ffn_w_down"], t["f1T"],
                          MLP1 // 128, int8=True, sdown_col=sdown1_col)
            emit_combine_ln(ctx, t["x1b"], t["s1T"], t["f1T"], t["yb"], do_ln=False)
            dma2 = ctx.sem("dma2")

            def f_fin(e, v=dma2.n):
                e.wait_ge(dma2.h, v)
            ctx.on("gpsimd", f_fin)
    return nc


_NC_CACHE = {}


def kernel(**inputs):
    n_cores = 8
    if "A" not in _NC_CACHE:
        _NC_CACHE["A"] = build_progA()
        _NC_CACHE["B"] = build_progB()
        _NC_CACHE["C"] = build_progC()
    x = np.ascontiguousarray(inputs["L0_attn_hidden_states"])
    base = dict(inputs)
    del base["L0_attn_hidden_states"]
    dummy_x = x[0:NB]

    import gc
    mA = dict(base)
    mA["L0_attn_hidden_states"] = dummy_x
    resA = run_bass_kernel_spmd(_NC_CACHE["A"], [mA], core_ids=[0])
    dq = {k: np.asarray(resA.results[0][k]) for k in ("Wg_dq", "Wu_dq", "Wd_dq")}
    del resA
    gc.collect()

    outs = []
    for b0 in range(NB):
        mB = [dict(base, L0_attn_hidden_states=x[2 * c + b0:2 * c + b0 + 1].repeat(NB, 0)
                   if False else np.broadcast_to(x[2 * c + b0][None], (NB, L, D)).copy(),
                   **dq) for c in range(n_cores)]
        resB = run_bass_kernel_spmd(_NC_CACHE["B"], mB, core_ids=list(range(n_cores)))
        mC = [dict(base,
                   L0_attn_hidden_states=mB[c]["L0_attn_hidden_states"],
                   x1b=np.asarray(resB.results[c]["x1b"]),
                   xn1d=np.asarray(resB.results[c]["xn1d"])) for c in range(n_cores)]
        del resB, mB
        gc.collect()
        resC = run_bass_kernel_spmd(_NC_CACHE["C"], mC, core_ids=list(range(n_cores)))
        outs.append(np.stack([np.asarray(resC.results[c]["yb"]) for c in range(n_cores)]))
        del resC, mC
        gc.collect()
    y = np.empty((2 * n_cores, L, D), np.float32)
    for b0 in range(NB):
        for c in range(n_cores):
            y[2 * c + b0] = outs[b0][c]
    return y
